# version 21
# baseline (speedup 1.0000x reference)
import contextlib
import numpy as np
import concourse.bass as bass
import concourse.mybir as mybir
from concourse.bass_utils import run_bass_kernel_spmd

F32 = mybir.dt.float32
BF16 = mybir.dt.bfloat16
F16 = mybir.dt.float16
I32 = mybir.dt.int32
AF = mybir.ActivationFunctionType
ALU = mybir.AluOpType
AX = mybir.AxisListType

ENGS = ("sync", "scalar", "vector", "tensor", "gpsimd")
IN_ORDER = {"tensor"}

D = 1024
H = 16
DH = 64
NCH = 8
TGT = 512
NSB = TGT // 128
S127 = 1.0 / 127.0


NPOOL = 12
DMA_QUEUES = ("sync", "gpsimd")
OPCAP = None


class _Op:
    __slots__ = ("eng", "fn", "dma", "idx", "need_inc", "deps", "incs", "post",
                 "waits", "tag", "semk", "slot", "defer", "gkey")


class Em:

    def __init__(self):
        self.ops = []
        self.last_w = {}
        self.last_r = {}
        self.pending_barrier = None
        self.barrier_seen = set()
        self.dma_count = {q: 0 for q in DMA_QUEUES}
        self.slot_last = {q: {} for q in DMA_QUEUES}
        self.open_groups = {}
        self.in_open = set()
        self.last_closed_tensor = None

    def _dep(self, d, gkey=None):
        if d.idx in self.in_open:
            if gkey is not None and d.gkey == gkey:
                return None
            sub = self.last_closed_tensor
            assert sub is not None and sub.idx > d.idx, (
                "open-group dep with no later observable PE op")
            return sub.idx
        return d.idx

    def op(self, eng, fn, reads=(), writes=(), dma=False, tag="", group=None,
           gclose=False):
        if OPCAP is not None and len(self.ops) >= OPCAP:
            d = _Op(); d.idx = -1; d.defer = None; d.gkey = None
            return d
        o = _Op()
        o.eng = eng
        o.fn = fn
        o.dma = dma
        o.tag = tag
        o.idx = len(self.ops)
        o.need_inc = dma
        o.slot = None
        o.defer = None
        o.gkey = group
        if eng == "tensor":
            if group is not None:
                chain = self.open_groups.setdefault(group, [])
                chain.append(o)
                if gclose:
                    for m in chain[:-1]:
                        m.defer = o.idx
                        self.in_open.discard(m.idx)
                    del self.open_groups[group]
                    self.last_closed_tensor = o
                else:
                    self.in_open.add(o.idx)
            else:
                self.last_closed_tensor = o
        deps = set()
        if self.pending_barrier is not None and eng not in self.barrier_seen:
            deps.update(self.pending_barrier)
            self.barrier_seen.add(eng)
        for r in reads:
            w = self.last_w.get(r)
            if w is not None:
                deps.add(self._dep(w, group))
        for r in writes:
            w = self.last_w.get(r)
            if w is not None:
                deps.add(self._dep(w, group))
            for rd in self.last_r.get(r, {}).values():
                deps.add(self._dep(rd, group))
        if dma:
            n = self.dma_count[eng]
            self.dma_count[eng] = n + 1
            o.slot = n % NPOOL
            prev = self.slot_last[eng].get(o.slot)
            if prev is not None:
                deps.add(prev.idx)
            self.slot_last[eng][o.slot] = o
        deps.discard(None)
        deps.discard(o.idx)
        o.deps = sorted(d for d in deps if d < o.idx)
        self.ops.append(o)
        for r in writes:
            self.last_w[r] = o
            self.last_r[r] = {}
        for r in reads:
            self.last_r.setdefault(r, {})[eng] = o
        return o

    def barrier(self):
        targets = []
        last_by_eng = {}
        for o in self.ops:
            last_by_eng[o.eng] = o
        for e, lo in last_by_eng.items():
            if e in DMA_QUEUES:
                targets.extend(p.idx for p in self.slot_last[e].values())
            else:
                targets.append(lo.idx)
        self.pending_barrier = sorted(set(targets))
        self.barrier_seen = set()

    def finalize(self, nc, stack):
        assert not self.open_groups, f"unclosed groups: {list(self.open_groups)}"

        def fin(idx):
            while self.ops[idx].defer is not None:
                idx = self.ops[idx].defer
            return idx

        for o in self.ops:
            o.deps = sorted({fin(d) for d in o.deps})
        for o in self.ops:
            for di in o.deps:
                d = self.ops[di]
                if d.eng == o.eng and o.eng in IN_ORDER:
                    continue
                d.need_inc = True
        semk_all = [("eng", e) for e in ENGS]
        for q in DMA_QUEUES:
            semk_all += [("slot", q, i) for i in range(NPOOL)]
        sems = {k: stack.enter_context(
            nc.semaphore(("sem_%s" % k[1]) if k[0] == "eng"
                         else "sem_%s_s%d" % (k[1], k[2])))
            for k in semk_all}
        cnt = {k: 0 for k in semk_all}
        for o in self.ops:
            o.semk = ("slot", o.eng, o.slot) if o.dma else ("eng", o.eng)
            o.incs = (16 if o.dma else 1) if o.need_inc else 0
            cnt[o.semk] += o.incs
            o.post = cnt[o.semk]
        per_eng = {e: [] for e in ENGS}
        for o in self.ops:
            per_eng[o.eng].append(o)
        for e in ENGS:
            hi = {}
            for o in per_eng[e]:
                need = {}
                for di in o.deps:
                    d = self.ops[di]
                    if d.eng == e and e in IN_ORDER:
                        continue
                    assert d.need_inc and d.post > 0
                    if d.post > hi.get(d.semk, 0):
                        need[d.semk] = max(need.get(d.semk, 0), d.post)
                o.waits = [(sems[k], v) for k, v in need.items()]
                if e != "tensor":
                    for k, v in need.items():
                        hi[k] = v
        block = stack.enter_context(nc.Block())

        def mk(e, items):
            def body(eng):
                for o in items:
                    for (sem, val) in o.waits:
                        eng.wait_ge(sem, val)
                    inst = o.fn(eng)
                    if o.incs:
                        if inst is None:
                            inst = eng.nop(nofuse=True)
                        inst.then_inc(sems[o.semk], o.incs)
                for k in semk_all:
                    if (k[1] == e) and cnt[k] > 0:
                        eng.wait_ge(sems[k], cnt[k])
            return body

        for e in ENGS:
            if per_eng[e]:
                getattr(block, e)(mk(e, per_eng[e]))
        return sems, cnt


C_IDENT = 0
C_GQBLK = 128
C_GKBLK = 130
C_BLK2 = 132
C_ONES = 134
C_TANHB = 135
C_EPSLN = 136
C_TANHB2 = 137
C_BO = 138
C_GAMMA = 146
C_BETA = 154
C_SEL1 = 162
C_SEL05 = 290
CC = 512


def _mk_consts(b_o, gamma, beta, g_q, g_k):
    cn = np.zeros((128, CC), np.float32)
    cn[:, C_IDENT:C_IDENT + 128] = np.eye(128, dtype=np.float32)
    cn[0:64, C_GQBLK] = g_q
    cn[64:128, C_GQBLK + 1] = g_q
    cn[0:64, C_GKBLK] = g_k
    cn[64:128, C_GKBLK + 1] = g_k
    cn[0:64, C_BLK2] = 1.0
    cn[64:128, C_BLK2 + 1] = 1.0
    cn[:, C_ONES] = 1.0
    cn[:, C_TANHB] = -4.05
    b_exact = -(15.0 * 0.27)
    cn[:, C_TANHB2] = np.float32(b_exact - np.float64(np.float32(-4.05)))
    cn[:, C_EPSLN] = 1e-5
    cn[:, C_BO:C_BO + 8] = b_o.reshape(8, 128).T
    cn[:, C_GAMMA:C_GAMMA + 8] = gamma.reshape(8, 128).T
    cn[:, C_BETA:C_BETA + 8] = beta.reshape(8, 128).T
    cn[0, C_SEL1:C_SEL1 + 64] = 1.0
    cn[1, C_SEL1 + 64:C_SEL1 + 128] = 1.0
    cn[0, C_SEL05:C_SEL05 + 64] = 0.5
    cn[1, C_SEL05 + 64:C_SEL05 + 128] = 0.5
    return cn


def build_program(S, T, HID, phases=(1, 1, 1)):
    NTG = T // TGT
    NKT = T // 128
    NHC = HID // 128
    nc = bass.Bass("TRN2", target_bir_lowering=False)
    stack = contextlib.ExitStack()
    nc._keepalive_stack = stack

    x_hbm = nc.dram_tensor("x", [S, T, D], F32, kind="ExternalInput").ap()
    wq_h = nc.dram_tensor("wq", [D, D], F32, kind="ExternalInput").ap()
    wk_h = nc.dram_tensor("wk", [D, D], F32, kind="ExternalInput").ap()
    wv_h = nc.dram_tensor("wv", [D, D], F32, kind="ExternalInput").ap()
    wo_h = nc.dram_tensor("wo", [D, D], F32, kind="ExternalInput").ap()
    wgT_h = nc.dram_tensor("wgT", [D, HID], I32, kind="ExternalInput").ap()
    wuT_h = nc.dram_tensor("wuT", [D, HID], I32, kind="ExternalInput").ap()
    wdT_h = nc.dram_tensor("wdT", [HID, D], I32, kind="ExternalInput").ap()
    cn_h = nc.dram_tensor("cn", [128, CC], F32, kind="ExternalInput").ap()
    out_h = nc.dram_tensor("out", [S, T, D], F32, kind="ExternalOutput").ap()
    sc_xnormT = nc.dram_tensor("sc_xnormT", [S, D, T], F32).ap()
    sc_xnT = nc.dram_tensor("sc_xnT", [S, D, T], F32).ap()
    sc_vp = nc.dram_tensor("sc_vp", [S, T, D], F32).ap()
    sc_k = nc.dram_tensor("sc_k", [S, D, T], F32).ap()
    sc_s = nc.dram_tensor("sc_s", [S, D, T], F32).ap()
    sc_wg = nc.dram_tensor("sc_wg", [NHC, 128, NCH * 128], F16).ap()
    sc_wu = nc.dram_tensor("sc_wu", [NHC, 128, NCH * 128], F16).ap()
    sc_wd = nc.dram_tensor("sc_wd", [NCH, 128, NHC * 128], F16).ap()

    em = Em()

    cn = stack.enter_context(nc.sbuf_tensor("cn_sb", [128, CC], F32))
    c_all = stack.enter_context(nc.sbuf_tensor("c_all", [128, 8], F32))
    ps = stack.enter_context(nc.psum_tensor("ps", [128, 4096], F32))

    def PS(b, o=0, w=512):
        return ps[:, 512 * b + o: 512 * b + o + w]

    ident = cn[:, C_IDENT:C_IDENT + 128]
    gqblk = cn[:, C_GQBLK:C_GQBLK + 2]
    gkblk = cn[:, C_GKBLK:C_GKBLK + 2]
    ones = cn[:, C_ONES:C_ONES + 1]
    tanhb = cn[:, C_TANHB:C_TANHB + 1]
    epsln = cn[:, C_EPSLN:C_EPSLN + 1]
    sel1 = cn[0:2, C_SEL1:C_SEL1 + 128]
    sel05 = cn[0:2, C_SEL05:C_SEL05 + 128]

    em.op("sync", lambda e: e.dma_start(out=cn[:], in_=cn_h[:, :]),
          writes=["cn"], dma=True, tag="cn_load")
    em.op("vector", lambda e: e.tensor_add(
        out=tanhb2[:], in0=cn[:, C_TANHB:C_TANHB + 1],
        in1=cn[:, C_TANHB2:C_TANHB2 + 1]),
        reads=["cn"], writes=["tanhb2"])

    nmu = stack.enter_context(nc.sbuf_tensor("nmu", [128, NSB], F32))
    var = stack.enter_context(nc.sbuf_tensor("var", [128, NSB], F32))
    rs = stack.enter_context(nc.sbuf_tensor("rs", [128, NSB], F32))
    sstile = stack.enter_context(nc.sbuf_tensor("sstile", [128, 64], F32))
    gdot = stack.enter_context(nc.sbuf_tensor("gdot", [128, 64], F32))
    gk_tm = stack.enter_context(nc.sbuf_tensor("gk_tm", [128, 64], F32))
    gq_tm = stack.enter_context(nc.sbuf_tensor("gq_tm", [128, 64], F32))
    n2t = stack.enter_context(nc.sbuf_tensor("n2t", [2, TGT], F32))
    tanhb2 = stack.enter_context(nc.sbuf_tensor("tanhb2", [128, 1], F32))

    def layer_norm(xin, xout, sqbuf, key_in, key_out, key_sq, affine=False):
        em.op("vector", lambda e: e.reduce_sum(out=nmu[:, 0:NSB], in_=xin[:],
                                               axis=AX.X, negate=True),
              reads=[key_in], writes=["nmu"])
        em.op("vector", lambda e: e.tensor_scalar_mul(out=nmu[:, 0:NSB],
                                                      in0=nmu[:, 0:NSB],
                                                      scalar1=1.0 / D),
              reads=["nmu"], writes=["nmu"])
        for s in range(NSB):
            em.op("scalar", lambda e, s=s: e.activation(
                out=xout[:, s, :], in_=xin[:, s, :], func=AF.Identity,
                bias=nmu[:, s:s + 1], scale=1.0),
                reads=[key_in, "nmu"], writes=[key_out])
        em.op("vector", lambda e: e.tensor_mul(out=sqbuf[:], in0=xout[:],
                                               in1=xout[:]),
              reads=[key_out], writes=[key_sq])
        em.op("vector", lambda e: e.reduce_sum(out=var[:, 0:NSB], in_=sqbuf[:],
                                               axis=AX.X),
              reads=[key_sq], writes=["var"])
        em.op("vector", lambda e: e.tensor_scalar_mul(out=var[:, 0:NSB],
                                                      in0=var[:, 0:NSB],
                                                      scalar1=1.0 / D),
              reads=["var"], writes=["var"])
        em.op("scalar", lambda e: e.activation(out=rs[:, 0:NSB],
                                               in_=var[:, 0:NSB],
                                               func=AF.Sqrt,
                                               bias=epsln, scale=1.0),
              reads=["var", "cn"], writes=["rs"])
        em.op("vector", lambda e: e.reciprocal(out=rs[:, 0:NSB],
                                               in_=rs[:, 0:NSB]),
              reads=["rs"], writes=["rs"])
        for s in range(NSB):
            em.op("scalar", lambda e, s=s: e.activation(
                out=xout[:, s, :], in_=xout[:, s, :], func=AF.Identity,
                bias=0.0, scale=rs[:, s:s + 1]),
                reads=[key_out, "rs"], writes=[key_out])

    def layer_norm2(xin, xout, key_in, key_out):
        em.op("vector", lambda e: e.reduce_sum(out=nmu[:, 0:NSB], in_=xin[:],
                                               axis=AX.X, negate=True),
              reads=[key_in], writes=["nmu"])
        em.op("vector", lambda e: e.tensor_scalar_mul(out=nmu[:, 0:NSB],
                                                      in0=nmu[:, 0:NSB],
                                                      scalar1=1.0 / D),
              reads=["nmu"], writes=["nmu"])
        em.op("vector", lambda e: e.tensor_mul(out=xout[:], in0=xin[:],
                                               in1=xin[:]),
              reads=[key_in], writes=[key_out])
        em.op("vector", lambda e: e.reduce_sum(out=var[:, 0:NSB], in_=xout[:],
                                               axis=AX.X),
              reads=[key_out], writes=["var"])
        em.op("vector", lambda e: e.tensor_mul(out=rs[:, 0:NSB],
                                               in0=nmu[:, 0:NSB],
                                               in1=nmu[:, 0:NSB]),
              reads=["nmu"], writes=["rs"])
        em.op("vector", lambda e: e.tensor_scalar_mul(out=var[:, 0:NSB],
                                                      in0=var[:, 0:NSB],
                                                      scalar1=1.0 / D),
              reads=["var"], writes=["var"])
        em.op("vector", lambda e: e.tensor_sub(out=var[:, 0:NSB],
                                               in0=var[:, 0:NSB],
                                               in1=rs[:, 0:NSB]),
              reads=["var", "rs"], writes=["var"])
        em.op("scalar", lambda e: e.activation(out=rs[:, 0:NSB],
                                               in_=var[:, 0:NSB],
                                               func=AF.Sqrt,
                                               bias=epsln, scale=1.0),
              reads=["var", "cn"], writes=["rs"])
        em.op("vector", lambda e: e.reciprocal(out=rs[:, 0:NSB],
                                               in_=rs[:, 0:NSB]),
              reads=["rs"], writes=["rs"])
        for s in range(NSB):
            em.op("scalar", lambda e, s=s: e.activation(
                out=xout[:, s, :], in_=xin[:, s, :], func=AF.Identity,
                bias=nmu[:, s:s + 1], scale=1.0),
                reads=[key_in, "nmu"], writes=[key_out])
        for s in range(NSB):
            em.op("scalar", lambda e, s=s: e.activation(
                out=xout[:, s, :], in_=xout[:, s, :], func=AF.Identity,
                bias=0.0, scale=rs[:, s:s + 1]),
                reads=[key_out, "rs"], writes=[key_out])

    def phase_a(b, bufAd, bufBd, xnT, xnormT, wk_sb, wv_sb, kT_tg,
                sq_sb, v_tg, rn2):
        em.op("vector", lambda e: e.memset(c_all[:], 0.0), writes=["c_all"])
        em.op("sync", lambda e: e.dma_start(
            out=wk_sb[:], in_=wk_h[:, :].rearrange("(c p) d -> p c d", p=128)),
              writes=["wk_sb"], dma=True, tag="wk")
        em.op("sync", lambda e: e.dma_start(
            out=wv_sb[:], in_=wv_h[:, :].rearrange("(c p) d -> p c d", p=128)),
              writes=["wv_sb"], dma=True, tag="wv")

        def pre(g):
            gp = g % 2
            bufA, bufB = bufAd[gp], bufBd[gp]
            kA, kB = f"bufA{gp}", f"bufB{gp}"
            t0 = TGT * g
            srcx = x_hbm[b, t0:t0 + TGT, :].rearrange("(s p) d -> p s d",
                                                      p=128)
            em.op("sync", lambda e, srcx=srcx, bufA=bufA: e.dma_start(
                out=bufA[:], in_=srcx),
                  writes=[kA], dma=True, tag="xload")
            layer_norm(bufA, bufB, bufA, kA, kB, kA)
            layer_norm2(bufB, bufA, kB, kA)

        def main(g):
            gp = g % 2
            bufA, bufB = bufAd[gp], bufBd[gp]
            kA, kB = f"bufA{gp}", f"bufB{gp}"
            t0 = TGT * g
            for c in range(NCH):
                tb = 3 * (c % 2)
                for s in range(NSB):
                    em.op("tensor", lambda e, c=c, s=s, bufB=bufB, tb=tb:
                          e.transpose(
                        PS(tb, 128 * s, 128), bufB[:, s, 128 * c:128 * c + 128],
                        ident),
                        reads=[kB, "cn"], writes=[f"ps{tb}"])
                em.op("vector", lambda e, c=c, tb=tb: e.tensor_copy(
                    out=xnT[:, c, :], in_=PS(tb)),
                    reads=[f"ps{tb}"], writes=["xnT"])
            dst = sc_xnT[b, :, t0:t0 + TGT].rearrange("(c p) t -> p c t",
                                                      p=128)
            em.op("gpsimd", lambda e, dst=dst: e.dma_start(out=dst,
                                                           in_=xnT[:]),
                  reads=["xnT"], dma=True)
            for c in range(NCH):
                tb = 3 * (c % 2)
                for s in range(NSB):
                    em.op("tensor", lambda e, c=c, s=s, bufA=bufA, tb=tb:
                          e.transpose(
                        PS(tb, 128 * s, 128), bufA[:, s, 128 * c:128 * c + 128],
                        ident),
                        reads=[kA, "cn"], writes=[f"ps{tb}"])
                em.op("scalar", lambda e, c=c, tb=tb: e.activation(
                    out=xnormT[:, c, :], in_=PS(tb), func=AF.Identity,
                    scale=cn[:, C_GAMMA + c:C_GAMMA + c + 1],
                    bias=cn[:, C_BETA + c:C_BETA + c + 1]),
                    reads=[f"ps{tb}", "cn"], writes=["xnormT"])
            dst = sc_xnormT[b, :, t0:t0 + TGT].rearrange("(c p) t -> p c t",
                                                         p=128)
            em.op("gpsimd", lambda e, dst=dst: e.dma_start(out=dst,
                                                           in_=xnormT[:]),
                  reads=["xnormT"], dma=True)
            for j in range(NCH):
                pb = 1 + (j % 2)
                for i in range(NCH):
                    em.op("tensor", lambda e, i=i, j=j, pb=pb: e.matmul(
                        PS(pb), wk_sb[:, i, 128 * j:128 * j + 128],
                        xnormT[:, i, :], start=(i == 0), stop=(i == NCH - 1)),
                        reads=["wk_sb", "xnormT"], writes=[f"ps{pb}"],
                        group=f"kp{g}_{j}", gclose=(i == NCH - 1))
                em.op("vector", lambda e, j=j, pb=pb: e.tensor_copy(
                    out=kT_tg[:, j, :], in_=PS(pb)),
                    reads=[f"ps{pb}"], writes=["kT_tg"])
            for j in range(NCH):
                em.op("scalar", lambda e, j=j: e.activation(
                    out=sq_sb[:], in_=kT_tg[:, j, :], func=AF.Square),
                    reads=["kT_tg"], writes=["sq_sb"])
                em.op("tensor", lambda e, j=j: e.matmul(
                    ps[0:2, 512 * 3:512 * 3 + TGT], cn[:, C_BLK2:C_BLK2 + 2],
                    sq_sb[:], start=True, stop=True),
                    reads=["sq_sb", "cn"], writes=["ps3"])
                em.op("scalar", lambda e, j=j: e.activation(
                    out=n2t[:, :], in_=ps[0:2, 512 * 3:512 * 3 + TGT],
                    func=AF.Sqrt),
                    reads=["ps3"], writes=["n2t"])
                em.op("vector", lambda e: e.tensor_single_scalar(
                    out=n2t[:, :], in_=n2t[:, :], scalar=1e-12, op=ALU.max),
                    reads=["n2t"], writes=["n2t"])
                em.op("vector", lambda e, j=j: e.reciprocal(
                    out=rn2[:, :], in_=n2t[:, :]),
                    reads=["n2t"], writes=["rn2"])
                em.op("tensor", lambda e, j=j: e.matmul(
                    PS(4), sel1, rn2[:, :], start=True, stop=True),
                    reads=["rn2", "cn"], writes=["ps4"])
                em.op("vector", lambda e, j=j: e.tensor_mul(
                    out=kT_tg[:, j, :], in0=kT_tg[:, j, :],
                    in1=PS(4)),
                    reads=["ps4", "kT_tg"], writes=["kT_tg"])
                for s in range(NSB):
                    em.op("tensor", lambda e, j=j, s=s: e.matmul(
                        ps[:, 512 * 5 + 16 * s + 2 * j:
                           512 * 5 + 16 * s + 2 * j + 2],
                        kT_tg[:, j, 128 * s:128 * s + 128], gkblk,
                        start=True, stop=True),
                        reads=["kT_tg", "cn"], writes=["ps5"])
            dst = sc_k[b, :, t0:t0 + TGT].rearrange("(c p) t -> p c t",
                                                    p=128)
            em.op("gpsimd", lambda e, dst=dst: e.dma_start(out=dst,
                                                           in_=kT_tg[:]),
                  reads=["kT_tg"], dma=True, tag="kstore")
            em.op("scalar", lambda e: e.activation(
                out=gdot[:, :], in_=ps[:, 512 * 5:512 * 5 + 64],
                func=AF.Sigmoid),
                reads=["ps5"], writes=["gdot"])
            em.op("vector", lambda e: e.tensor_scalar(
                out=gk_tm[:, :], in0=gdot[:, :], scalar1=0.75, scalar2=0.25,
                op0=ALU.mult, op1=ALU.add),
                reads=["gdot"], writes=["gk_tm"])
            for s in range(NSB):
                for jh in range(2):
                    pb = 6 + jh
                    for i in range(NCH):
                        em.op("tensor", lambda e, i=i, s=s, jh=jh, pb=pb:
                              e.matmul(PS(pb),
                                       xnT[:, i, 128 * s:128 * s + 128],
                                       wv_sb[:, i, 512 * jh:512 * jh + 512],
                                       start=(i == 0), stop=(i == NCH - 1)),
                              reads=["wv_sb", "xnT"], writes=[f"ps{pb}"],
                              group=f"vp{g}_{s}_{jh}", gclose=(i == NCH - 1))
                    for hh in range(8):
                        h = 8 * jh + hh
                        em.op("vector", lambda e, s=s, jh=jh, hh=hh, h=h,
                              pb=pb: e.tensor_scalar_mul(
                                  out=v_tg[:, s, 512 * jh + 64 * hh:
                                           512 * jh + 64 * hh + 64],
                                  in0=PS(pb, 64 * hh, 64),
                                  scalar1=gk_tm[:, 16 * s + h:16 * s + h + 1]),
                              reads=[f"ps{pb}", "gk_tm"], writes=["v_tg"])
            for s in range(NSB):
                for p in range(8):
                    em.op("tensor", lambda e, s=s, p=p: e.matmul(
                        ps[:, 512 * 4 + p:512 * 4 + p + 1],
                        v_tg[:, s, 128 * p:128 * p + 128], ones,
                        start=True, stop=True),
                        reads=["v_tg", "cn"], writes=["ps4"])
                em.op("vector", lambda e, s=s: e.tensor_add(
                    out=c_all[:], in0=c_all[:], in1=ps[:, 512 * 4:512 * 4 + 8]),
                    reads=["ps4", "c_all"], writes=["c_all"])
            dst = sc_vp[b, t0:t0 + TGT, :].rearrange("(s p) d -> p s d", p=128)
            em.op("gpsimd", lambda e, dst=dst: e.dma_start(out=dst,
                                                           in_=v_tg[:]),
                  reads=["v_tg"], dma=True)

        pre(0)
        for g in range(NTG):
            if g + 1 < NTG:
                pre(g + 1)
            main(g)

    def phase_b1(b, xnormT2, qstage, sqq, qnT, wq_sb, wo_sb, kc_sb, vp_sb,
                 t_sb, f1, fusedT, sT_st, gq_hm):
        em.op("sync", lambda e: e.dma_start(
            out=wq_sb[:], in_=wq_h[:, :].rearrange("(c p) d -> p c d", p=128)),
              writes=["wq_sb"], dma=True, tag="wq")
        em.op("sync", lambda e: e.dma_start(
            out=wo_sb[:], in_=wo_h[:, :].rearrange("(c p) d -> p c d", p=128)),
              writes=["wo_sb"], dma=True, tag="wo")
        src0 = sc_xnormT[b, :, 0:TGT].rearrange("(c p) t -> p c t", p=128)
        em.op("sync", lambda e, src=src0: e.dma_start(out=xnormT2[0][:],
                                                      in_=src),
              writes=["xn0"], dma=True)
        def qproj_group(qt, gi):
            s, jh = gi // 2, gi % 2
            pb = jh
            xnormT = xnormT2[qt % 2]
            xkey = f"xn{qt % 2}"
            for i in range(NCH):
                em.op("tensor", lambda e, i=i, s=s, jh=jh, pb=pb,
                      xnormT=xnormT:
                      e.matmul(PS(pb),
                               xnormT[:, i, 128 * s:128 * s + 128],
                               wq_sb[:, i, 512 * jh:512 * jh + 512],
                               start=(i == 0), stop=(i == NCH - 1)),
                      reads=["wq_sb", xkey], writes=[f"ps{pb}"],
                      group=f"qp{qt}_{s}_{jh}", gclose=(i == NCH - 1))
            em.op("vector", lambda e, s=s, jh=jh, pb=pb, qt=qt: e.tensor_copy(
                out=qstage[:, s, 512 * jh:512 * jh + 512], in_=PS(pb)),
                reads=[f"ps{pb}"], writes=["qst"])

        for qt in range(NTG):
            q0 = TGT * qt
            xp = qt % 2
            xnormT = xnormT2[xp]
            xkey = f"xn{xp}"
            if qt + 1 < NTG:
                srcn = sc_xnormT[b, :, q0 + TGT:q0 + 2 * TGT].rearrange(
                    "(c p) t -> p c t", p=128)
                em.op("sync", lambda e, src=srcn, xq=(qt + 1) % 2:
                      e.dma_start(out=xnormT2[xq][:], in_=src),
                      writes=[f"xn{(qt + 1) % 2}"], dma=True)
            if qt == 0:
                for gi in range(2 * NSB):
                    qproj_group(0, gi)
            for s in range(NSB):
                for jh in range(2):
                    em.op("scalar", lambda e, s=s, jh=jh: e.activation(
                        out=sqq[:, :],
                        in_=qstage[:, s, 512 * jh:512 * jh + 512],
                        func=AF.Square),
                        reads=["qst"], writes=["sqq"])
                    em.op("vector", lambda e, s=s, jh=jh: e.reduce_sum(
                        out=sstile[:, 16 * s + 8 * jh:16 * s + 8 * jh + 8],
                        in_=sqq[:].rearrange("p (h d) -> p h d", d=DH),
                        axis=AX.X),
                        reads=["sqq"], writes=["sstile"])
            em.op("scalar", lambda e: e.activation(out=sstile[:, :],
                                                   in_=sstile[:, :],
                                                   func=AF.Sqrt),
                  reads=["sstile"], writes=["sstile"])
            em.op("vector", lambda e: e.tensor_single_scalar(
                out=sstile[:, :], in_=sstile[:, :], scalar=1e-12, op=ALU.max),
                reads=["sstile"], writes=["sstile"])
            em.op("vector", lambda e: e.reciprocal(out=sstile[:, :],
                                                   in_=sstile[:, :]),
                  reads=["sstile"], writes=["sstile"])
            for s in range(NSB):
                em.op("vector", lambda e, s=s: e.tensor_mul(
                    out=qstage[:, s, :].rearrange("p (h d) -> p h d", d=DH),
                    in0=qstage[:, s, :].rearrange("p (h d) -> p h d", d=DH),
                    in1=sstile[:, 16 * s:16 * s + 16].unsqueeze(2)
                        .broadcast_to([128, 16, DH])),
                    reads=["qst", "sstile"], writes=["qst"])
            for c in range(NCH):
                tb = 2 + (c % 2)
                for s in range(NSB):
                    em.op("tensor", lambda e, c=c, s=s, tb=tb: e.transpose(
                        PS(tb, 128 * s, 128),
                        qstage[:, s, 128 * c:128 * c + 128], ident),
                        reads=["qst", "cn"], writes=[f"ps{tb}"])
                if c % 2 == 0:
                    em.op("scalar", lambda e, c=c, tb=tb: e.activation(
                        out=qnT[:, c, :], in_=PS(tb), func=AF.Identity),
                        reads=[f"ps{tb}"], writes=["qnT"])
                else:
                    em.op("vector", lambda e, c=c, tb=tb: e.tensor_copy(
                        out=qnT[:, c, :], in_=PS(tb)),
                        reads=[f"ps{tb}"], writes=["qnT"])
            for s in range(NSB):
                for c in range(NCH):
                    em.op("tensor", lambda e, c=c, s=s: e.matmul(
                        ps[:, 512 * 3 + 16 * s + 2 * c:
                           512 * 3 + 16 * s + 2 * c + 2],
                        qnT[:, c, 128 * s:128 * s + 128], gqblk,
                        start=True, stop=True),
                        reads=["qnT", "cn"], writes=["ps3"])
            em.op("scalar", lambda e: e.activation(
                out=gdot[:, :], in_=ps[:, 512 * 3:512 * 3 + 64],
                func=AF.Sigmoid),
                reads=["ps3"], writes=["gdot"])
            em.op("vector", lambda e: e.tensor_scalar(
                out=gq_tm[:, :], in0=gdot[:, :], scalar1=0.75, scalar2=0.25,
                op0=ALU.mult, op1=ALU.add),
                reads=["gdot"], writes=["gq_tm"])
            for c in range(NCH):
                cp = c % 2
                src = sc_vp[b, :, 128 * c:128 * c + 128].rearrange(
                    "(kt p) d -> p kt d", p=128)
                em.op("sync", lambda e, src=src, cp=cp: e.dma_start(
                    out=vp_sb[cp][:], in_=src),
                    writes=[f"vp{cp}"], dma=True, tag="vp")
                em.op("sync", lambda e, c=c, cp=cp: e.dma_start(
                    out=kc_sb[cp][:], in_=sc_k[b, 128 * c:128 * c + 128, :]),
                    writes=[f"kc{cp}"], dma=True, tag="kc")

                def align(kt):
                    alb = 2 + 2 * (kt % 2)
                    em.op("tensor", lambda e, kt=kt, alb=alb, cp=cp, c=c:
                          e.matmul(
                        PS(alb), kc_sb[cp][0:64, 128 * kt:128 * kt + 128],
                        qnT[0:64, c, :], start=True, stop=True),
                        reads=[f"kc{cp}", "qnT"], writes=[f"ps{alb}"])
                    em.op("tensor", lambda e, kt=kt, alb=alb, cp=cp, c=c:
                          e.matmul(
                        PS(alb + 1), kc_sb[cp][64:128, 128 * kt:128 * kt + 128],
                        qnT[64:128, c, :], start=True, stop=True,
                        tile_position=(64, 0)),
                        reads=[f"kc{cp}", "qnT"], writes=[f"ps{alb + 1}"])

                align(0)
                for kt in range(NKT):
                    kp = kt % 2
                    alb = 2 + 2 * kp
                    em.op("scalar", lambda e, kp=kp, alb=alb: e.activation(
                        out=t_sb[kp][:], in_=ps[:, 512 * alb:512 * alb + 1024],
                        func=AF.Tanh, scale=15.0, bias=tanhb2[:, 0:1]),
                        reads=[f"ps{alb}", f"ps{alb + 1}", "tanhb2"],
                        writes=[f"t{kp}"])
                    if kt + 1 < NKT:
                        align(kt + 1)
                    em.op("tensor", lambda e, kt=kt, kp=kp, cp=cp: e.matmul(
                        ps[0:64, 512 * 6:512 * 6 + 512],
                        vp_sb[cp][:, kt, 0:64], t_sb[kp][:, 0:512],
                        start=(kt == 0), stop=(kt == NKT - 1)),
                        reads=[f"vp{cp}", f"t{kp}"], writes=["ps6"],
                        group=f"fuA{qt}_{c}", gclose=(kt == NKT - 1))
                    em.op("tensor", lambda e, kt=kt, kp=kp, cp=cp: e.matmul(
                        ps[64:128, 512 * 7:512 * 7 + 512],
                        vp_sb[cp][:, kt, 64:128], t_sb[kp][:, 512:1024],
                        start=(kt == 0), stop=(kt == NKT - 1),
                        tile_position=(0, 64)),
                        reads=[f"vp{cp}", f"t{kp}"], writes=["ps7"],
                        group=f"fuB{qt}_{c}", gclose=(kt == NKT - 1))
                for s in range(NSB):
                    em.op("tensor", lambda e, c=c, s=s: e.transpose(
                        ps[0:2, 512 * 3 + 128 * s:512 * 3 + 128 * s + 128],
                        gq_tm[:, 16 * s + 2 * c:16 * s + 2 * c + 2], ident),
                        reads=["gq_tm", "cn"], writes=["ps3"])
                em.op("vector", lambda e, c=c: e.tensor_copy(
                    out=gq_hm[c % 2][:, :], in_=ps[0:2, 512 * 3:512 * 3 + TGT]),
                    reads=["ps3"], writes=[f"gq_hm{c % 2}"])
                em.op("tensor", lambda e, c=c: e.matmul(
                    PS(2), sel05, gq_hm[c % 2][:, :], start=True, stop=True),
                    reads=[f"gq_hm{c % 2}", "cn"], writes=["ps2"])
                em.op("vector", lambda e: e.tensor_copy(
                    out=f1[:], in_=PS(2)),
                    reads=["ps2"], writes=["sqq"])
                em.op("vector", lambda e, c=c: e.scalar_tensor_tensor(
                    out=fusedT[0:64, c, :], in0=ps[0:64, 512 * 6:512 * 6 + 512],
                    scalar=c_all[0:64, c:c + 1], in1=f1[0:64, :],
                    op0=ALU.add, op1=ALU.mult),
                    reads=["ps6", "sqq", "c_all"], writes=["qf"])
                em.op("vector", lambda e, c=c: e.scalar_tensor_tensor(
                    out=fusedT[64:128, c, :], in0=ps[64:128, 512 * 7:512 * 7 + 512],
                    scalar=c_all[64:128, c:c + 1], in1=f1[64:128, :],
                    op0=ALU.add, op1=ALU.mult),
                    reads=["ps7", "sqq", "c_all"], writes=["qf"])
                if qt + 1 < NTG:
                    qproj_group(qt + 1, c)
            for j in range(NCH):
                pb = j % 2
                for i in range(NCH):
                    em.op("tensor", lambda e, i=i, j=j, pb=pb: e.matmul(
                        PS(pb), wo_sb[:, i, 128 * j:128 * j + 128],
                        fusedT[:, i, :], start=(i == 0), stop=(i == NCH - 1)),
                        reads=["wo_sb", "qf"], writes=[f"ps{pb}"],
                        group=f"wo{qt}_{j}", gclose=(i == NCH - 1))
                sp = j % 2
                em.op("scalar", lambda e, j=j, pb=pb, sp=sp: e.activation(
                    out=sT_st[sp][:], in_=PS(pb), func=AF.Identity,
                    bias=cn[:, C_BO + j:C_BO + j + 1]),
                    reads=[f"ps{pb}", "cn"], writes=[f"sT{sp}"])
                dst = sc_s[b, 128 * j:128 * j + 128, q0:q0 + TGT]
                em.op("gpsimd", lambda e, dst=dst, sp=sp: e.dma_start(
                    out=dst, in_=sT_st[sp][:]),
                    reads=[f"sT{sp}"], dma=True)

    def phase_b2(b, xnT2d, xh, xl, wg_sb, wu_sb, wd_sb, hf, sg_sb, hh, hl,
                 g_sb, bufx, s_sb, prod, out_sb):
        NHC_l = NHC
        src0 = sc_xnT[b, :, 0:TGT].rearrange("(c p) t -> p c t", p=128)
        em.op("sync", lambda e, src=src0: e.dma_start(out=xnT2d[0][:],
                                                      in_=src),
              writes=["xt0"], dma=True)

        def split(qt):
            xp = qt % 2
            xnT2 = xnT2d[xp]
            xtkey = f"xt{xp}"
            for c in range(NCH):
                em.op("vector", lambda e, c=c, xnT2=xnT2: e.tensor_copy(
                    out=xh[:, c, :], in_=xnT2[:, c, :]),
                    reads=[xtkey], writes=["xh"])
                em.op("vector", lambda e, c=c, xnT2=xnT2: e.tensor_sub(
                    out=xl[:, c, :], in0=xnT2[:, c, :], in1=xh[:, c, :]),
                    reads=[xtkey, "xh"], writes=["xl"])

        split(0)
        for qt in range(NTG):
            q0 = TGT * qt
            if qt + 1 < NTG:
                srcn = sc_xnT[b, :, q0 + TGT:q0 + 2 * TGT].rearrange(
                    "(c p) t -> p c t", p=128)
                em.op("sync", lambda e, src=srcn, xq=(qt + 1) % 2:
                      e.dma_start(out=xnT2d[xq][:], in_=src),
                      writes=[f"xt{(qt + 1) % 2}"], dma=True)
            for hc in range(NHC_l):
                wp = hc % 2
                em.op("sync", lambda e, hc=hc, wp=wp: e.dma_start(
                    out=wg_sb[wp][:],
                    in_=sc_wg[hc, :, :].rearrange("p (c h) -> p c h", h=128)),
                    writes=[f"wg{wp}"], dma=True, tag="wg")
                em.op("sync", lambda e, hc=hc, wp=wp: e.dma_start(
                    out=wu_sb[wp][:],
                    in_=sc_wu[hc, :, :].rearrange("p (c h) -> p c h", h=128)),
                    writes=[f"wu{wp}"], dma=True, tag="wu")

                for i in range(NCH):
                    em.op("tensor", lambda e, i=i, wp=wp: e.matmul(
                        PS(0), wg_sb[wp][:, i, :], xh[:, i, :],
                        start=(i == 0), stop=False),
                        reads=[f"wg{wp}", "xh"], writes=["ps0"],
                        group=f"ga{qt}_{hc}")
                    em.op("tensor", lambda e, i=i, wp=wp: e.matmul(
                        PS(0), wg_sb[wp][:, i, :], xl[:, i, :],
                        start=False, stop=(i == NCH - 1)),
                        reads=[f"wg{wp}", "xl"], writes=["ps0"],
                        group=f"ga{qt}_{hc}", gclose=(i == NCH - 1))
                for i in range(NCH):
                    em.op("tensor", lambda e, i=i, wp=wp: e.matmul(
                        PS(1), wu_sb[wp][:, i, :], xh[:, i, :],
                        start=(i == 0), stop=False),
                        reads=[f"wu{wp}", "xh"], writes=["ps1"],
                        group=f"up{qt}_{hc}")
                    em.op("tensor", lambda e, i=i, wp=wp: e.matmul(
                        PS(1), wu_sb[wp][:, i, :], xl[:, i, :],
                        start=False, stop=(i == NCH - 1)),
                        reads=[f"wu{wp}", "xl"], writes=["ps1"],
                        group=f"up{qt}_{hc}", gclose=(i == NCH - 1))
                em.op("scalar", lambda e: e.activation(
                    out=sg_sb[:], in_=PS(0), func=AF.Silu, scale=S127),
                    reads=["ps0"], writes=["sg_sb"])
                em.op("vector", lambda e: e.scalar_tensor_tensor(
                    out=hf[:], in0=PS(1), scalar=S127, in1=sg_sb[:],
                    op0=ALU.mult, op1=ALU.mult),
                    reads=["ps1", "sg_sb"], writes=["hf"])
                em.op("vector", lambda e, hc=hc: e.tensor_copy(
                    out=hh[:, hc, :], in_=hf[:]),
                    reads=["hf"], writes=["hh"])
                em.op("vector", lambda e, hc=hc: e.tensor_sub(
                    out=hl[:, hc, :], in0=hf[:], in1=hh[:, hc, :]),
                    reads=["hf", "hh"], writes=["hl"])
            if qt + 1 < NTG:
                split(qt + 1)
            for j in range(NCH):
                pb = 2 + (j % 2)
                wp = j % 2
                em.op("sync", lambda e, j=j, wp=wp: e.dma_start(
                    out=wd_sb[wp][:],
                    in_=sc_wd[j, :, :].rearrange("p (c d) -> p c d", d=128)),
                    writes=[f"wd{wp}"], dma=True, tag="wd")
                for hc in range(NHC_l):
                    em.op("tensor", lambda e, hc=hc, j=j, pb=pb, wp=wp:
                          e.matmul(
                        PS(pb), wd_sb[wp][:, hc, :],
                        hh[:, hc, :], start=(hc == 0), stop=False),
                        reads=[f"wd{wp}", "hh"], writes=[f"ps{pb}"],
                        group=f"dn{qt}_{j}")
                    em.op("tensor", lambda e, hc=hc, j=j, pb=pb, wp=wp:
                          e.matmul(
                        PS(pb), wd_sb[wp][:, hc, :],
                        hl[:, hc, :], start=False, stop=(hc == NHC_l - 1)),
                        reads=[f"wd{wp}", "hl"], writes=[f"ps{pb}"],
                        group=f"dn{qt}_{j}", gclose=(hc == NHC_l - 1))
                em.op("scalar", lambda e, j=j, pb=pb: e.activation(
                    out=g_sb[:, j, :], in_=PS(pb), func=AF.Silu, scale=S127),
                    reads=[f"ps{pb}"], writes=["g_sb"])
            src = x_hbm[b, q0:q0 + TGT, :].rearrange("(s p) d -> p s d", p=128)
            em.op("sync", lambda e, src=src: e.dma_start(out=bufx[:], in_=src),
                  writes=["bufx"], dma=True)
            src = sc_s[b, :, q0:q0 + TGT].rearrange("(c p) t -> p c t", p=128)
            em.op("sync", lambda e, src=src: e.dma_start(out=s_sb[:], in_=src),
                  writes=["s_sb"], dma=True)
            for j in range(NCH):
                tb = 4 + (j % 2)
                em.op("vector", lambda e, j=j: e.tensor_mul(
                    out=prod[:], in0=g_sb[:, j, :], in1=s_sb[:, j, :]),
                    reads=["g_sb", "s_sb"], writes=["hf"])
                for s in range(NSB):
                    em.op("tensor", lambda e, j=j, s=s, tb=tb: e.transpose(
                        PS(tb, 128 * s, 128), prod[:, 128 * s:128 * s + 128],
                        ident),
                        reads=["hf", "cn"], writes=[f"ps{tb}"])
                for s in range(NSB):
                    em.op("vector", lambda e, j=j, s=s, tb=tb: e.tensor_add(
                        out=out_sb[:, s, 128 * j:128 * j + 128],
                        in0=PS(tb, 128 * s, 128),
                        in1=bufx[:, s, 128 * j:128 * j + 128]),
                        reads=[f"ps{tb}", "bufx"], writes=["out_sb"])
            dst = out_h[b, q0:q0 + TGT, :].rearrange("(s p) d -> p s d", p=128)
            em.op("gpsimd", lambda e, dst=dst: e.dma_start(out=dst,
                                                           in_=out_sb[:]),
                  reads=["out_sb"], dma=True, tag="out")

    with contextlib.ExitStack() as pp:
        wbuf = [pp.enter_context(nc.sbuf_tensor(f"wrep{i}", [128, HID], F16))
                for i in range(2)]
        for c in range(NCH):
            wp = c % 2
            for (wsrc, wdst, tag) in ((wgT_h, sc_wg, "rg"), (wuT_h, sc_wu, "ru")):
                em.op("gpsimd", lambda e, c=c, wp=wp, wsrc=wsrc: e.dma_start(
                    out=wbuf[wp][:],
                    in_=wsrc[128 * c:128 * c + 128, :]),
                    writes=[f"wrep{wp}"], dma=True, tag=tag)
                dst = wdst[:, :, 128 * c:128 * c + 128].rearrange(
                    "h p j -> p h j")
                em.op("gpsimd", lambda e, dst=dst, wp=wp: e.dma_start(
                    out=dst, in_=wbuf[wp][:].rearrange("p (h j) -> p h j",
                                                       j=128)),
                    reads=[f"wrep{wp}"], dma=True, tag=tag)
        wdbuf = [pp.enter_context(nc.sbuf_tensor(f"wdrep{i}", [128, D], F16))
                 for i in range(2)]
        for c in range(NHC):
            wp = c % 2
            em.op("gpsimd", lambda e, c=c, wp=wp: e.dma_start(
                out=wdbuf[wp][:],
                in_=wdT_h[128 * c:128 * c + 128, :]),
                writes=[f"wdrep{wp}"], dma=True, tag="rd")
            dst = sc_wd[:, :, 128 * c:128 * c + 128].rearrange(
                "j p d -> p j d")
            em.op("gpsimd", lambda e, dst=dst, wp=wp: e.dma_start(
                out=dst, in_=wdbuf[wp][:].rearrange("p (j d) -> p j d",
                                                    d=128)),
                reads=[f"wdrep{wp}"], dma=True, tag="rd")
    em.barrier()
    for b in range(S):
        if True:
            with contextlib.ExitStack() as pa:
                bufAd = [pa.enter_context(
                    nc.sbuf_tensor(f"bufA{i}_{b}", [128, NSB, D], F32))
                    for i in range(2)]
                bufBd = [pa.enter_context(
                    nc.sbuf_tensor(f"bufB{i}_{b}", [128, NSB, D], F32))
                    for i in range(2)]
                xnT = pa.enter_context(nc.sbuf_tensor(f"xnT_{b}", [128, NCH, TGT],
                                                      F32))
                xnormT = pa.enter_context(
                    nc.sbuf_tensor(f"xnormT_{b}", [128, NCH, TGT], F32))
                wk_sb = pa.enter_context(
                    nc.sbuf_tensor(f"wk_sb_{b}", [128, NCH, D], F32))
                wv_sb = pa.enter_context(
                    nc.sbuf_tensor(f"wv_sb_{b}", [128, NCH, D], F32))
                kT_tg = pa.enter_context(
                    nc.sbuf_tensor(f"kT_tg_{b}", [128, NCH, TGT], F32))
                sq_sb = pa.enter_context(nc.sbuf_tensor(f"sq_sb_{b}", [128, TGT],
                                                        F32))
                v_tg = pa.enter_context(nc.sbuf_tensor(f"v_tg_{b}", [128, NSB, D],
                                                       F32))
                rn2 = pa.enter_context(nc.sbuf_tensor(f"rn2_{b}", [2, TGT],
                                                      F32))
                if phases[0]:
                    phase_a(b, bufAd, bufBd, xnT, xnormT, wk_sb, wv_sb,
                            kT_tg, sq_sb, v_tg, rn2)
            em.barrier()
            with contextlib.ExitStack() as pb:
                xnormT2 = [pb.enter_context(
                    nc.sbuf_tensor(f"xn{i}_{b}", [128, NCH, TGT], F32))
                    for i in range(2)]
                qf = pb.enter_context(
                    nc.sbuf_tensor(f"qf_{b}", [128, NSB * D], F32))
                qstage = pb.enter_context(
                    nc.sbuf_tensor(f"qst_{b}", [128, NSB, D], F32))
                sqq = pb.enter_context(nc.sbuf_tensor(f"sqq_{b}", [128, TGT], F32))
                qnT = pb.enter_context(nc.sbuf_tensor(f"qnT_{b}", [128, NCH, TGT],
                                                      F32))
                wq_sb = pb.enter_context(
                    nc.sbuf_tensor(f"wq_sb_{b}", [128, NCH, D], F32))
                wo_sb = pb.enter_context(
                    nc.sbuf_tensor(f"wo_sb_{b}", [128, NCH, D], F32))
                kc_sb = [pb.enter_context(
                    nc.sbuf_tensor(f"kc{i}_{b}", [128, T], F32))
                    for i in range(2)]
                vp_sb = [pb.enter_context(
                    nc.sbuf_tensor(f"vp{i}_{b}", [128, NKT, 128], F32))
                    for i in range(2)]
                t_sb = [pb.enter_context(
                    nc.sbuf_tensor(f"t{i}_{b}", [128, 1024], F32))
                    for i in range(2)]
                f1 = sqq
                fusedT = qf[:].rearrange("p (c t) -> p c t", c=NCH)
                sT_st = [pb.enter_context(
                    nc.sbuf_tensor(f"sT{i}_{b}", [128, TGT], F32))
                    for i in range(2)]
                gq_hm = [pb.enter_context(
                    nc.sbuf_tensor(f"gq_hm{c}_{b}", [2, TGT], F32))
                    for c in range(2)]
                if phases[1]:
                    phase_b1(b, xnormT2, qstage, sqq, qnT, wq_sb, wo_sb, kc_sb,
                             vp_sb, t_sb, f1, fusedT, sT_st, gq_hm)
            em.barrier()
        with contextlib.ExitStack() as pc:
            xnT2d = [pc.enter_context(
                nc.sbuf_tensor(f"xt{i}_{b}", [128, NCH, TGT], F32))
                for i in range(2)]
            out_sb = pc.enter_context(
                nc.sbuf_tensor(f"out_sb_{b}", [128, NSB, D], F32))
            xh = pc.enter_context(nc.sbuf_tensor(f"xh_b2_{b}", [128, NCH, TGT],
                                                 F16))
            xl = pc.enter_context(nc.sbuf_tensor(f"xl_b2_{b}", [128, NCH, TGT],
                                                 F16))
            wg_sb = [pc.enter_context(
                nc.sbuf_tensor(f"wg{i}_{b}", [128, NCH, 128], F16))
                for i in range(2)]
            wu_sb = [pc.enter_context(
                nc.sbuf_tensor(f"wu{i}_{b}", [128, NCH, 128], F16))
                for i in range(2)]
            wd_sb = [pc.enter_context(
                nc.sbuf_tensor(f"wd{i}_{b}", [128, NHC, 128], F16))
                for i in range(2)]
            hf = pc.enter_context(nc.sbuf_tensor(f"hf_{b}", [128, TGT], F32))
            sg_sb = pc.enter_context(nc.sbuf_tensor(f"sg_sb_{b}", [128, TGT], F32))
            hh = pc.enter_context(nc.sbuf_tensor(f"hh_{b}", [128, NHC, TGT], F16))
            hl = pc.enter_context(nc.sbuf_tensor(f"hl_{b}", [128, NHC, TGT], F16))
            g_sb = pc.enter_context(nc.sbuf_tensor(f"g_sb_{b}", [128, NCH, TGT],
                                                   F32))
            bufx = pc.enter_context(nc.sbuf_tensor(f"bufx_{b}", [128, NSB, D],
                                                   F32))
            s_sb = pc.enter_context(nc.sbuf_tensor(f"s_sb_{b}", [128, NCH, TGT],
                                                   F32))
            if phases[2]:
                phase_b2(b, xnT2d, xh, xl, wg_sb, wu_sb, wd_sb, hf, sg_sb, hh,
                         hl, g_sb, bufx, s_sb, hf, out_sb)
        em.barrier()

    em.finalize(nc, stack)
    return nc


_PROG_CACHE = {}


def _get_program(S, T, HID):
    key = (S, T, HID)
    if key not in _PROG_CACHE:
        _PROG_CACHE[key] = build_program(S, T, HID)
    return _PROG_CACHE[key]


def kernel(L0_attn_x, L0_attn_w_q, L0_attn_w_k, L0_attn_w_v, L0_attn_g_q,
           L0_attn_g_k, L0_attn_w_o, L0_attn_b_o, L0_attn_gamma, L0_attn_beta,
           L0_ffn_w_gate, L0_ffn_w_up, L0_ffn_w_down):
    x = np.ascontiguousarray(np.asarray(L0_attn_x, dtype=np.float32))
    B, T, D_ = x.shape
    assert D_ == D
    n_cores = 8
    S = B // n_cores
    HID = L0_ffn_w_gate.shape[0]
    cn = _mk_consts(np.asarray(L0_attn_b_o, np.float32),
                    np.asarray(L0_attn_gamma, np.float32),
                    np.asarray(L0_attn_beta, np.float32),
                    np.asarray(L0_attn_g_q, np.float32),
                    np.asarray(L0_attn_g_k, np.float32))
    wgT = np.ascontiguousarray(np.asarray(L0_ffn_w_gate, np.int32).T)
    wuT = np.ascontiguousarray(np.asarray(L0_ffn_w_up, np.int32).T)
    wdT = np.ascontiguousarray(np.asarray(L0_ffn_w_down, np.int32).T)
    base = {
        "wq": np.ascontiguousarray(np.asarray(L0_attn_w_q, np.float32)),
        "wk": np.ascontiguousarray(np.asarray(L0_attn_w_k, np.float32)),
        "wv": np.ascontiguousarray(np.asarray(L0_attn_w_v, np.float32)),
        "wo": np.ascontiguousarray(np.asarray(L0_attn_w_o, np.float32)),
        "wgT": wgT, "wuT": wuT, "wdT": wdT, "cn": cn,
    }
    in_maps = [dict(base, x=np.ascontiguousarray(x[S * i:S * (i + 1)]))
               for i in range(n_cores)]
    nc = _get_program(S, T, HID)
    res = run_bass_kernel_spmd(nc, in_maps, core_ids=list(range(n_cores)))
    return np.concatenate([r["out"] for r in res.results], axis=0)


# version 22
# speedup vs baseline: 1.0103x; 1.0103x over previous
import contextlib
import numpy as np
import concourse.bass as bass
import concourse.mybir as mybir
from concourse.bass_utils import run_bass_kernel_spmd

F32 = mybir.dt.float32
BF16 = mybir.dt.bfloat16
F16 = mybir.dt.float16
I32 = mybir.dt.int32
AF = mybir.ActivationFunctionType
ALU = mybir.AluOpType
AX = mybir.AxisListType

ENGS = ("sync", "scalar", "vector", "tensor", "gpsimd")
IN_ORDER = {"tensor"}

D = 1024
H = 16
DH = 64
NCH = 8
TGT = 512
NSB = TGT // 128
S127 = 1.0 / 127.0


NPOOL = 12
DMA_QUEUES = ("sync", "gpsimd")
OPCAP = None


class _Op:
    __slots__ = ("eng", "fn", "dma", "idx", "need_inc", "deps", "incs", "post",
                 "waits", "tag", "semk", "slot", "defer", "gkey")


class Em:

    def __init__(self):
        self.ops = []
        self.last_w = {}
        self.last_r = {}
        self.pending_barrier = None
        self.barrier_seen = set()
        self.dma_count = {q: 0 for q in DMA_QUEUES}
        self.slot_last = {q: {} for q in DMA_QUEUES}
        self.open_groups = {}
        self.in_open = set()
        self.last_closed_tensor = None

    def _dep(self, d, gkey=None):
        if d.idx in self.in_open:
            if gkey is not None and d.gkey == gkey:
                return None
            sub = self.last_closed_tensor
            assert sub is not None and sub.idx > d.idx, (
                "open-group dep with no later observable PE op")
            return sub.idx
        return d.idx

    def op(self, eng, fn, reads=(), writes=(), dma=False, tag="", group=None,
           gclose=False):
        if OPCAP is not None and len(self.ops) >= OPCAP:
            d = _Op(); d.idx = -1; d.defer = None; d.gkey = None
            return d
        o = _Op()
        o.eng = eng
        o.fn = fn
        o.dma = dma
        o.tag = tag
        o.idx = len(self.ops)
        o.need_inc = dma
        o.slot = None
        o.defer = None
        o.gkey = group
        if eng == "tensor":
            if group is not None:
                chain = self.open_groups.setdefault(group, [])
                chain.append(o)
                if gclose:
                    for m in chain[:-1]:
                        m.defer = o.idx
                        self.in_open.discard(m.idx)
                    del self.open_groups[group]
                    self.last_closed_tensor = o
                else:
                    self.in_open.add(o.idx)
            else:
                self.last_closed_tensor = o
        deps = set()
        if self.pending_barrier is not None and eng not in self.barrier_seen:
            deps.update(self.pending_barrier)
            self.barrier_seen.add(eng)
        for r in reads:
            w = self.last_w.get(r)
            if w is not None:
                deps.add(self._dep(w, group))
        for r in writes:
            w = self.last_w.get(r)
            if w is not None:
                deps.add(self._dep(w, group))
            for rd in self.last_r.get(r, {}).values():
                deps.add(self._dep(rd, group))
        if dma:
            n = self.dma_count[eng]
            self.dma_count[eng] = n + 1
            o.slot = n % NPOOL
            prev = self.slot_last[eng].get(o.slot)
            if prev is not None:
                deps.add(prev.idx)
            self.slot_last[eng][o.slot] = o
        deps.discard(None)
        deps.discard(o.idx)
        o.deps = sorted(d for d in deps if d < o.idx)
        self.ops.append(o)
        for r in writes:
            self.last_w[r] = o
            self.last_r[r] = {}
        for r in reads:
            self.last_r.setdefault(r, {})[eng] = o
        return o

    def barrier(self):
        targets = []
        last_by_eng = {}
        for o in self.ops:
            last_by_eng[o.eng] = o
        for e, lo in last_by_eng.items():
            if e in DMA_QUEUES:
                targets.extend(p.idx for p in self.slot_last[e].values())
            else:
                targets.append(lo.idx)
        self.pending_barrier = sorted(set(targets))
        self.barrier_seen = set()

    def finalize(self, nc, stack):
        assert not self.open_groups, f"unclosed groups: {list(self.open_groups)}"

        def fin(idx):
            while self.ops[idx].defer is not None:
                idx = self.ops[idx].defer
            return idx

        for o in self.ops:
            o.deps = sorted({fin(d) for d in o.deps})
        for o in self.ops:
            for di in o.deps:
                d = self.ops[di]
                if d.eng == o.eng and o.eng in IN_ORDER:
                    continue
                d.need_inc = True
        semk_all = [("eng", e) for e in ENGS]
        for q in DMA_QUEUES:
            semk_all += [("slot", q, i) for i in range(NPOOL)]
        sems = {k: stack.enter_context(
            nc.semaphore(("sem_%s" % k[1]) if k[0] == "eng"
                         else "sem_%s_s%d" % (k[1], k[2])))
            for k in semk_all}
        cnt = {k: 0 for k in semk_all}
        for o in self.ops:
            o.semk = ("slot", o.eng, o.slot) if o.dma else ("eng", o.eng)
            o.incs = (16 if o.dma else 1) if o.need_inc else 0
            cnt[o.semk] += o.incs
            o.post = cnt[o.semk]
        per_eng = {e: [] for e in ENGS}
        for o in self.ops:
            per_eng[o.eng].append(o)
        for e in ENGS:
            hi = {}
            for o in per_eng[e]:
                need = {}
                for di in o.deps:
                    d = self.ops[di]
                    if d.eng == e and e in IN_ORDER:
                        continue
                    assert d.need_inc and d.post > 0
                    if d.post > hi.get(d.semk, 0):
                        need[d.semk] = max(need.get(d.semk, 0), d.post)
                o.waits = [(sems[k], v) for k, v in need.items()]
                if e != "tensor":
                    for k, v in need.items():
                        hi[k] = v
        block = stack.enter_context(nc.Block())

        def mk(e, items):
            def body(eng):
                for o in items:
                    for (sem, val) in o.waits:
                        eng.wait_ge(sem, val)
                    inst = o.fn(eng)
                    if o.incs:
                        if inst is None:
                            inst = eng.nop(nofuse=True)
                        inst.then_inc(sems[o.semk], o.incs)
                for k in semk_all:
                    if (k[1] == e) and cnt[k] > 0:
                        eng.wait_ge(sems[k], cnt[k])
            return body

        for e in ENGS:
            if per_eng[e]:
                getattr(block, e)(mk(e, per_eng[e]))
        return sems, cnt


C_IDENT = 0
C_GQBLK = 128
C_GKBLK = 130
C_BLK2 = 132
C_ONES = 134
C_TANHB = 135
C_EPSLN = 136
C_TANHB2 = 137
C_BO = 138
C_GAMMA = 146
C_BETA = 154
C_SEL1 = 162
C_SEL05 = 290
CC = 512


def _mk_consts(b_o, gamma, beta, g_q, g_k):
    cn = np.zeros((128, CC), np.float32)
    cn[:, C_IDENT:C_IDENT + 128] = np.eye(128, dtype=np.float32)
    cn[0:64, C_GQBLK] = g_q
    cn[64:128, C_GQBLK + 1] = g_q
    cn[0:64, C_GKBLK] = g_k
    cn[64:128, C_GKBLK + 1] = g_k
    cn[0:64, C_BLK2] = 1.0
    cn[64:128, C_BLK2 + 1] = 1.0
    cn[:, C_ONES] = 1.0
    cn[:, C_TANHB] = -4.05
    b_exact = -(15.0 * 0.27)
    cn[:, C_TANHB2] = np.float32(b_exact - np.float64(np.float32(-4.05)))
    cn[:, C_EPSLN] = 1e-5
    cn[:, C_BO:C_BO + 8] = b_o.reshape(8, 128).T
    cn[:, C_GAMMA:C_GAMMA + 8] = gamma.reshape(8, 128).T
    cn[:, C_BETA:C_BETA + 8] = beta.reshape(8, 128).T
    cn[0, C_SEL1:C_SEL1 + 64] = 1.0
    cn[1, C_SEL1 + 64:C_SEL1 + 128] = 1.0
    cn[0, C_SEL05:C_SEL05 + 64] = 0.5
    cn[1, C_SEL05 + 64:C_SEL05 + 128] = 0.5
    return cn


def build_program(S, T, HID, phases=(1, 1, 1)):
    NTG = T // TGT
    NKT = T // 128
    NHC = HID // 128
    nc = bass.Bass("TRN2", target_bir_lowering=False)
    stack = contextlib.ExitStack()
    nc._keepalive_stack = stack

    x_hbm = nc.dram_tensor("x", [S, T, D], F32, kind="ExternalInput").ap()
    wq_h = nc.dram_tensor("wq", [D, D], F32, kind="ExternalInput").ap()
    wk_h = nc.dram_tensor("wk", [D, D], F32, kind="ExternalInput").ap()
    wv_h = nc.dram_tensor("wv", [D, D], F32, kind="ExternalInput").ap()
    wo_h = nc.dram_tensor("wo", [D, D], F32, kind="ExternalInput").ap()
    wgT_h = nc.dram_tensor("wgT", [D, HID], I32, kind="ExternalInput").ap()
    wuT_h = nc.dram_tensor("wuT", [D, HID], I32, kind="ExternalInput").ap()
    wdT_h = nc.dram_tensor("wdT", [HID, D], I32, kind="ExternalInput").ap()
    cn_h = nc.dram_tensor("cn", [128, CC], F32, kind="ExternalInput").ap()
    out_h = nc.dram_tensor("out", [S, T, D], F32, kind="ExternalOutput").ap()
    sc_xnh = nc.dram_tensor("sc_xnh", [S, D, T], F16).ap()
    sc_xnl = nc.dram_tensor("sc_xnl", [S, D, T], F16).ap()
    sc_xth = nc.dram_tensor("sc_xth", [S, D, T], F16).ap()
    sc_xtl = nc.dram_tensor("sc_xtl", [S, D, T], F16).ap()
    sc_vp = nc.dram_tensor("sc_vp", [S, T, D], F32).ap()
    sc_k = nc.dram_tensor("sc_k", [S, D, T], F32).ap()
    sc_s = nc.dram_tensor("sc_s", [S, D, T], F32).ap()
    sc_wg = nc.dram_tensor("sc_wg", [NHC, 128, NCH * 128], F16).ap()
    sc_wu = nc.dram_tensor("sc_wu", [NHC, 128, NCH * 128], F16).ap()
    sc_wd = nc.dram_tensor("sc_wd", [NCH, 128, NHC * 128], F16).ap()
    sc_wa = {w: nc.dram_tensor(f"sc_{w}", [128, NCH * D], F16).ap()
             for w in ("wqh", "wql", "wkh", "wkl", "wvh", "wvl")}

    em = Em()

    cn = stack.enter_context(nc.sbuf_tensor("cn_sb", [128, CC], F32))
    c_all = stack.enter_context(nc.sbuf_tensor("c_all", [128, 8], F32))
    ps = stack.enter_context(nc.psum_tensor("ps", [128, 4096], F32))

    def PS(b, o=0, w=512):
        return ps[:, 512 * b + o: 512 * b + o + w]

    ident = cn[:, C_IDENT:C_IDENT + 128]
    gqblk = cn[:, C_GQBLK:C_GQBLK + 2]
    gkblk = cn[:, C_GKBLK:C_GKBLK + 2]
    ones = cn[:, C_ONES:C_ONES + 1]
    tanhb = cn[:, C_TANHB:C_TANHB + 1]
    epsln = cn[:, C_EPSLN:C_EPSLN + 1]
    sel1 = cn[0:2, C_SEL1:C_SEL1 + 128]
    sel05 = cn[0:2, C_SEL05:C_SEL05 + 128]

    em.op("sync", lambda e: e.dma_start(out=cn[:], in_=cn_h[:, :]),
          writes=["cn"], dma=True, tag="cn_load")
    em.op("vector", lambda e: e.tensor_add(
        out=tanhb2[:], in0=cn[:, C_TANHB:C_TANHB + 1],
        in1=cn[:, C_TANHB2:C_TANHB2 + 1]),
        reads=["cn"], writes=["tanhb2"])

    nmu = stack.enter_context(nc.sbuf_tensor("nmu", [128, NSB], F32))
    var = stack.enter_context(nc.sbuf_tensor("var", [128, NSB], F32))
    rs = stack.enter_context(nc.sbuf_tensor("rs", [128, NSB], F32))
    sstile = stack.enter_context(nc.sbuf_tensor("sstile", [128, 64], F32))
    gdot = stack.enter_context(nc.sbuf_tensor("gdot", [128, 64], F32))
    gk_tm = stack.enter_context(nc.sbuf_tensor("gk_tm", [128, 64], F32))
    gq_tm = stack.enter_context(nc.sbuf_tensor("gq_tm", [128, 64], F32))
    n2t = stack.enter_context(nc.sbuf_tensor("n2t", [2, TGT], F32))
    tanhb2 = stack.enter_context(nc.sbuf_tensor("tanhb2", [128, 1], F32))

    def layer_norm(xin, xout, sqbuf, key_in, key_out, key_sq, affine=False):
        em.op("vector", lambda e: e.reduce_sum(out=nmu[:, 0:NSB], in_=xin[:],
                                               axis=AX.X, negate=True),
              reads=[key_in], writes=["nmu"])
        em.op("vector", lambda e: e.tensor_scalar_mul(out=nmu[:, 0:NSB],
                                                      in0=nmu[:, 0:NSB],
                                                      scalar1=1.0 / D),
              reads=["nmu"], writes=["nmu"])
        for s in range(NSB):
            em.op("scalar", lambda e, s=s: e.activation(
                out=xout[:, s, :], in_=xin[:, s, :], func=AF.Identity,
                bias=nmu[:, s:s + 1], scale=1.0),
                reads=[key_in, "nmu"], writes=[key_out])
        em.op("vector", lambda e: e.tensor_mul(out=sqbuf[:], in0=xout[:],
                                               in1=xout[:]),
              reads=[key_out], writes=[key_sq])
        em.op("vector", lambda e: e.reduce_sum(out=var[:, 0:NSB], in_=sqbuf[:],
                                               axis=AX.X),
              reads=[key_sq], writes=["var"])
        em.op("vector", lambda e: e.tensor_scalar_mul(out=var[:, 0:NSB],
                                                      in0=var[:, 0:NSB],
                                                      scalar1=1.0 / D),
              reads=["var"], writes=["var"])
        em.op("scalar", lambda e: e.activation(out=rs[:, 0:NSB],
                                               in_=var[:, 0:NSB],
                                               func=AF.Sqrt,
                                               bias=epsln, scale=1.0),
              reads=["var", "cn"], writes=["rs"])
        em.op("vector", lambda e: e.reciprocal(out=rs[:, 0:NSB],
                                               in_=rs[:, 0:NSB]),
              reads=["rs"], writes=["rs"])
        for s in range(NSB):
            em.op("scalar", lambda e, s=s: e.activation(
                out=xout[:, s, :], in_=xout[:, s, :], func=AF.Identity,
                bias=0.0, scale=rs[:, s:s + 1]),
                reads=[key_out, "rs"], writes=[key_out])

    def layer_norm2(xin, xout, key_in, key_out):
        em.op("vector", lambda e: e.reduce_sum(out=nmu[:, 0:NSB], in_=xin[:],
                                               axis=AX.X, negate=True),
              reads=[key_in], writes=["nmu"])
        em.op("vector", lambda e: e.tensor_scalar_mul(out=nmu[:, 0:NSB],
                                                      in0=nmu[:, 0:NSB],
                                                      scalar1=1.0 / D),
              reads=["nmu"], writes=["nmu"])
        em.op("vector", lambda e: e.tensor_mul(out=xout[:], in0=xin[:],
                                               in1=xin[:]),
              reads=[key_in], writes=[key_out])
        em.op("vector", lambda e: e.reduce_sum(out=var[:, 0:NSB], in_=xout[:],
                                               axis=AX.X),
              reads=[key_out], writes=["var"])
        em.op("vector", lambda e: e.tensor_mul(out=rs[:, 0:NSB],
                                               in0=nmu[:, 0:NSB],
                                               in1=nmu[:, 0:NSB]),
              reads=["nmu"], writes=["rs"])
        em.op("vector", lambda e: e.tensor_scalar_mul(out=var[:, 0:NSB],
                                                      in0=var[:, 0:NSB],
                                                      scalar1=1.0 / D),
              reads=["var"], writes=["var"])
        em.op("vector", lambda e: e.tensor_sub(out=var[:, 0:NSB],
                                               in0=var[:, 0:NSB],
                                               in1=rs[:, 0:NSB]),
              reads=["var", "rs"], writes=["var"])
        em.op("scalar", lambda e: e.activation(out=rs[:, 0:NSB],
                                               in_=var[:, 0:NSB],
                                               func=AF.Sqrt,
                                               bias=epsln, scale=1.0),
              reads=["var", "cn"], writes=["rs"])
        em.op("vector", lambda e: e.reciprocal(out=rs[:, 0:NSB],
                                               in_=rs[:, 0:NSB]),
              reads=["rs"], writes=["rs"])
        for s in range(NSB):
            em.op("scalar", lambda e, s=s: e.activation(
                out=xout[:, s, :], in_=xin[:, s, :], func=AF.Identity,
                bias=nmu[:, s:s + 1], scale=1.0),
                reads=[key_in, "nmu"], writes=[key_out])
        for s in range(NSB):
            em.op("scalar", lambda e, s=s: e.activation(
                out=xout[:, s, :], in_=xout[:, s, :], func=AF.Identity,
                bias=0.0, scale=rs[:, s:s + 1]),
                reads=[key_out, "rs"], writes=[key_out])

    def phase_a(b, bufAd, bufBd, xth, xtl, xnh, xnl, xnstg, wkh_sb, wkl_sb,
                wvh_sb, wvl_sb, kT_tg, sq_sb, v_tg, rn2):
        em.op("vector", lambda e: e.memset(c_all[:], 0.0), writes=["c_all"])
        for wsb, wkey in ((wkh_sb, "wkh"), (wkl_sb, "wkl"),
                          (wvh_sb, "wvh"), (wvl_sb, "wvl")):
            em.op("sync", lambda e, wsb=wsb, wkey=wkey: e.dma_start(
                out=wsb[:],
                in_=sc_wa[wkey][:, :].rearrange("p (c d) -> p c d", d=D)),
                  writes=[wkey], dma=True, tag=wkey)

        def pre(g):
            gp = g % 2
            bufA, bufB = bufAd[gp], bufBd[gp]
            kA, kB = f"bufA{gp}", f"bufB{gp}"
            t0 = TGT * g
            srcx = x_hbm[b, t0:t0 + TGT, :].rearrange("(s p) d -> p s d",
                                                      p=128)
            em.op("sync", lambda e, srcx=srcx, bufA=bufA: e.dma_start(
                out=bufA[:], in_=srcx),
                  writes=[kA], dma=True, tag="xload")
            layer_norm(bufA, bufB, bufA, kA, kB, kA)
            layer_norm2(bufB, bufA, kB, kA)

        def main(g):
            gp = g % 2
            bufA, bufB = bufAd[gp], bufBd[gp]
            kA, kB = f"bufA{gp}", f"bufB{gp}"
            t0 = TGT * g
            for c in range(NCH):
                tb = 3 * (c % 2)
                for s in range(NSB):
                    em.op("tensor", lambda e, c=c, s=s, bufB=bufB, tb=tb:
                          e.transpose(
                        PS(tb, 128 * s, 128), bufB[:, s, 128 * c:128 * c + 128],
                        ident),
                        reads=[kB, "cn"], writes=[f"ps{tb}"])
                em.op("vector", lambda e, c=c, tb=tb: e.tensor_copy(
                    out=xth[:, c, :], in_=PS(tb)),
                    reads=[f"ps{tb}"], writes=["xth"])
                em.op("vector", lambda e, c=c, tb=tb: e.tensor_sub(
                    out=xtl[:, c, :], in0=PS(tb), in1=xth[:, c, :]),
                    reads=[f"ps{tb}", "xth"], writes=["xtl"])
            for sc_t, src_sb, skey in ((sc_xth, xth, "xth"), (sc_xtl, xtl,
                                                             "xtl")):
                dst = sc_t[b, :, t0:t0 + TGT].rearrange("(c p) t -> p c t",
                                                        p=128)
                em.op("gpsimd", lambda e, dst=dst, src_sb=src_sb: e.dma_start(
                    out=dst, in_=src_sb[:]),
                      reads=[skey], dma=True)
            for c in range(NCH):
                tb = 3 * (c % 2)
                for s in range(NSB):
                    em.op("tensor", lambda e, c=c, s=s, bufA=bufA, tb=tb:
                          e.transpose(
                        PS(tb, 128 * s, 128), bufA[:, s, 128 * c:128 * c + 128],
                        ident),
                        reads=[kA, "cn"], writes=[f"ps{tb}"])
                em.op("scalar", lambda e, c=c, tb=tb: e.activation(
                    out=xnstg[:], in_=PS(tb), func=AF.Identity,
                    scale=cn[:, C_GAMMA + c:C_GAMMA + c + 1],
                    bias=cn[:, C_BETA + c:C_BETA + c + 1]),
                    reads=[f"ps{tb}", "cn"], writes=["xnstg"])
                em.op("vector", lambda e, c=c: e.tensor_copy(
                    out=xnh[:, c, :], in_=xnstg[:]),
                    reads=["xnstg"], writes=["xnh"])
                em.op("vector", lambda e, c=c: e.tensor_sub(
                    out=xnl[:, c, :], in0=xnstg[:], in1=xnh[:, c, :]),
                    reads=["xnstg", "xnh"], writes=["xnl"])
            for sc_t, src_sb, skey in ((sc_xnh, xnh, "xnh"), (sc_xnl, xnl,
                                                             "xnl")):
                dst = sc_t[b, :, t0:t0 + TGT].rearrange("(c p) t -> p c t",
                                                        p=128)
                em.op("gpsimd", lambda e, dst=dst, src_sb=src_sb: e.dma_start(
                    out=dst, in_=src_sb[:]),
                      reads=[skey], dma=True)
            for j in range(NCH):
                pb = 1 + (j % 2)
                for i in range(NCH):
                    for p, (wsb, wkey, xsb, xkey) in enumerate((
                            (wkh_sb, "wkh", xnh, "xnh"),
                            (wkl_sb, "wkl", xnh, "xnh"),
                            (wkh_sb, "wkh", xnl, "xnl"))):
                        last = (i == NCH - 1 and p == 2)
                        em.op("tensor", lambda e, i=i, j=j, pb=pb, wsb=wsb,
                              xsb=xsb, first=(i == 0 and p == 0), last=last:
                              e.matmul(
                            PS(pb), wsb[:, i, 128 * j:128 * j + 128],
                            xsb[:, i, :], start=first, stop=last),
                            reads=[wkey, xkey], writes=[f"ps{pb}"],
                            group=f"kp{g}_{j}", gclose=last)
                em.op("vector", lambda e, j=j, pb=pb: e.tensor_copy(
                    out=kT_tg[:, j, :], in_=PS(pb)),
                    reads=[f"ps{pb}"], writes=["kT_tg"])
            for j in range(NCH):
                em.op("scalar", lambda e, j=j: e.activation(
                    out=sq_sb[:], in_=kT_tg[:, j, :], func=AF.Square),
                    reads=["kT_tg"], writes=["sq_sb"])
                em.op("tensor", lambda e, j=j: e.matmul(
                    ps[0:2, 512 * 3:512 * 3 + TGT], cn[:, C_BLK2:C_BLK2 + 2],
                    sq_sb[:], start=True, stop=True),
                    reads=["sq_sb", "cn"], writes=["ps3"])
                em.op("scalar", lambda e, j=j: e.activation(
                    out=n2t[:, :], in_=ps[0:2, 512 * 3:512 * 3 + TGT],
                    func=AF.Sqrt),
                    reads=["ps3"], writes=["n2t"])
                em.op("vector", lambda e: e.tensor_single_scalar(
                    out=n2t[:, :], in_=n2t[:, :], scalar=32.0 * 1e-12,
                    op=ALU.max),
                    reads=["n2t"], writes=["n2t"])
                em.op("vector", lambda e, j=j: e.reciprocal(
                    out=rn2[:, :], in_=n2t[:, :]),
                    reads=["n2t"], writes=["rn2"])
                em.op("tensor", lambda e, j=j: e.matmul(
                    PS(4), sel1, rn2[:, :], start=True, stop=True),
                    reads=["rn2", "cn"], writes=["ps4"])
                em.op("vector", lambda e, j=j: e.tensor_mul(
                    out=kT_tg[:, j, :], in0=kT_tg[:, j, :],
                    in1=PS(4)),
                    reads=["ps4", "kT_tg"], writes=["kT_tg"])
                for s in range(NSB):
                    em.op("tensor", lambda e, j=j, s=s: e.matmul(
                        ps[:, 512 * 5 + 16 * s + 2 * j:
                           512 * 5 + 16 * s + 2 * j + 2],
                        kT_tg[:, j, 128 * s:128 * s + 128], gkblk,
                        start=True, stop=True),
                        reads=["kT_tg", "cn"], writes=["ps5"])
            dst = sc_k[b, :, t0:t0 + TGT].rearrange("(c p) t -> p c t",
                                                    p=128)
            em.op("gpsimd", lambda e, dst=dst: e.dma_start(out=dst,
                                                           in_=kT_tg[:]),
                  reads=["kT_tg"], dma=True, tag="kstore")
            em.op("scalar", lambda e: e.activation(
                out=gdot[:, :], in_=ps[:, 512 * 5:512 * 5 + 64],
                func=AF.Sigmoid),
                reads=["ps5"], writes=["gdot"])
            em.op("vector", lambda e: e.tensor_scalar(
                out=gk_tm[:, :], in0=gdot[:, :], scalar1=0.75 / 32.0,
                scalar2=0.25 / 32.0, op0=ALU.mult, op1=ALU.add),
                reads=["gdot"], writes=["gk_tm"])
            for s in range(NSB):
                for jh in range(2):
                    pb = 6 + jh
                    for i in range(NCH):
                        for p, (xsb, xkey, wsb, wkey) in enumerate((
                                (xth, "xth", wvh_sb, "wvh"),
                                (xth, "xth", wvl_sb, "wvl"),
                                (xtl, "xtl", wvh_sb, "wvh"))):
                            last = (i == NCH - 1 and p == 2)
                            em.op("tensor", lambda e, i=i, s=s, jh=jh, pb=pb,
                                  xsb=xsb, wsb=wsb,
                                  first=(i == 0 and p == 0), last=last:
                                  e.matmul(PS(pb),
                                           xsb[:, i, 128 * s:128 * s + 128],
                                           wsb[:, i, 512 * jh:512 * jh + 512],
                                           start=first, stop=last),
                                  reads=[wkey, xkey], writes=[f"ps{pb}"],
                                  group=f"vp{g}_{s}_{jh}", gclose=last)
                    for hh in range(8):
                        h = 8 * jh + hh
                        em.op("vector", lambda e, s=s, jh=jh, hh=hh, h=h,
                              pb=pb: e.tensor_scalar_mul(
                                  out=v_tg[:, s, 512 * jh + 64 * hh:
                                           512 * jh + 64 * hh + 64],
                                  in0=PS(pb, 64 * hh, 64),
                                  scalar1=gk_tm[:, 16 * s + h:16 * s + h + 1]),
                              reads=[f"ps{pb}", "gk_tm"], writes=["v_tg"])
            for s in range(NSB):
                for p in range(8):
                    em.op("tensor", lambda e, s=s, p=p: e.matmul(
                        ps[:, 512 * 4 + p:512 * 4 + p + 1],
                        v_tg[:, s, 128 * p:128 * p + 128], ones,
                        start=True, stop=True),
                        reads=["v_tg", "cn"], writes=["ps4"])
                em.op("vector", lambda e, s=s: e.tensor_add(
                    out=c_all[:], in0=c_all[:], in1=ps[:, 512 * 4:512 * 4 + 8]),
                    reads=["ps4", "c_all"], writes=["c_all"])
            dst = sc_vp[b, t0:t0 + TGT, :].rearrange("(s p) d -> p s d", p=128)
            em.op("gpsimd", lambda e, dst=dst: e.dma_start(out=dst,
                                                           in_=v_tg[:]),
                  reads=["v_tg"], dma=True)

        pre(0)
        for g in range(NTG):
            if g + 1 < NTG:
                pre(g + 1)
            main(g)

    def phase_b1(b, xnh2, xnl2, qstage, sqq, qnT, wqh_sb, wql_sb, wo_sb,
                 kc_sb, vp_sb, t_sb, f1, fusedT, sT_st, gq_hm):
        for wsb, wkey in ((wqh_sb, "wqh"), (wql_sb, "wql")):
            em.op("sync", lambda e, wsb=wsb, wkey=wkey: e.dma_start(
                out=wsb[:],
                in_=sc_wa[wkey][:, :].rearrange("p (c d) -> p c d", d=D)),
                  writes=[wkey], dma=True, tag=wkey)
        em.op("sync", lambda e: e.dma_start(
            out=wo_sb[:], in_=wo_h[:, :].rearrange("(c p) d -> p c d", p=128)),
              writes=["wo_sb"], dma=True, tag="wo")

        def ldxn(qt):
            xq = qt % 2
            for sc_t, dst2, pfx in ((sc_xnh, xnh2, "xnh"), (sc_xnl, xnl2,
                                                           "xnl")):
                src = sc_t[b, :, TGT * qt:TGT * qt + TGT].rearrange(
                    "(c p) t -> p c t", p=128)
                em.op("sync", lambda e, src=src, dst=dst2[xq]: e.dma_start(
                    out=dst[:], in_=src),
                      writes=[f"{pfx}{xq}"], dma=True)

        ldxn(0)
        for qt in range(NTG):
            q0 = TGT * qt
            xp = qt % 2
            xh_t, xl_t = xnh2[xp], xnl2[xp]
            xhkey, xlkey = f"xnh{xp}", f"xnl{xp}"
            if qt + 1 < NTG:
                ldxn(qt + 1)
            for s in range(NSB):
                for jh in range(2):
                    pb = jh
                    for i in range(NCH):
                        for p, (xsb, xkey, wsb, wkey) in enumerate((
                                (xh_t, xhkey, wqh_sb, "wqh"),
                                (xh_t, xhkey, wql_sb, "wql"),
                                (xl_t, xlkey, wqh_sb, "wqh"))):
                            last = (i == NCH - 1 and p == 2)
                            em.op("tensor", lambda e, i=i, s=s, jh=jh, pb=pb,
                                  xsb=xsb, wsb=wsb,
                                  first=(i == 0 and p == 0), last=last:
                                  e.matmul(PS(pb),
                                           xsb[:, i, 128 * s:128 * s + 128],
                                           wsb[:, i, 512 * jh:512 * jh + 512],
                                           start=first, stop=last),
                                  reads=[wkey, xkey], writes=[f"ps{pb}"],
                                  group=f"qp{qt}_{s}_{jh}", gclose=last)
                    em.op("vector", lambda e, s=s, jh=jh, pb=pb: e.tensor_copy(
                        out=qstage[:, s, 512 * jh:512 * jh + 512], in_=PS(pb)),
                        reads=[f"ps{pb}"], writes=["qf"])
                    em.op("scalar", lambda e, s=s, jh=jh: e.activation(
                        out=sqq[:, 512 * jh:512 * jh + 512],
                        in_=qstage[:, s, 512 * jh:512 * jh + 512],
                        func=AF.Square),
                        reads=["qf"], writes=["sqq"])
                em.op("vector", lambda e, s=s: e.reduce_sum(
                    out=sstile[:, 16 * s:16 * s + 16],
                    in_=sqq[:].rearrange("p (h d) -> p h d", d=DH), axis=AX.X),
                    reads=["sqq"], writes=["sstile"])
            em.op("scalar", lambda e: e.activation(out=sstile[:, :],
                                                   in_=sstile[:, :],
                                                   func=AF.Sqrt),
                  reads=["sstile"], writes=["sstile"])
            em.op("vector", lambda e: e.tensor_single_scalar(
                out=sstile[:, :], in_=sstile[:, :], scalar=32.0 * 1e-12,
                op=ALU.max),
                reads=["sstile"], writes=["sstile"])
            em.op("vector", lambda e: e.reciprocal(out=sstile[:, :],
                                                   in_=sstile[:, :]),
                  reads=["sstile"], writes=["sstile"])
            for s in range(NSB):
                em.op("vector", lambda e, s=s: e.tensor_mul(
                    out=qstage[:, s, :].rearrange("p (h d) -> p h d", d=DH),
                    in0=qstage[:, s, :].rearrange("p (h d) -> p h d", d=DH),
                    in1=sstile[:, 16 * s:16 * s + 16].unsqueeze(2)
                        .broadcast_to([128, 16, DH])),
                    reads=["qf", "sstile"], writes=["qf"])
            for c in range(NCH):
                tb = 2 + (c % 2)
                for s in range(NSB):
                    em.op("tensor", lambda e, c=c, s=s, tb=tb: e.transpose(
                        PS(tb, 128 * s, 128),
                        qstage[:, s, 128 * c:128 * c + 128], ident),
                        reads=["qf", "cn"], writes=[f"ps{tb}"])
                if c % 2 == 0:
                    em.op("scalar", lambda e, c=c, tb=tb: e.activation(
                        out=qnT[:, c, :], in_=PS(tb), func=AF.Identity),
                        reads=[f"ps{tb}"], writes=["qnT"])
                else:
                    em.op("vector", lambda e, c=c, tb=tb: e.tensor_copy(
                        out=qnT[:, c, :], in_=PS(tb)),
                        reads=[f"ps{tb}"], writes=["qnT"])
            for s in range(NSB):
                for c in range(NCH):
                    em.op("tensor", lambda e, c=c, s=s: e.matmul(
                        ps[:, 512 * 3 + 16 * s + 2 * c:
                           512 * 3 + 16 * s + 2 * c + 2],
                        qnT[:, c, 128 * s:128 * s + 128], gqblk,
                        start=True, stop=True),
                        reads=["qnT", "cn"], writes=["ps3"])
            em.op("scalar", lambda e: e.activation(
                out=gdot[:, :], in_=ps[:, 512 * 3:512 * 3 + 64],
                func=AF.Sigmoid),
                reads=["ps3"], writes=["gdot"])
            em.op("vector", lambda e: e.tensor_scalar(
                out=gq_tm[:, :], in0=gdot[:, :], scalar1=0.75, scalar2=0.25,
                op0=ALU.mult, op1=ALU.add),
                reads=["gdot"], writes=["gq_tm"])
            for c in range(NCH):
                for s in range(NSB):
                    em.op("tensor", lambda e, c=c, s=s: e.transpose(
                        ps[0:2, 512 * 3 + 128 * s:512 * 3 + 128 * s + 128],
                        gq_tm[:, 16 * s + 2 * c:16 * s + 2 * c + 2], ident),
                        reads=["gq_tm", "cn"], writes=["ps3"])
                em.op("vector", lambda e, c=c: e.tensor_copy(
                    out=gq_hm[c][:, :], in_=ps[0:2, 512 * 3:512 * 3 + TGT]),
                    reads=["ps3"], writes=[f"gq_hm{c}"])
            for c in range(NCH):
                cp = c % 2
                src = sc_vp[b, :, 128 * c:128 * c + 128].rearrange(
                    "(kt p) d -> p kt d", p=128)
                em.op("sync", lambda e, src=src, cp=cp: e.dma_start(
                    out=vp_sb[cp][:], in_=src),
                    writes=[f"vp{cp}"], dma=True, tag="vp")
                em.op("sync", lambda e, c=c, cp=cp: e.dma_start(
                    out=kc_sb[cp][:], in_=sc_k[b, 128 * c:128 * c + 128, :]),
                    writes=[f"kc{cp}"], dma=True, tag="kc")

                def align(kt):
                    alb = 2 + 2 * (kt % 2)
                    em.op("tensor", lambda e, kt=kt, alb=alb, cp=cp, c=c:
                          e.matmul(
                        PS(alb), kc_sb[cp][0:64, 128 * kt:128 * kt + 128],
                        qnT[0:64, c, :], start=True, stop=True),
                        reads=[f"kc{cp}", "qnT"], writes=[f"ps{alb}"])
                    em.op("tensor", lambda e, kt=kt, alb=alb, cp=cp, c=c:
                          e.matmul(
                        PS(alb + 1), kc_sb[cp][64:128, 128 * kt:128 * kt + 128],
                        qnT[64:128, c, :], start=True, stop=True,
                        tile_position=(64, 0)),
                        reads=[f"kc{cp}", "qnT"], writes=[f"ps{alb + 1}"])

                align(0)
                for kt in range(NKT):
                    kp = kt % 2
                    alb = 2 + 2 * kp
                    em.op("scalar", lambda e, kp=kp, alb=alb: e.activation(
                        out=t_sb[kp][:], in_=ps[:, 512 * alb:512 * alb + 1024],
                        func=AF.Tanh, scale=15.0, bias=tanhb2[:, 0:1]),
                        reads=[f"ps{alb}", f"ps{alb + 1}", "tanhb2"],
                        writes=[f"t{kp}"])
                    if kt + 1 < NKT:
                        align(kt + 1)
                    em.op("tensor", lambda e, kt=kt, kp=kp, cp=cp: e.matmul(
                        ps[0:64, 512 * 6:512 * 6 + 512],
                        vp_sb[cp][:, kt, 0:64], t_sb[kp][:, 0:512],
                        start=(kt == 0), stop=(kt == NKT - 1)),
                        reads=[f"vp{cp}", f"t{kp}"], writes=["ps6"],
                        group=f"fuA{qt}_{c}", gclose=(kt == NKT - 1))
                    em.op("tensor", lambda e, kt=kt, kp=kp, cp=cp: e.matmul(
                        ps[64:128, 512 * 7:512 * 7 + 512],
                        vp_sb[cp][:, kt, 64:128], t_sb[kp][:, 512:1024],
                        start=(kt == 0), stop=(kt == NKT - 1),
                        tile_position=(0, 64)),
                        reads=[f"vp{cp}", f"t{kp}"], writes=["ps7"],
                        group=f"fuB{qt}_{c}", gclose=(kt == NKT - 1))
                em.op("tensor", lambda e, c=c: e.matmul(
                    PS(2), sel05, gq_hm[c][:, :], start=True, stop=True),
                    reads=[f"gq_hm{c}", "cn"], writes=["ps2"])
                em.op("vector", lambda e: e.tensor_copy(
                    out=f1[:], in_=PS(2)),
                    reads=["ps2"], writes=["f1"])
                em.op("vector", lambda e, c=c: e.scalar_tensor_tensor(
                    out=fusedT[0:64, c, :], in0=ps[0:64, 512 * 6:512 * 6 + 512],
                    scalar=c_all[0:64, c:c + 1], in1=f1[0:64, :],
                    op0=ALU.add, op1=ALU.mult),
                    reads=["ps6", "f1", "c_all"], writes=["qf"])
                em.op("vector", lambda e, c=c: e.scalar_tensor_tensor(
                    out=fusedT[64:128, c, :], in0=ps[64:128, 512 * 7:512 * 7 + 512],
                    scalar=c_all[64:128, c:c + 1], in1=f1[64:128, :],
                    op0=ALU.add, op1=ALU.mult),
                    reads=["ps7", "f1", "c_all"], writes=["qf"])
            for j in range(NCH):
                pb = j % 2
                for i in range(NCH):
                    em.op("tensor", lambda e, i=i, j=j, pb=pb: e.matmul(
                        PS(pb), wo_sb[:, i, 128 * j:128 * j + 128],
                        fusedT[:, i, :], start=(i == 0), stop=(i == NCH - 1)),
                        reads=["wo_sb", "qf"], writes=[f"ps{pb}"],
                        group=f"wo{qt}_{j}", gclose=(i == NCH - 1))
                sp = j % 2
                em.op("scalar", lambda e, j=j, pb=pb, sp=sp: e.activation(
                    out=sT_st[sp][:], in_=PS(pb), func=AF.Identity,
                    bias=cn[:, C_BO + j:C_BO + j + 1]),
                    reads=[f"ps{pb}", "cn"], writes=[f"sT{sp}"])
                dst = sc_s[b, 128 * j:128 * j + 128, q0:q0 + TGT]
                em.op("gpsimd", lambda e, dst=dst, sp=sp: e.dma_start(
                    out=dst, in_=sT_st[sp][:]),
                    reads=[f"sT{sp}"], dma=True)

    def phase_b2(b, xh2d, xl2d, wg_sb, wu_sb, wd_sb, hf, sg_sb, hh, hl,
                 g_sb, bufx, s_sb, prod, out_sb):
        NHC_l = NHC

        def ldx(qt):
            xp = qt % 2
            for sc_t, dst2, pfx in ((sc_xth, xh2d, "xbh"), (sc_xtl, xl2d,
                                                           "xbl")):
                src = sc_t[b, :, TGT * qt:TGT * qt + TGT].rearrange(
                    "(c p) t -> p c t", p=128)
                em.op("sync", lambda e, src=src, dst=dst2[xp]: e.dma_start(
                    out=dst[:], in_=src),
                      writes=[f"{pfx}{xp}"], dma=True)

        ldx(0)
        for qt in range(NTG):
            q0 = TGT * qt
            xh, xl = xh2d[qt % 2], xl2d[qt % 2]
            xhk, xlk = f"xbh{qt % 2}", f"xbl{qt % 2}"
            if qt + 1 < NTG:
                ldx(qt + 1)
            for hc in range(NHC_l):
                wp = hc % 2
                em.op("sync", lambda e, hc=hc, wp=wp: e.dma_start(
                    out=wg_sb[wp][:],
                    in_=sc_wg[hc, :, :].rearrange("p (c h) -> p c h", h=128)),
                    writes=[f"wg{wp}"], dma=True, tag="wg")
                em.op("sync", lambda e, hc=hc, wp=wp: e.dma_start(
                    out=wu_sb[wp][:],
                    in_=sc_wu[hc, :, :].rearrange("p (c h) -> p c h", h=128)),
                    writes=[f"wu{wp}"], dma=True, tag="wu")

                for i in range(NCH):
                    em.op("tensor", lambda e, i=i, wp=wp, xh=xh: e.matmul(
                        PS(0), wg_sb[wp][:, i, :], xh[:, i, :],
                        start=(i == 0), stop=False),
                        reads=[f"wg{wp}", xhk], writes=["ps0"],
                        group=f"ga{qt}_{hc}")
                    em.op("tensor", lambda e, i=i, wp=wp, xl=xl: e.matmul(
                        PS(0), wg_sb[wp][:, i, :], xl[:, i, :],
                        start=False, stop=(i == NCH - 1)),
                        reads=[f"wg{wp}", xlk], writes=["ps0"],
                        group=f"ga{qt}_{hc}", gclose=(i == NCH - 1))
                for i in range(NCH):
                    em.op("tensor", lambda e, i=i, wp=wp, xh=xh: e.matmul(
                        PS(1), wu_sb[wp][:, i, :], xh[:, i, :],
                        start=(i == 0), stop=False),
                        reads=[f"wu{wp}", xhk], writes=["ps1"],
                        group=f"up{qt}_{hc}")
                    em.op("tensor", lambda e, i=i, wp=wp, xl=xl: e.matmul(
                        PS(1), wu_sb[wp][:, i, :], xl[:, i, :],
                        start=False, stop=(i == NCH - 1)),
                        reads=[f"wu{wp}", xlk], writes=["ps1"],
                        group=f"up{qt}_{hc}", gclose=(i == NCH - 1))
                em.op("scalar", lambda e: e.activation(
                    out=sg_sb[:], in_=PS(0), func=AF.Silu, scale=S127),
                    reads=["ps0"], writes=["sg_sb"])
                em.op("vector", lambda e: e.scalar_tensor_tensor(
                    out=hf[:], in0=PS(1), scalar=S127, in1=sg_sb[:],
                    op0=ALU.mult, op1=ALU.mult),
                    reads=["ps1", "sg_sb"], writes=["hf"])
                em.op("vector", lambda e, hc=hc: e.tensor_copy(
                    out=hh[:, hc, :], in_=hf[:]),
                    reads=["hf"], writes=["hh"])
                em.op("vector", lambda e, hc=hc: e.tensor_sub(
                    out=hl[:, hc, :], in0=hf[:], in1=hh[:, hc, :]),
                    reads=["hf", "hh"], writes=["hl"])
            for j in range(NCH):
                pb = 2 + (j % 2)
                wp = j % 2
                em.op("sync", lambda e, j=j, wp=wp: e.dma_start(
                    out=wd_sb[wp][:],
                    in_=sc_wd[j, :, :].rearrange("p (c d) -> p c d", d=128)),
                    writes=[f"wd{wp}"], dma=True, tag="wd")
                for hc in range(NHC_l):
                    em.op("tensor", lambda e, hc=hc, j=j, pb=pb, wp=wp:
                          e.matmul(
                        PS(pb), wd_sb[wp][:, hc, :],
                        hh[:, hc, :], start=(hc == 0), stop=False),
                        reads=[f"wd{wp}", "hh"], writes=[f"ps{pb}"],
                        group=f"dn{qt}_{j}")
                    em.op("tensor", lambda e, hc=hc, j=j, pb=pb, wp=wp:
                          e.matmul(
                        PS(pb), wd_sb[wp][:, hc, :],
                        hl[:, hc, :], start=False, stop=(hc == NHC_l - 1)),
                        reads=[f"wd{wp}", "hl"], writes=[f"ps{pb}"],
                        group=f"dn{qt}_{j}", gclose=(hc == NHC_l - 1))
                em.op("scalar", lambda e, j=j, pb=pb: e.activation(
                    out=g_sb[:, j, :], in_=PS(pb), func=AF.Silu, scale=S127),
                    reads=[f"ps{pb}"], writes=["g_sb"])
            src = x_hbm[b, q0:q0 + TGT, :].rearrange("(s p) d -> p s d", p=128)
            em.op("sync", lambda e, src=src: e.dma_start(out=bufx[:], in_=src),
                  writes=["bufx"], dma=True)
            src = sc_s[b, :, q0:q0 + TGT].rearrange("(c p) t -> p c t", p=128)
            em.op("sync", lambda e, src=src: e.dma_start(out=s_sb[:], in_=src),
                  writes=["s_sb"], dma=True)
            for j in range(NCH):
                tb = 4 + (j % 2)
                em.op("vector", lambda e, j=j: e.tensor_mul(
                    out=prod[:], in0=g_sb[:, j, :], in1=s_sb[:, j, :]),
                    reads=["g_sb", "s_sb"], writes=["hf"])
                for s in range(NSB):
                    em.op("tensor", lambda e, j=j, s=s, tb=tb: e.transpose(
                        PS(tb, 128 * s, 128), prod[:, 128 * s:128 * s + 128],
                        ident),
                        reads=["hf", "cn"], writes=[f"ps{tb}"])
                for s in range(NSB):
                    em.op("vector", lambda e, j=j, s=s, tb=tb: e.tensor_add(
                        out=out_sb[:, s, 128 * j:128 * j + 128],
                        in0=PS(tb, 128 * s, 128),
                        in1=bufx[:, s, 128 * j:128 * j + 128]),
                        reads=[f"ps{tb}", "bufx"], writes=["out_sb"])
            dst = out_h[b, q0:q0 + TGT, :].rearrange("(s p) d -> p s d", p=128)
            em.op("gpsimd", lambda e, dst=dst: e.dma_start(out=dst,
                                                           in_=out_sb[:]),
                  reads=["out_sb"], dma=True, tag="out")

    with contextlib.ExitStack() as pp:
        wbuf = [pp.enter_context(nc.sbuf_tensor(f"wrep{i}", [128, HID], F16))
                for i in range(2)]
        for c in range(NCH):
            wp = c % 2
            for (wsrc, wdst, tag) in ((wgT_h, sc_wg, "rg"), (wuT_h, sc_wu, "ru")):
                em.op("gpsimd", lambda e, c=c, wp=wp, wsrc=wsrc: e.dma_start(
                    out=wbuf[wp][:],
                    in_=wsrc[128 * c:128 * c + 128, :]),
                    writes=[f"wrep{wp}"], dma=True, tag=tag)
                dst = wdst[:, :, 128 * c:128 * c + 128].rearrange(
                    "h p j -> p h j")
                em.op("gpsimd", lambda e, dst=dst, wp=wp: e.dma_start(
                    out=dst, in_=wbuf[wp][:].rearrange("p (h j) -> p h j",
                                                       j=128)),
                    reads=[f"wrep{wp}"], dma=True, tag=tag)
        wdbuf = [pp.enter_context(nc.sbuf_tensor(f"wdrep{i}", [128, D], F16))
                 for i in range(2)]
        for c in range(NHC):
            wp = c % 2
            em.op("gpsimd", lambda e, c=c, wp=wp: e.dma_start(
                out=wdbuf[wp][:],
                in_=wdT_h[128 * c:128 * c + 128, :]),
                writes=[f"wdrep{wp}"], dma=True, tag="rd")
            dst = sc_wd[:, :, 128 * c:128 * c + 128].rearrange(
                "j p d -> p j d")
            em.op("gpsimd", lambda e, dst=dst, wp=wp: e.dma_start(
                out=dst, in_=wdbuf[wp][:].rearrange("p (j d) -> p j d",
                                                    d=128)),
                reads=[f"wdrep{wp}"], dma=True, tag="rd")
        aw_stg = [pp.enter_context(nc.sbuf_tensor(f"awstg{i}", [128, D], F32))
                  for i in range(2)]
        aw_hi = [pp.enter_context(nc.sbuf_tensor(f"awhi{i}", [128, D], F16))
                 for i in range(2)]
        aw_lo = [pp.enter_context(nc.sbuf_tensor(f"awlo{i}", [128, D], F16))
                 for i in range(2)]
        for wi, (wsrc, hkey, lkey) in enumerate((
                (wq_h, "wqh", "wql"), (wk_h, "wkh", "wkl"),
                (wv_h, "wvh", "wvl"))):
            for c in range(NCH):
                wp = (wi * NCH + c) % 2
                em.op("sync", lambda e, c=c, wp=wp, wsrc=wsrc: e.dma_start(
                    out=aw_stg[wp][:], in_=wsrc[128 * c:128 * c + 128, :]),
                    writes=[f"awstg{wp}"], dma=True, tag="aw")
                em.op("vector", lambda e, wp=wp: e.tensor_scalar_mul(
                    out=aw_stg[wp][:], in0=aw_stg[wp][:], scalar1=32.0),
                    reads=[f"awstg{wp}"], writes=[f"awstg{wp}"])
                em.op("vector", lambda e, wp=wp: e.tensor_copy(
                    out=aw_hi[wp][:], in_=aw_stg[wp][:]),
                    reads=[f"awstg{wp}"], writes=[f"awhi{wp}"])
                em.op("vector", lambda e, wp=wp: e.tensor_sub(
                    out=aw_lo[wp][:], in0=aw_stg[wp][:], in1=aw_hi[wp][:]),
                    reads=[f"awstg{wp}", f"awhi{wp}"], writes=[f"awlo{wp}"])
                em.op("gpsimd", lambda e, c=c, wp=wp, hkey=hkey: e.dma_start(
                    out=sc_wa[hkey][:, D * c:D * c + D], in_=aw_hi[wp][:]),
                    reads=[f"awhi{wp}"], dma=True, tag="aw")
                em.op("gpsimd", lambda e, c=c, wp=wp, lkey=lkey: e.dma_start(
                    out=sc_wa[lkey][:, D * c:D * c + D], in_=aw_lo[wp][:]),
                    reads=[f"awlo{wp}"], dma=True, tag="aw")
    em.barrier()
    for b in range(S):
        if True:
            with contextlib.ExitStack() as pa:
                bufAd = [pa.enter_context(
                    nc.sbuf_tensor(f"bufA{i}_{b}", [128, NSB, D], F32))
                    for i in range(2)]
                bufBd = [pa.enter_context(
                    nc.sbuf_tensor(f"bufB{i}_{b}", [128, NSB, D], F32))
                    for i in range(2)]
                xth = pa.enter_context(nc.sbuf_tensor(f"xth_{b}", [128, NCH, TGT],
                                                      F16))
                xtl = pa.enter_context(nc.sbuf_tensor(f"xtl_{b}", [128, NCH, TGT],
                                                      F16))
                xnh = pa.enter_context(nc.sbuf_tensor(f"xnh_{b}", [128, NCH, TGT],
                                                      F16))
                xnl = pa.enter_context(nc.sbuf_tensor(f"xnl_{b}", [128, NCH, TGT],
                                                      F16))
                xnstg = pa.enter_context(
                    nc.sbuf_tensor(f"xnstg_{b}", [128, TGT], F32))
                wkh_sb = pa.enter_context(
                    nc.sbuf_tensor(f"wkh_sb_{b}", [128, NCH, D], F16))
                wkl_sb = pa.enter_context(
                    nc.sbuf_tensor(f"wkl_sb_{b}", [128, NCH, D], F16))
                wvh_sb = pa.enter_context(
                    nc.sbuf_tensor(f"wvh_sb_{b}", [128, NCH, D], F16))
                wvl_sb = pa.enter_context(
                    nc.sbuf_tensor(f"wvl_sb_{b}", [128, NCH, D], F16))
                kT_tg = pa.enter_context(
                    nc.sbuf_tensor(f"kT_tg_{b}", [128, NCH, TGT], F32))
                sq_sb = pa.enter_context(nc.sbuf_tensor(f"sq_sb_{b}", [128, TGT],
                                                        F32))
                v_tg = pa.enter_context(nc.sbuf_tensor(f"v_tg_{b}", [128, NSB, D],
                                                       F32))
                rn2 = pa.enter_context(nc.sbuf_tensor(f"rn2_{b}", [2, TGT],
                                                      F32))
                if phases[0]:
                    phase_a(b, bufAd, bufBd, xth, xtl, xnh, xnl, xnstg,
                            wkh_sb, wkl_sb, wvh_sb, wvl_sb,
                            kT_tg, sq_sb, v_tg, rn2)
            em.barrier()
            with contextlib.ExitStack() as pb:
                xnh2 = [pb.enter_context(
                    nc.sbuf_tensor(f"xnh{i}_{b}", [128, NCH, TGT], F16))
                    for i in range(2)]
                xnl2 = [pb.enter_context(
                    nc.sbuf_tensor(f"xnl{i}_{b}", [128, NCH, TGT], F16))
                    for i in range(2)]
                qf = pb.enter_context(
                    nc.sbuf_tensor(f"qf_{b}", [128, NSB * D], F32))
                qstage = qf[:].rearrange("p (s c) -> p s c", s=NSB)
                sqq = pb.enter_context(nc.sbuf_tensor(f"sqq_{b}", [128, D], F32))
                qnT = pb.enter_context(nc.sbuf_tensor(f"qnT_{b}", [128, NCH, TGT],
                                                      F32))
                wqh_sb = pb.enter_context(
                    nc.sbuf_tensor(f"wqh_sb_{b}", [128, NCH, D], F16))
                wql_sb = pb.enter_context(
                    nc.sbuf_tensor(f"wql_sb_{b}", [128, NCH, D], F16))
                wo_sb = pb.enter_context(
                    nc.sbuf_tensor(f"wo_sb_{b}", [128, NCH, D], F32))
                kc_sb = [pb.enter_context(
                    nc.sbuf_tensor(f"kc{i}_{b}", [128, T], F32))
                    for i in range(2)]
                vp_sb = [pb.enter_context(
                    nc.sbuf_tensor(f"vp{i}_{b}", [128, NKT, 128], F32))
                    for i in range(2)]
                t_sb = [pb.enter_context(
                    nc.sbuf_tensor(f"t{i}_{b}", [128, 1024], F32))
                    for i in range(2)]
                f1 = pb.enter_context(nc.sbuf_tensor(f"f1_{b}", [128, TGT], F32))
                fusedT = qf[:].rearrange("p (c t) -> p c t", c=NCH)
                sT_st = [pb.enter_context(
                    nc.sbuf_tensor(f"sT{i}_{b}", [128, TGT], F32))
                    for i in range(2)]
                gq_hm = [pb.enter_context(
                    nc.sbuf_tensor(f"gq_hm{c}_{b}", [2, TGT], F32))
                    for c in range(NCH)]
                if phases[1]:
                    phase_b1(b, xnh2, xnl2, qstage, sqq, qnT, wqh_sb, wql_sb,
                             wo_sb, kc_sb, vp_sb, t_sb, f1, fusedT, sT_st,
                             gq_hm)
            em.barrier()
        with contextlib.ExitStack() as pc:
            out_sb = pc.enter_context(
                nc.sbuf_tensor(f"out_sb_{b}", [128, NSB, D], F32))
            xh2d = [pc.enter_context(
                nc.sbuf_tensor(f"xbh{i}_{b}", [128, NCH, TGT], F16))
                for i in range(2)]
            xl2d = [pc.enter_context(
                nc.sbuf_tensor(f"xbl{i}_{b}", [128, NCH, TGT], F16))
                for i in range(2)]
            wg_sb = [pc.enter_context(
                nc.sbuf_tensor(f"wg{i}_{b}", [128, NCH, 128], F16))
                for i in range(2)]
            wu_sb = [pc.enter_context(
                nc.sbuf_tensor(f"wu{i}_{b}", [128, NCH, 128], F16))
                for i in range(2)]
            wd_sb = [pc.enter_context(
                nc.sbuf_tensor(f"wd{i}_{b}", [128, NHC, 128], F16))
                for i in range(2)]
            hf = pc.enter_context(nc.sbuf_tensor(f"hf_{b}", [128, TGT], F32))
            sg_sb = pc.enter_context(nc.sbuf_tensor(f"sg_sb_{b}", [128, TGT], F32))
            hh = pc.enter_context(nc.sbuf_tensor(f"hh_{b}", [128, NHC, TGT], F16))
            hl = pc.enter_context(nc.sbuf_tensor(f"hl_{b}", [128, NHC, TGT], F16))
            g_sb = pc.enter_context(nc.sbuf_tensor(f"g_sb_{b}", [128, NCH, TGT],
                                                   F32))
            bufx = pc.enter_context(nc.sbuf_tensor(f"bufx_{b}", [128, NSB, D],
                                                   F32))
            s_sb = pc.enter_context(nc.sbuf_tensor(f"s_sb_{b}", [128, NCH, TGT],
                                                   F32))
            if phases[2]:
                phase_b2(b, xh2d, xl2d, wg_sb, wu_sb, wd_sb, hf, sg_sb, hh,
                         hl, g_sb, bufx, s_sb, hf, out_sb)
        em.barrier()

    em.finalize(nc, stack)
    return nc


_PROG_CACHE = {}


def _get_program(S, T, HID):
    key = (S, T, HID)
    if key not in _PROG_CACHE:
        _PROG_CACHE[key] = build_program(S, T, HID)
    return _PROG_CACHE[key]


def kernel(L0_attn_x, L0_attn_w_q, L0_attn_w_k, L0_attn_w_v, L0_attn_g_q,
           L0_attn_g_k, L0_attn_w_o, L0_attn_b_o, L0_attn_gamma, L0_attn_beta,
           L0_ffn_w_gate, L0_ffn_w_up, L0_ffn_w_down):
    x = np.ascontiguousarray(np.asarray(L0_attn_x, dtype=np.float32))
    B, T, D_ = x.shape
    assert D_ == D
    n_cores = 8
    S = B // n_cores
    HID = L0_ffn_w_gate.shape[0]
    cn = _mk_consts(np.asarray(L0_attn_b_o, np.float32),
                    np.asarray(L0_attn_gamma, np.float32),
                    np.asarray(L0_attn_beta, np.float32),
                    np.asarray(L0_attn_g_q, np.float32),
                    np.asarray(L0_attn_g_k, np.float32))
    wgT = np.ascontiguousarray(np.asarray(L0_ffn_w_gate, np.int32).T)
    wuT = np.ascontiguousarray(np.asarray(L0_ffn_w_up, np.int32).T)
    wdT = np.ascontiguousarray(np.asarray(L0_ffn_w_down, np.int32).T)
    base = {
        "wq": np.ascontiguousarray(np.asarray(L0_attn_w_q, np.float32)),
        "wk": np.ascontiguousarray(np.asarray(L0_attn_w_k, np.float32)),
        "wv": np.ascontiguousarray(np.asarray(L0_attn_w_v, np.float32)),
        "wo": np.ascontiguousarray(np.asarray(L0_attn_w_o, np.float32)),
        "wgT": wgT, "wuT": wuT, "wdT": wdT, "cn": cn,
    }
    in_maps = [dict(base, x=np.ascontiguousarray(x[S * i:S * (i + 1)]))
               for i in range(n_cores)]
    nc = _get_program(S, T, HID)
    res = run_bass_kernel_spmd(nc, in_maps, core_ids=list(range(n_cores)))
    return np.concatenate([r["out"] for r in res.results], axis=0)


# version 23
# speedup vs baseline: 1.0616x; 1.0508x over previous
import contextlib
import numpy as np
import concourse.bass as bass
import concourse.mybir as mybir
from concourse.bass_utils import run_bass_kernel_spmd

F32 = mybir.dt.float32
BF16 = mybir.dt.bfloat16
F16 = mybir.dt.float16
I32 = mybir.dt.int32
AF = mybir.ActivationFunctionType
ALU = mybir.AluOpType
AX = mybir.AxisListType

ENGS = ("sync", "scalar", "vector", "tensor", "gpsimd")
IN_ORDER = {"tensor"}

D = 1024
H = 16
DH = 64
NCH = 8
TGT = 512
NSB = TGT // 128
S127 = 1.0 / 127.0


NPOOL = 12
DMA_QUEUES = ("sync", "gpsimd")
OPCAP = None


class _Op:
    __slots__ = ("eng", "fn", "dma", "idx", "need_inc", "deps", "incs", "post",
                 "waits", "tag", "semk", "slot", "defer", "gkey")


class Em:

    def __init__(self):
        self.ops = []
        self.last_w = {}
        self.last_r = {}
        self.pending_barrier = None
        self.barrier_seen = set()
        self.dma_count = {q: 0 for q in DMA_QUEUES}
        self.slot_last = {q: {} for q in DMA_QUEUES}
        self.open_groups = {}
        self.in_open = set()
        self.last_closed_tensor = None

    def _dep(self, d, gkey=None):
        if d.idx in self.in_open:
            if gkey is not None and d.gkey == gkey:
                return None
            sub = self.last_closed_tensor
            assert sub is not None and sub.idx > d.idx, (
                "open-group dep with no later observable PE op")
            return sub.idx
        return d.idx

    def op(self, eng, fn, reads=(), writes=(), dma=False, tag="", group=None,
           gclose=False):
        if OPCAP is not None and len(self.ops) >= OPCAP:
            d = _Op(); d.idx = -1; d.defer = None; d.gkey = None
            return d
        o = _Op()
        o.eng = eng
        o.fn = fn
        o.dma = dma
        o.tag = tag
        o.idx = len(self.ops)
        o.need_inc = dma
        o.slot = None
        o.defer = None
        o.gkey = group
        if eng == "tensor":
            if group is not None:
                chain = self.open_groups.setdefault(group, [])
                chain.append(o)
                if gclose:
                    for m in chain[:-1]:
                        m.defer = o.idx
                        self.in_open.discard(m.idx)
                    del self.open_groups[group]
                    self.last_closed_tensor = o
                else:
                    self.in_open.add(o.idx)
            else:
                self.last_closed_tensor = o
        deps = set()
        if self.pending_barrier is not None and eng not in self.barrier_seen:
            deps.update(self.pending_barrier)
            self.barrier_seen.add(eng)
        for r in reads:
            w = self.last_w.get(r)
            if w is not None:
                deps.add(self._dep(w, group))
        for r in writes:
            w = self.last_w.get(r)
            if w is not None:
                deps.add(self._dep(w, group))
            for rd in self.last_r.get(r, {}).values():
                deps.add(self._dep(rd, group))
        if dma:
            n = self.dma_count[eng]
            self.dma_count[eng] = n + 1
            o.slot = n % NPOOL
            prev = self.slot_last[eng].get(o.slot)
            if prev is not None:
                deps.add(prev.idx)
            self.slot_last[eng][o.slot] = o
        deps.discard(None)
        deps.discard(o.idx)
        o.deps = sorted(d for d in deps if d < o.idx)
        self.ops.append(o)
        for r in writes:
            self.last_w[r] = o
            self.last_r[r] = {}
        for r in reads:
            self.last_r.setdefault(r, {})[eng] = o
        return o

    def barrier(self):
        targets = []
        last_by_eng = {}
        for o in self.ops:
            last_by_eng[o.eng] = o
        for e, lo in last_by_eng.items():
            if e in DMA_QUEUES:
                targets.extend(p.idx for p in self.slot_last[e].values())
            else:
                targets.append(lo.idx)
        self.pending_barrier = sorted(set(targets))
        self.barrier_seen = set()

    def finalize(self, nc, stack):
        assert not self.open_groups, f"unclosed groups: {list(self.open_groups)}"

        def fin(idx):
            while self.ops[idx].defer is not None:
                idx = self.ops[idx].defer
            return idx

        for o in self.ops:
            o.deps = sorted({fin(d) for d in o.deps})
        for o in self.ops:
            for di in o.deps:
                d = self.ops[di]
                if d.eng == o.eng and o.eng in IN_ORDER:
                    continue
                d.need_inc = True
        semk_all = [("eng", e) for e in ENGS]
        for q in DMA_QUEUES:
            semk_all += [("slot", q, i) for i in range(NPOOL)]
        sems = {k: stack.enter_context(
            nc.semaphore(("sem_%s" % k[1]) if k[0] == "eng"
                         else "sem_%s_s%d" % (k[1], k[2])))
            for k in semk_all}
        cnt = {k: 0 for k in semk_all}
        for o in self.ops:
            o.semk = ("slot", o.eng, o.slot) if o.dma else ("eng", o.eng)
            o.incs = (16 if o.dma else 1) if o.need_inc else 0
            cnt[o.semk] += o.incs
            o.post = cnt[o.semk]
        per_eng = {e: [] for e in ENGS}
        for o in self.ops:
            per_eng[o.eng].append(o)
        for e in ENGS:
            hi = {}
            for o in per_eng[e]:
                need = {}
                for di in o.deps:
                    d = self.ops[di]
                    if d.eng == e and e in IN_ORDER:
                        continue
                    assert d.need_inc and d.post > 0
                    if d.post > hi.get(d.semk, 0):
                        need[d.semk] = max(need.get(d.semk, 0), d.post)
                o.waits = [(sems[k], v) for k, v in need.items()]
                if e != "tensor":
                    for k, v in need.items():
                        hi[k] = v
        block = stack.enter_context(nc.Block())

        def mk(e, items):
            def body(eng):
                for o in items:
                    for (sem, val) in o.waits:
                        eng.wait_ge(sem, val)
                    inst = o.fn(eng)
                    if o.incs:
                        if inst is None:
                            inst = eng.nop(nofuse=True)
                        inst.then_inc(sems[o.semk], o.incs)
                for k in semk_all:
                    if (k[1] == e) and cnt[k] > 0:
                        eng.wait_ge(sems[k], cnt[k])
            return body

        for e in ENGS:
            if per_eng[e]:
                getattr(block, e)(mk(e, per_eng[e]))
        return sems, cnt


C_IDENT = 0
C_GQBLK = 128
C_GKBLK = 130
C_BLK2 = 132
C_ONES = 134
C_TANHB = 135
C_EPSLN = 136
C_TANHB2 = 137
C_BO = 138
C_GAMMA = 146
C_BETA = 154
C_SEL1 = 162
C_SEL05 = 290
CC = 512


def _mk_consts(b_o, gamma, beta, g_q, g_k):
    cn = np.zeros((128, CC), np.float32)
    cn[:, C_IDENT:C_IDENT + 128] = np.eye(128, dtype=np.float32)
    cn[0:64, C_GQBLK] = g_q
    cn[64:128, C_GQBLK + 1] = g_q
    cn[0:64, C_GKBLK] = g_k
    cn[64:128, C_GKBLK + 1] = g_k
    cn[0:64, C_BLK2] = 1.0
    cn[64:128, C_BLK2 + 1] = 1.0
    cn[:, C_ONES] = 1.0
    cn[:, C_TANHB] = -4.05
    b_exact = -(15.0 * 0.27)
    cn[:, C_TANHB2] = np.float32(b_exact - np.float64(np.float32(-4.05)))
    cn[:, C_EPSLN] = 1e-5
    cn[:, C_BO:C_BO + 8] = b_o.reshape(8, 128).T
    cn[:, C_GAMMA:C_GAMMA + 8] = gamma.reshape(8, 128).T
    cn[:, C_BETA:C_BETA + 8] = beta.reshape(8, 128).T
    cn[0, C_SEL1:C_SEL1 + 64] = 1.0
    cn[1, C_SEL1 + 64:C_SEL1 + 128] = 1.0
    cn[0, C_SEL05:C_SEL05 + 64] = 0.5
    cn[1, C_SEL05 + 64:C_SEL05 + 128] = 0.5
    return cn


def build_program(S, T, HID, phases=(1, 1, 1)):
    NTG = T // TGT
    NKT = T // 128
    NHC = HID // 128
    nc = bass.Bass("TRN2", target_bir_lowering=False)
    stack = contextlib.ExitStack()
    nc._keepalive_stack = stack

    x_hbm = nc.dram_tensor("x", [S, T, D], F32, kind="ExternalInput").ap()
    wq_h = nc.dram_tensor("wq", [D, D], F32, kind="ExternalInput").ap()
    wk_h = nc.dram_tensor("wk", [D, D], F32, kind="ExternalInput").ap()
    wv_h = nc.dram_tensor("wv", [D, D], F32, kind="ExternalInput").ap()
    wo_h = nc.dram_tensor("wo", [D, D], F32, kind="ExternalInput").ap()
    wgT_h = nc.dram_tensor("wgT", [D, HID], I32, kind="ExternalInput").ap()
    wuT_h = nc.dram_tensor("wuT", [D, HID], I32, kind="ExternalInput").ap()
    wdT_h = nc.dram_tensor("wdT", [HID, D], I32, kind="ExternalInput").ap()
    cn_h = nc.dram_tensor("cn", [128, CC], F32, kind="ExternalInput").ap()
    out_h = nc.dram_tensor("out", [S, T, D], F32, kind="ExternalOutput").ap()
    sc_xnormT = nc.dram_tensor("sc_xnormT", [S, D, T], F32).ap()
    sc_xnT = nc.dram_tensor("sc_xnT", [S, D, T], F32).ap()
    sc_vp = nc.dram_tensor("sc_vp", [S, T, D], F32).ap()
    sc_k = nc.dram_tensor("sc_k", [S, D, T], F32).ap()
    sc_s = nc.dram_tensor("sc_s", [S, D, T], F32).ap()
    sc_wg = nc.dram_tensor("sc_wg", [NHC, 128, NCH * 128], F16).ap()
    sc_wu = nc.dram_tensor("sc_wu", [NHC, 128, NCH * 128], F16).ap()
    sc_wd = nc.dram_tensor("sc_wd", [NCH, 128, NHC * 128], F16).ap()

    em = Em()

    cn = stack.enter_context(nc.sbuf_tensor("cn_sb", [128, CC], F32))
    c_all = stack.enter_context(nc.sbuf_tensor("c_all", [128, 8], F32))
    ps = stack.enter_context(nc.psum_tensor("ps", [128, 4096], F32))

    def PS(b, o=0, w=512):
        return ps[:, 512 * b + o: 512 * b + o + w]

    ident = cn[:, C_IDENT:C_IDENT + 128]
    gqblk = cn[:, C_GQBLK:C_GQBLK + 2]
    gkblk = cn[:, C_GKBLK:C_GKBLK + 2]
    ones = cn[:, C_ONES:C_ONES + 1]
    tanhb = cn[:, C_TANHB:C_TANHB + 1]
    epsln = cn[:, C_EPSLN:C_EPSLN + 1]
    sel1 = cn[0:2, C_SEL1:C_SEL1 + 128]
    sel05 = cn[0:2, C_SEL05:C_SEL05 + 128]

    em.op("sync", lambda e: e.dma_start(out=cn[:], in_=cn_h[:, :]),
          writes=["cn"], dma=True, tag="cn_load")
    em.op("vector", lambda e: e.tensor_add(
        out=tanhb2[:], in0=cn[:, C_TANHB:C_TANHB + 1],
        in1=cn[:, C_TANHB2:C_TANHB2 + 1]),
        reads=["cn"], writes=["tanhb2"])

    nmu = stack.enter_context(nc.sbuf_tensor("nmu", [128, NSB], F32))
    var = stack.enter_context(nc.sbuf_tensor("var", [128, NSB], F32))
    rs = stack.enter_context(nc.sbuf_tensor("rs", [128, NSB], F32))
    sstile = stack.enter_context(nc.sbuf_tensor("sstile", [128, 64], F32))
    gdot = stack.enter_context(nc.sbuf_tensor("gdot", [128, 64], F32))
    gk_tm = stack.enter_context(nc.sbuf_tensor("gk_tm", [128, 64], F32))
    gq_tm = stack.enter_context(nc.sbuf_tensor("gq_tm", [128, 64], F32))
    n2t = stack.enter_context(nc.sbuf_tensor("n2t", [2, TGT], F32))
    tanhb2 = stack.enter_context(nc.sbuf_tensor("tanhb2", [128, 1], F32))

    def layer_norm(xin, xout, sqbuf, key_in, key_out, key_sq, affine=False):
        em.op("vector", lambda e: e.reduce_sum(out=nmu[:, 0:NSB], in_=xin[:],
                                               axis=AX.X, negate=True),
              reads=[key_in], writes=["nmu"])
        em.op("vector", lambda e: e.tensor_scalar_mul(out=nmu[:, 0:NSB],
                                                      in0=nmu[:, 0:NSB],
                                                      scalar1=1.0 / D),
              reads=["nmu"], writes=["nmu"])
        for s in range(NSB):
            em.op("scalar", lambda e, s=s: e.activation(
                out=xout[:, s, :], in_=xin[:, s, :], func=AF.Identity,
                bias=nmu[:, s:s + 1], scale=1.0),
                reads=[key_in, "nmu"], writes=[key_out])
        em.op("vector", lambda e: e.tensor_mul(out=sqbuf[:], in0=xout[:],
                                               in1=xout[:]),
              reads=[key_out], writes=[key_sq])
        em.op("vector", lambda e: e.reduce_sum(out=var[:, 0:NSB], in_=sqbuf[:],
                                               axis=AX.X),
              reads=[key_sq], writes=["var"])
        em.op("vector", lambda e: e.tensor_scalar_mul(out=var[:, 0:NSB],
                                                      in0=var[:, 0:NSB],
                                                      scalar1=1.0 / D),
              reads=["var"], writes=["var"])
        em.op("scalar", lambda e: e.activation(out=rs[:, 0:NSB],
                                               in_=var[:, 0:NSB],
                                               func=AF.Sqrt,
                                               bias=epsln, scale=1.0),
              reads=["var", "cn"], writes=["rs"])
        em.op("vector", lambda e: e.reciprocal(out=rs[:, 0:NSB],
                                               in_=rs[:, 0:NSB]),
              reads=["rs"], writes=["rs"])
        for s in range(NSB):
            em.op("scalar", lambda e, s=s: e.activation(
                out=xout[:, s, :], in_=xout[:, s, :], func=AF.Identity,
                bias=0.0, scale=rs[:, s:s + 1]),
                reads=[key_out, "rs"], writes=[key_out])

    def layer_norm2(xin, xout, key_in, key_out):
        em.op("vector", lambda e: e.reduce_sum(out=nmu[:, 0:NSB], in_=xin[:],
                                               axis=AX.X, negate=True),
              reads=[key_in], writes=["nmu"])
        em.op("vector", lambda e: e.tensor_scalar_mul(out=nmu[:, 0:NSB],
                                                      in0=nmu[:, 0:NSB],
                                                      scalar1=1.0 / D),
              reads=["nmu"], writes=["nmu"])
        em.op("vector", lambda e: e.tensor_mul(out=xout[:], in0=xin[:],
                                               in1=xin[:]),
              reads=[key_in], writes=[key_out])
        em.op("vector", lambda e: e.reduce_sum(out=var[:, 0:NSB], in_=xout[:],
                                               axis=AX.X),
              reads=[key_out], writes=["var"])
        em.op("vector", lambda e: e.tensor_mul(out=rs[:, 0:NSB],
                                               in0=nmu[:, 0:NSB],
                                               in1=nmu[:, 0:NSB]),
              reads=["nmu"], writes=["rs"])
        em.op("vector", lambda e: e.tensor_scalar_mul(out=var[:, 0:NSB],
                                                      in0=var[:, 0:NSB],
                                                      scalar1=1.0 / D),
              reads=["var"], writes=["var"])
        em.op("vector", lambda e: e.tensor_sub(out=var[:, 0:NSB],
                                               in0=var[:, 0:NSB],
                                               in1=rs[:, 0:NSB]),
              reads=["var", "rs"], writes=["var"])
        em.op("scalar", lambda e: e.activation(out=rs[:, 0:NSB],
                                               in_=var[:, 0:NSB],
                                               func=AF.Sqrt,
                                               bias=epsln, scale=1.0),
              reads=["var", "cn"], writes=["rs"])
        em.op("vector", lambda e: e.reciprocal(out=rs[:, 0:NSB],
                                               in_=rs[:, 0:NSB]),
              reads=["rs"], writes=["rs"])
        for s in range(NSB):
            em.op("scalar", lambda e, s=s: e.activation(
                out=xout[:, s, :], in_=xin[:, s, :], func=AF.Identity,
                bias=nmu[:, s:s + 1], scale=1.0),
                reads=[key_in, "nmu"], writes=[key_out])
        for s in range(NSB):
            em.op("scalar", lambda e, s=s: e.activation(
                out=xout[:, s, :], in_=xout[:, s, :], func=AF.Identity,
                bias=0.0, scale=rs[:, s:s + 1]),
                reads=[key_out, "rs"], writes=[key_out])

    def phase_a(b, bufAd, bufBd, xnT, xnormT, wk_sb, wv_sb, kT_tg,
                sq_sb, v_tg, rn2):
        em.op("vector", lambda e: e.memset(c_all[:], 0.0), writes=["c_all"])
        em.op("sync", lambda e: e.dma_start(
            out=wk_sb[:], in_=wk_h[:, :].rearrange("(c p) d -> p c d", p=128)),
              writes=["wk_sb"], dma=True, tag="wk")
        em.op("sync", lambda e: e.dma_start(
            out=wv_sb[:], in_=wv_h[:, :].rearrange("(c p) d -> p c d", p=128)),
              writes=["wv_sb"], dma=True, tag="wv")

        def pre_load(g):
            gp = g % 2
            bufA = bufAd[gp]
            kA = f"bufA{gp}"
            t0 = TGT * g
            srcx = x_hbm[b, t0:t0 + TGT, :].rearrange("(s p) d -> p s d",
                                                      p=128)
            em.op("sync", lambda e, srcx=srcx, bufA=bufA: e.dma_start(
                out=bufA[:], in_=srcx),
                  writes=[kA], dma=True, tag="xload")

        def pre_ln(g):
            gp = g % 2
            bufA, bufB = bufAd[gp], bufBd[gp]
            kA, kB = f"bufA{gp}", f"bufB{gp}"
            layer_norm(bufA, bufB, bufA, kA, kB, kA)
            layer_norm2(bufB, bufA, kB, kA)

        def main1(g):
            gp = g % 2
            bufA, bufB = bufAd[gp], bufBd[gp]
            kA, kB = f"bufA{gp}", f"bufB{gp}"
            t0 = TGT * g
            for c in range(NCH):
                tb = 3 * (c % 2)
                for s in range(NSB):
                    em.op("tensor", lambda e, c=c, s=s, bufB=bufB, tb=tb:
                          e.transpose(
                        PS(tb, 128 * s, 128), bufB[:, s, 128 * c:128 * c + 128],
                        ident),
                        reads=[kB, "cn"], writes=[f"ps{tb}"])
                em.op("vector", lambda e, c=c, tb=tb: e.tensor_copy(
                    out=xnT[:, c, :], in_=PS(tb)),
                    reads=[f"ps{tb}"], writes=["xnT"])
            dst = sc_xnT[b, :, t0:t0 + TGT].rearrange("(c p) t -> p c t",
                                                      p=128)
            em.op("gpsimd", lambda e, dst=dst: e.dma_start(out=dst,
                                                           in_=xnT[:]),
                  reads=["xnT"], dma=True)
            for c in range(NCH):
                tb = 3 * (c % 2)
                for s in range(NSB):
                    em.op("tensor", lambda e, c=c, s=s, bufA=bufA, tb=tb:
                          e.transpose(
                        PS(tb, 128 * s, 128), bufA[:, s, 128 * c:128 * c + 128],
                        ident),
                        reads=[kA, "cn"], writes=[f"ps{tb}"])
                em.op("scalar", lambda e, c=c, tb=tb: e.activation(
                    out=xnormT[:, c, :], in_=PS(tb), func=AF.Identity,
                    scale=cn[:, C_GAMMA + c:C_GAMMA + c + 1],
                    bias=cn[:, C_BETA + c:C_BETA + c + 1]),
                    reads=[f"ps{tb}", "cn"], writes=["xnormT"])
            dst = sc_xnormT[b, :, t0:t0 + TGT].rearrange("(c p) t -> p c t",
                                                         p=128)
            em.op("gpsimd", lambda e, dst=dst: e.dma_start(out=dst,
                                                           in_=xnormT[:]),
                  reads=["xnormT"], dma=True)
            def chainA(j):
                em.op("scalar", lambda e, j=j: e.activation(
                    out=sq_sb[:], in_=kT_tg[:, j, :], func=AF.Square),
                    reads=["kT_tg"], writes=["sq_sb"])
                em.op("tensor", lambda e, j=j: e.matmul(
                    ps[0:2, 512 * 3:512 * 3 + TGT], cn[:, C_BLK2:C_BLK2 + 2],
                    sq_sb[:], start=True, stop=True),
                    reads=["sq_sb", "cn"], writes=["ps3"])
                em.op("scalar", lambda e, j=j: e.activation(
                    out=n2t[:, :], in_=ps[0:2, 512 * 3:512 * 3 + TGT],
                    func=AF.Sqrt),
                    reads=["ps3"], writes=["n2t"])

            def chainB(j):
                em.op("vector", lambda e: e.tensor_single_scalar(
                    out=n2t[:, :], in_=n2t[:, :], scalar=1e-12, op=ALU.max),
                    reads=["n2t"], writes=["n2t"])
                em.op("vector", lambda e, j=j: e.reciprocal(
                    out=rn2[:, :], in_=n2t[:, :]),
                    reads=["n2t"], writes=["rn2"])
                em.op("tensor", lambda e, j=j: e.matmul(
                    PS(4), sel1, rn2[:, :], start=True, stop=True),
                    reads=["rn2", "cn"], writes=["ps4"])
                em.op("vector", lambda e, j=j: e.tensor_mul(
                    out=kT_tg[:, j, :], in0=kT_tg[:, j, :],
                    in1=PS(4)),
                    reads=["ps4", "kT_tg"], writes=["kT_tg"])
                for s in range(NSB):
                    em.op("tensor", lambda e, j=j, s=s: e.matmul(
                        ps[:, 512 * 5 + 16 * s + 2 * j:
                           512 * 5 + 16 * s + 2 * j + 2],
                        kT_tg[:, j, 128 * s:128 * s + 128], gkblk,
                        start=True, stop=True),
                        reads=["kT_tg", "cn"], writes=["ps5"])

            for j in range(NCH):
                pb = 1 + (j % 2)
                for i in range(NCH):
                    em.op("tensor", lambda e, i=i, j=j, pb=pb: e.matmul(
                        PS(pb), wk_sb[:, i, 128 * j:128 * j + 128],
                        xnormT[:, i, :], start=(i == 0), stop=(i == NCH - 1)),
                        reads=["wk_sb", "xnormT"], writes=[f"ps{pb}"],
                        group=f"kp{g}_{j}", gclose=(i == NCH - 1))
                if j >= 2:
                    chainB(j - 2)
                if j >= 1:
                    chainA(j - 1)
                em.op("vector", lambda e, j=j, pb=pb: e.tensor_copy(
                    out=kT_tg[:, j, :], in_=PS(pb)),
                    reads=[f"ps{pb}"], writes=["kT_tg"])
            chainB(NCH - 2)
            chainA(NCH - 1)
            chainB(NCH - 1)
            dst = sc_k[b, :, t0:t0 + TGT].rearrange("(c p) t -> p c t",
                                                    p=128)
            em.op("gpsimd", lambda e, dst=dst: e.dma_start(out=dst,
                                                           in_=kT_tg[:]),
                  reads=["kT_tg"], dma=True, tag="kstore")
            em.op("scalar", lambda e: e.activation(
                out=gdot[:, :], in_=ps[:, 512 * 5:512 * 5 + 64],
                func=AF.Sigmoid),
                reads=["ps5"], writes=["gdot"])
            em.op("vector", lambda e: e.tensor_scalar(
                out=gk_tm[:, :], in0=gdot[:, :], scalar1=0.75, scalar2=0.25,
                op0=ALU.mult, op1=ALU.add),
                reads=["gdot"], writes=["gk_tm"])

        def main2(g):
            gp = g % 2
            t0 = TGT * g
            VB = (1, 2, 6, 7)
            for s in range(NSB):
                for jh in range(2):
                    pb = VB[(2 * s + jh) % 4]
                    for i in range(NCH):
                        em.op("tensor", lambda e, i=i, s=s, jh=jh, pb=pb:
                              e.matmul(PS(pb),
                                       xnT[:, i, 128 * s:128 * s + 128],
                                       wv_sb[:, i, 512 * jh:512 * jh + 512],
                                       start=(i == 0), stop=(i == NCH - 1)),
                              reads=["wv_sb", "xnT"], writes=[f"ps{pb}"],
                              group=f"vp{g}_{s}_{jh}", gclose=(i == NCH - 1))
                    for hh in range(8):
                        h = 8 * jh + hh
                        em.op("vector", lambda e, s=s, jh=jh, hh=hh, h=h,
                              pb=pb: e.tensor_scalar_mul(
                                  out=v_tg[:, s, 512 * jh + 64 * hh:
                                           512 * jh + 64 * hh + 64],
                                  in0=PS(pb, 64 * hh, 64),
                                  scalar1=gk_tm[:, 16 * s + h:16 * s + h + 1]),
                              reads=[f"ps{pb}", "gk_tm"], writes=["v_tg"])
            for s in range(NSB):
                for p in range(8):
                    em.op("tensor", lambda e, s=s, p=p: e.matmul(
                        ps[:, 512 * 4 + p:512 * 4 + p + 1],
                        v_tg[:, s, 128 * p:128 * p + 128], ones,
                        start=True, stop=True),
                        reads=["v_tg", "cn"], writes=["ps4"])
                em.op("vector", lambda e, s=s: e.tensor_add(
                    out=c_all[:], in0=c_all[:], in1=ps[:, 512 * 4:512 * 4 + 8]),
                    reads=["ps4", "c_all"], writes=["c_all"])
            dst = sc_vp[b, t0:t0 + TGT, :].rearrange("(s p) d -> p s d", p=128)
            em.op("gpsimd", lambda e, dst=dst: e.dma_start(out=dst,
                                                           in_=v_tg[:]),
                  reads=["v_tg"], dma=True)

        pre_load(0)
        pre_ln(0)
        for g in range(NTG):
            if g + 1 < NTG:
                pre_load(g + 1)
            main1(g)
            if g + 1 < NTG:
                pre_ln(g + 1)
            main2(g)

    def phase_b1(b, xnormT2, qstage, sqq, qnT, wq_sb, wo_sb, kc_sb, vp_sb,
                 t_sb, f1, fusedT, sT_st, gq_hm):
        em.op("sync", lambda e: e.dma_start(
            out=wq_sb[:], in_=wq_h[:, :].rearrange("(c p) d -> p c d", p=128)),
              writes=["wq_sb"], dma=True, tag="wq")
        em.op("sync", lambda e: e.dma_start(
            out=wo_sb[:], in_=wo_h[:, :].rearrange("(c p) d -> p c d", p=128)),
              writes=["wo_sb"], dma=True, tag="wo")
        src0 = sc_xnormT[b, :, 0:TGT].rearrange("(c p) t -> p c t", p=128)
        em.op("sync", lambda e, src=src0: e.dma_start(out=xnormT2[0][:],
                                                      in_=src),
              writes=["xn0"], dma=True)
        for qt in range(NTG):
            q0 = TGT * qt
            xp = qt % 2
            xnormT = xnormT2[xp]
            xkey = f"xn{xp}"
            if qt + 1 < NTG:
                srcn = sc_xnormT[b, :, q0 + TGT:q0 + 2 * TGT].rearrange(
                    "(c p) t -> p c t", p=128)
                em.op("sync", lambda e, src=srcn, xq=(qt + 1) % 2:
                      e.dma_start(out=xnormT2[xq][:], in_=src),
                      writes=[f"xn{(qt + 1) % 2}"], dma=True)
            for s in range(NSB):
                for jh in range(2):
                    pb = jh
                    for i in range(NCH):
                        em.op("tensor", lambda e, i=i, s=s, jh=jh, pb=pb,
                              xnormT=xnormT:
                              e.matmul(PS(pb),
                                       xnormT[:, i, 128 * s:128 * s + 128],
                                       wq_sb[:, i, 512 * jh:512 * jh + 512],
                                       start=(i == 0), stop=(i == NCH - 1)),
                              reads=["wq_sb", xkey], writes=[f"ps{pb}"],
                              group=f"qp{qt}_{s}_{jh}", gclose=(i == NCH - 1))
                    em.op("vector", lambda e, s=s, jh=jh, pb=pb: e.tensor_copy(
                        out=qstage[:, s, 512 * jh:512 * jh + 512], in_=PS(pb)),
                        reads=[f"ps{pb}"], writes=["qf"])
                    em.op("scalar", lambda e, s=s, jh=jh: e.activation(
                        out=sqq[:, 512 * jh:512 * jh + 512],
                        in_=qstage[:, s, 512 * jh:512 * jh + 512],
                        func=AF.Square),
                        reads=["qf"], writes=["sqq"])
                em.op("vector", lambda e, s=s: e.reduce_sum(
                    out=sstile[:, 16 * s:16 * s + 16],
                    in_=sqq[:].rearrange("p (h d) -> p h d", d=DH), axis=AX.X),
                    reads=["sqq"], writes=["sstile"])
            em.op("scalar", lambda e: e.activation(out=sstile[:, :],
                                                   in_=sstile[:, :],
                                                   func=AF.Sqrt),
                  reads=["sstile"], writes=["sstile"])
            em.op("vector", lambda e: e.tensor_single_scalar(
                out=sstile[:, :], in_=sstile[:, :], scalar=1e-12, op=ALU.max),
                reads=["sstile"], writes=["sstile"])
            em.op("vector", lambda e: e.reciprocal(out=sstile[:, :],
                                                   in_=sstile[:, :]),
                  reads=["sstile"], writes=["sstile"])
            for s in range(NSB):
                em.op("vector", lambda e, s=s: e.tensor_mul(
                    out=qstage[:, s, :].rearrange("p (h d) -> p h d", d=DH),
                    in0=qstage[:, s, :].rearrange("p (h d) -> p h d", d=DH),
                    in1=sstile[:, 16 * s:16 * s + 16].unsqueeze(2)
                        .broadcast_to([128, 16, DH])),
                    reads=["qf", "sstile"], writes=["qf"])
            for c in range(NCH):
                tb = 2 + (c % 2)
                for s in range(NSB):
                    em.op("tensor", lambda e, c=c, s=s, tb=tb: e.transpose(
                        PS(tb, 128 * s, 128),
                        qstage[:, s, 128 * c:128 * c + 128], ident),
                        reads=["qf", "cn"], writes=[f"ps{tb}"])
                if c % 2 == 0:
                    em.op("scalar", lambda e, c=c, tb=tb: e.activation(
                        out=qnT[:, c, :], in_=PS(tb), func=AF.Identity),
                        reads=[f"ps{tb}"], writes=["qnT"])
                else:
                    em.op("vector", lambda e, c=c, tb=tb: e.tensor_copy(
                        out=qnT[:, c, :], in_=PS(tb)),
                        reads=[f"ps{tb}"], writes=["qnT"])
            for s in range(NSB):
                for c in range(NCH):
                    em.op("tensor", lambda e, c=c, s=s: e.matmul(
                        ps[:, 512 * 3 + 16 * s + 2 * c:
                           512 * 3 + 16 * s + 2 * c + 2],
                        qnT[:, c, 128 * s:128 * s + 128], gqblk,
                        start=True, stop=True),
                        reads=["qnT", "cn"], writes=["ps3"])
            em.op("scalar", lambda e: e.activation(
                out=gdot[:, :], in_=ps[:, 512 * 3:512 * 3 + 64],
                func=AF.Sigmoid),
                reads=["ps3"], writes=["gdot"])
            em.op("vector", lambda e: e.tensor_scalar(
                out=gq_tm[:, :], in0=gdot[:, :], scalar1=0.75, scalar2=0.25,
                op0=ALU.mult, op1=ALU.add),
                reads=["gdot"], writes=["gq_tm"])
            for c in range(NCH):
                for s in range(NSB):
                    em.op("tensor", lambda e, c=c, s=s: e.transpose(
                        ps[0:2, 512 * 3 + 128 * s:512 * 3 + 128 * s + 128],
                        gq_tm[:, 16 * s + 2 * c:16 * s + 2 * c + 2], ident),
                        reads=["gq_tm", "cn"], writes=["ps3"])
                em.op("vector", lambda e, c=c: e.tensor_copy(
                    out=gq_hm[c][:, :], in_=ps[0:2, 512 * 3:512 * 3 + TGT]),
                    reads=["ps3"], writes=[f"gq_hm{c}"])
            for c in range(NCH):
                cp = c % 2
                src = sc_vp[b, :, 128 * c:128 * c + 128].rearrange(
                    "(kt p) d -> p kt d", p=128)
                em.op("sync", lambda e, src=src, cp=cp: e.dma_start(
                    out=vp_sb[cp][:], in_=src),
                    writes=[f"vp{cp}"], dma=True, tag="vp")
                em.op("sync", lambda e, c=c, cp=cp: e.dma_start(
                    out=kc_sb[cp][:], in_=sc_k[b, 128 * c:128 * c + 128, :]),
                    writes=[f"kc{cp}"], dma=True, tag="kc")

                def align(kt):
                    alb = 2 + 2 * (kt % 2)
                    em.op("tensor", lambda e, kt=kt, alb=alb, cp=cp, c=c:
                          e.matmul(
                        PS(alb), kc_sb[cp][0:64, 128 * kt:128 * kt + 128],
                        qnT[0:64, c, :], start=True, stop=True),
                        reads=[f"kc{cp}", "qnT"], writes=[f"ps{alb}"])
                    em.op("tensor", lambda e, kt=kt, alb=alb, cp=cp, c=c:
                          e.matmul(
                        PS(alb + 1), kc_sb[cp][64:128, 128 * kt:128 * kt + 128],
                        qnT[64:128, c, :], start=True, stop=True,
                        tile_position=(64, 0)),
                        reads=[f"kc{cp}", "qnT"], writes=[f"ps{alb + 1}"])

                align(0)
                for kt in range(NKT):
                    kp = kt % 2
                    alb = 2 + 2 * kp
                    em.op("scalar", lambda e, kp=kp, alb=alb: e.activation(
                        out=t_sb[kp][:], in_=ps[:, 512 * alb:512 * alb + 1024],
                        func=AF.Tanh, scale=15.0, bias=tanhb2[:, 0:1]),
                        reads=[f"ps{alb}", f"ps{alb + 1}", "tanhb2"],
                        writes=[f"t{kp}"])
                    if kt + 1 < NKT:
                        align(kt + 1)
                    em.op("tensor", lambda e, kt=kt, kp=kp, cp=cp: e.matmul(
                        ps[0:64, 512 * 6:512 * 6 + 512],
                        vp_sb[cp][:, kt, 0:64], t_sb[kp][:, 0:512],
                        start=(kt == 0), stop=(kt == NKT - 1)),
                        reads=[f"vp{cp}", f"t{kp}"], writes=["ps6"],
                        group=f"fuA{qt}_{c}", gclose=(kt == NKT - 1))
                    em.op("tensor", lambda e, kt=kt, kp=kp, cp=cp: e.matmul(
                        ps[64:128, 512 * 7:512 * 7 + 512],
                        vp_sb[cp][:, kt, 64:128], t_sb[kp][:, 512:1024],
                        start=(kt == 0), stop=(kt == NKT - 1),
                        tile_position=(0, 64)),
                        reads=[f"vp{cp}", f"t{kp}"], writes=["ps7"],
                        group=f"fuB{qt}_{c}", gclose=(kt == NKT - 1))
                em.op("tensor", lambda e, c=c: e.matmul(
                    PS(2), sel05, gq_hm[c][:, :], start=True, stop=True),
                    reads=[f"gq_hm{c}", "cn"], writes=["ps2"])
                em.op("vector", lambda e: e.tensor_copy(
                    out=f1[:], in_=PS(2)),
                    reads=["ps2"], writes=["f1"])
                em.op("vector", lambda e, c=c: e.scalar_tensor_tensor(
                    out=fusedT[0:64, c, :], in0=ps[0:64, 512 * 6:512 * 6 + 512],
                    scalar=c_all[0:64, c:c + 1], in1=f1[0:64, :],
                    op0=ALU.add, op1=ALU.mult),
                    reads=["ps6", "f1", "c_all"], writes=["qf"])
                em.op("vector", lambda e, c=c: e.scalar_tensor_tensor(
                    out=fusedT[64:128, c, :], in0=ps[64:128, 512 * 7:512 * 7 + 512],
                    scalar=c_all[64:128, c:c + 1], in1=f1[64:128, :],
                    op0=ALU.add, op1=ALU.mult),
                    reads=["ps7", "f1", "c_all"], writes=["qf"])
            for j in range(NCH):
                pb = j % 2
                for i in range(NCH):
                    em.op("tensor", lambda e, i=i, j=j, pb=pb: e.matmul(
                        PS(pb), wo_sb[:, i, 128 * j:128 * j + 128],
                        fusedT[:, i, :], start=(i == 0), stop=(i == NCH - 1)),
                        reads=["wo_sb", "qf"], writes=[f"ps{pb}"],
                        group=f"wo{qt}_{j}", gclose=(i == NCH - 1))
                sp = j % 2
                em.op("scalar", lambda e, j=j, pb=pb, sp=sp: e.activation(
                    out=sT_st[sp][:], in_=PS(pb), func=AF.Identity,
                    bias=cn[:, C_BO + j:C_BO + j + 1]),
                    reads=[f"ps{pb}", "cn"], writes=[f"sT{sp}"])
                dst = sc_s[b, 128 * j:128 * j + 128, q0:q0 + TGT]
                em.op("gpsimd", lambda e, dst=dst, sp=sp: e.dma_start(
                    out=dst, in_=sT_st[sp][:]),
                    reads=[f"sT{sp}"], dma=True)

    def phase_b2(b, xnT2d, xh, xl, wg_sb, wu_sb, wd_sb, hf, sg_sb, hh, hl,
                 g_sb, bufx, s_sb, prod, out_sb):
        NHC_l = NHC
        src0 = sc_xnT[b, :, 0:TGT].rearrange("(c p) t -> p c t", p=128)
        em.op("sync", lambda e, src=src0: e.dma_start(out=xnT2d[0][:],
                                                      in_=src),
              writes=["xt0"], dma=True)

        def split(qt):
            xp = qt % 2
            xnT2 = xnT2d[xp]
            xtkey = f"xt{xp}"
            for c in range(NCH):
                em.op("vector", lambda e, c=c, xnT2=xnT2: e.tensor_copy(
                    out=xh[:, c, :], in_=xnT2[:, c, :]),
                    reads=[xtkey], writes=["xh"])
                em.op("vector", lambda e, c=c, xnT2=xnT2: e.tensor_sub(
                    out=xl[:, c, :], in0=xnT2[:, c, :], in1=xh[:, c, :]),
                    reads=[xtkey, "xh"], writes=["xl"])

        split(0)
        for qt in range(NTG):
            q0 = TGT * qt
            if qt + 1 < NTG:
                srcn = sc_xnT[b, :, q0 + TGT:q0 + 2 * TGT].rearrange(
                    "(c p) t -> p c t", p=128)
                em.op("sync", lambda e, src=srcn, xq=(qt + 1) % 2:
                      e.dma_start(out=xnT2d[xq][:], in_=src),
                      writes=[f"xt{(qt + 1) % 2}"], dma=True)
            for hc in range(NHC_l):
                wp = hc % 2
                em.op("sync", lambda e, hc=hc, wp=wp: e.dma_start(
                    out=wg_sb[wp][:],
                    in_=sc_wg[hc, :, :].rearrange("p (c h) -> p c h", h=128)),
                    writes=[f"wg{wp}"], dma=True, tag="wg")
                em.op("sync", lambda e, hc=hc, wp=wp: e.dma_start(
                    out=wu_sb[wp][:],
                    in_=sc_wu[hc, :, :].rearrange("p (c h) -> p c h", h=128)),
                    writes=[f"wu{wp}"], dma=True, tag="wu")

                for i in range(NCH):
                    em.op("tensor", lambda e, i=i, wp=wp: e.matmul(
                        PS(0), wg_sb[wp][:, i, :], xh[:, i, :],
                        start=(i == 0), stop=False),
                        reads=[f"wg{wp}", "xh"], writes=["ps0"],
                        group=f"ga{qt}_{hc}")
                    em.op("tensor", lambda e, i=i, wp=wp: e.matmul(
                        PS(0), wg_sb[wp][:, i, :], xl[:, i, :],
                        start=False, stop=(i == NCH - 1)),
                        reads=[f"wg{wp}", "xl"], writes=["ps0"],
                        group=f"ga{qt}_{hc}", gclose=(i == NCH - 1))
                for i in range(NCH):
                    em.op("tensor", lambda e, i=i, wp=wp: e.matmul(
                        PS(1), wu_sb[wp][:, i, :], xh[:, i, :],
                        start=(i == 0), stop=False),
                        reads=[f"wu{wp}", "xh"], writes=["ps1"],
                        group=f"up{qt}_{hc}")
                    em.op("tensor", lambda e, i=i, wp=wp: e.matmul(
                        PS(1), wu_sb[wp][:, i, :], xl[:, i, :],
                        start=False, stop=(i == NCH - 1)),
                        reads=[f"wu{wp}", "xl"], writes=["ps1"],
                        group=f"up{qt}_{hc}", gclose=(i == NCH - 1))
                em.op("scalar", lambda e: e.activation(
                    out=sg_sb[:], in_=PS(0), func=AF.Silu, scale=S127),
                    reads=["ps0"], writes=["sg_sb"])
                em.op("vector", lambda e: e.scalar_tensor_tensor(
                    out=hf[:], in0=PS(1), scalar=S127, in1=sg_sb[:],
                    op0=ALU.mult, op1=ALU.mult),
                    reads=["ps1", "sg_sb"], writes=["hf"])
                em.op("vector", lambda e, hc=hc: e.tensor_copy(
                    out=hh[:, hc, :], in_=hf[:]),
                    reads=["hf"], writes=["hh"])
                em.op("vector", lambda e, hc=hc: e.tensor_sub(
                    out=hl[:, hc, :], in0=hf[:], in1=hh[:, hc, :]),
                    reads=["hf", "hh"], writes=["hl"])
            if qt + 1 < NTG:
                split(qt + 1)
            for j in range(NCH):
                pb = 2 + (j % 2)
                wp = j % 2
                em.op("sync", lambda e, j=j, wp=wp: e.dma_start(
                    out=wd_sb[wp][:],
                    in_=sc_wd[j, :, :].rearrange("p (c d) -> p c d", d=128)),
                    writes=[f"wd{wp}"], dma=True, tag="wd")
                for hc in range(NHC_l):
                    em.op("tensor", lambda e, hc=hc, j=j, pb=pb, wp=wp:
                          e.matmul(
                        PS(pb), wd_sb[wp][:, hc, :],
                        hh[:, hc, :], start=(hc == 0), stop=False),
                        reads=[f"wd{wp}", "hh"], writes=[f"ps{pb}"],
                        group=f"dn{qt}_{j}")
                    em.op("tensor", lambda e, hc=hc, j=j, pb=pb, wp=wp:
                          e.matmul(
                        PS(pb), wd_sb[wp][:, hc, :],
                        hl[:, hc, :], start=False, stop=(hc == NHC_l - 1)),
                        reads=[f"wd{wp}", "hl"], writes=[f"ps{pb}"],
                        group=f"dn{qt}_{j}", gclose=(hc == NHC_l - 1))
                em.op("scalar", lambda e, j=j, pb=pb: e.activation(
                    out=g_sb[:, j, :], in_=PS(pb), func=AF.Silu, scale=S127),
                    reads=[f"ps{pb}"], writes=["g_sb"])
            src = x_hbm[b, q0:q0 + TGT, :].rearrange("(s p) d -> p s d", p=128)
            em.op("sync", lambda e, src=src: e.dma_start(out=bufx[:], in_=src),
                  writes=["bufx"], dma=True)
            src = sc_s[b, :, q0:q0 + TGT].rearrange("(c p) t -> p c t", p=128)
            em.op("sync", lambda e, src=src: e.dma_start(out=s_sb[:], in_=src),
                  writes=["s_sb"], dma=True)
            for j in range(NCH):
                tb = 4 + (j % 2)
                em.op("vector", lambda e, j=j: e.tensor_mul(
                    out=prod[:], in0=g_sb[:, j, :], in1=s_sb[:, j, :]),
                    reads=["g_sb", "s_sb"], writes=["hf"])
                for s in range(NSB):
                    em.op("tensor", lambda e, j=j, s=s, tb=tb: e.transpose(
                        PS(tb, 128 * s, 128), prod[:, 128 * s:128 * s + 128],
                        ident),
                        reads=["hf", "cn"], writes=[f"ps{tb}"])
                for s in range(NSB):
                    em.op("vector", lambda e, j=j, s=s, tb=tb: e.tensor_add(
                        out=out_sb[:, s, 128 * j:128 * j + 128],
                        in0=PS(tb, 128 * s, 128),
                        in1=bufx[:, s, 128 * j:128 * j + 128]),
                        reads=[f"ps{tb}", "bufx"], writes=["out_sb"])
            dst = out_h[b, q0:q0 + TGT, :].rearrange("(s p) d -> p s d", p=128)
            em.op("gpsimd", lambda e, dst=dst: e.dma_start(out=dst,
                                                           in_=out_sb[:]),
                  reads=["out_sb"], dma=True, tag="out")

    with contextlib.ExitStack() as pp:
        wbuf = [pp.enter_context(nc.sbuf_tensor(f"wrep{i}", [128, HID], F16))
                for i in range(2)]
        for c in range(NCH):
            wp = c % 2
            for (wsrc, wdst, tag) in ((wgT_h, sc_wg, "rg"), (wuT_h, sc_wu, "ru")):
                em.op("gpsimd", lambda e, c=c, wp=wp, wsrc=wsrc: e.dma_start(
                    out=wbuf[wp][:],
                    in_=wsrc[128 * c:128 * c + 128, :]),
                    writes=[f"wrep{wp}"], dma=True, tag=tag)
                dst = wdst[:, :, 128 * c:128 * c + 128].rearrange(
                    "h p j -> p h j")
                em.op("gpsimd", lambda e, dst=dst, wp=wp: e.dma_start(
                    out=dst, in_=wbuf[wp][:].rearrange("p (h j) -> p h j",
                                                       j=128)),
                    reads=[f"wrep{wp}"], dma=True, tag=tag)
        wdbuf = [pp.enter_context(nc.sbuf_tensor(f"wdrep{i}", [128, D], F16))
                 for i in range(2)]
        for c in range(NHC):
            wp = c % 2
            em.op("gpsimd", lambda e, c=c, wp=wp: e.dma_start(
                out=wdbuf[wp][:],
                in_=wdT_h[128 * c:128 * c + 128, :]),
                writes=[f"wdrep{wp}"], dma=True, tag="rd")
            dst = sc_wd[:, :, 128 * c:128 * c + 128].rearrange(
                "j p d -> p j d")
            em.op("gpsimd", lambda e, dst=dst, wp=wp: e.dma_start(
                out=dst, in_=wdbuf[wp][:].rearrange("p (j d) -> p j d",
                                                    d=128)),
                reads=[f"wdrep{wp}"], dma=True, tag="rd")
    em.barrier()
    for b in range(S):
        if True:
            with contextlib.ExitStack() as pa:
                bufAd = [pa.enter_context(
                    nc.sbuf_tensor(f"bufA{i}_{b}", [128, NSB, D], F32))
                    for i in range(2)]
                bufBd = [pa.enter_context(
                    nc.sbuf_tensor(f"bufB{i}_{b}", [128, NSB, D], F32))
                    for i in range(2)]
                xnT = pa.enter_context(nc.sbuf_tensor(f"xnT_{b}", [128, NCH, TGT],
                                                      F32))
                xnormT = pa.enter_context(
                    nc.sbuf_tensor(f"xnormT_{b}", [128, NCH, TGT], F32))
                wk_sb = pa.enter_context(
                    nc.sbuf_tensor(f"wk_sb_{b}", [128, NCH, D], F32))
                wv_sb = pa.enter_context(
                    nc.sbuf_tensor(f"wv_sb_{b}", [128, NCH, D], F32))
                kT_tg = pa.enter_context(
                    nc.sbuf_tensor(f"kT_tg_{b}", [128, NCH, TGT], F32))
                sq_sb = pa.enter_context(nc.sbuf_tensor(f"sq_sb_{b}", [128, TGT],
                                                        F32))
                v_tg = pa.enter_context(nc.sbuf_tensor(f"v_tg_{b}", [128, NSB, D],
                                                       F32))
                rn2 = pa.enter_context(nc.sbuf_tensor(f"rn2_{b}", [2, TGT],
                                                      F32))
                if phases[0]:
                    phase_a(b, bufAd, bufBd, xnT, xnormT, wk_sb, wv_sb,
                            kT_tg, sq_sb, v_tg, rn2)
            em.barrier()
            with contextlib.ExitStack() as pb:
                xnormT2 = [pb.enter_context(
                    nc.sbuf_tensor(f"xn{i}_{b}", [128, NCH, TGT], F32))
                    for i in range(2)]
                qf = pb.enter_context(
                    nc.sbuf_tensor(f"qf_{b}", [128, NSB * D], F32))
                qstage = qf[:].rearrange("p (s c) -> p s c", s=NSB)
                sqq = pb.enter_context(nc.sbuf_tensor(f"sqq_{b}", [128, D], F32))
                qnT = pb.enter_context(nc.sbuf_tensor(f"qnT_{b}", [128, NCH, TGT],
                                                      F32))
                wq_sb = pb.enter_context(
                    nc.sbuf_tensor(f"wq_sb_{b}", [128, NCH, D], F32))
                wo_sb = pb.enter_context(
                    nc.sbuf_tensor(f"wo_sb_{b}", [128, NCH, D], F32))
                kc_sb = [pb.enter_context(
                    nc.sbuf_tensor(f"kc{i}_{b}", [128, T], F32))
                    for i in range(2)]
                vp_sb = [pb.enter_context(
                    nc.sbuf_tensor(f"vp{i}_{b}", [128, NKT, 128], F32))
                    for i in range(2)]
                t_sb = [pb.enter_context(
                    nc.sbuf_tensor(f"t{i}_{b}", [128, 1024], F32))
                    for i in range(2)]
                f1 = pb.enter_context(nc.sbuf_tensor(f"f1_{b}", [128, TGT], F32))
                fusedT = qf[:].rearrange("p (c t) -> p c t", c=NCH)
                sT_st = [pb.enter_context(
                    nc.sbuf_tensor(f"sT{i}_{b}", [128, TGT], F32))
                    for i in range(2)]
                gq_hm = [pb.enter_context(
                    nc.sbuf_tensor(f"gq_hm{c}_{b}", [2, TGT], F32))
                    for c in range(NCH)]
                if phases[1]:
                    phase_b1(b, xnormT2, qstage, sqq, qnT, wq_sb, wo_sb, kc_sb,
                             vp_sb, t_sb, f1, fusedT, sT_st, gq_hm)
            em.barrier()
        with contextlib.ExitStack() as pc:
            xnT2d = [pc.enter_context(
                nc.sbuf_tensor(f"xt{i}_{b}", [128, NCH, TGT], F32))
                for i in range(2)]
            out_sb = pc.enter_context(
                nc.sbuf_tensor(f"out_sb_{b}", [128, NSB, D], F32))
            xh = pc.enter_context(nc.sbuf_tensor(f"xh_b2_{b}", [128, NCH, TGT],
                                                 F16))
            xl = pc.enter_context(nc.sbuf_tensor(f"xl_b2_{b}", [128, NCH, TGT],
                                                 F16))
            wg_sb = [pc.enter_context(
                nc.sbuf_tensor(f"wg{i}_{b}", [128, NCH, 128], F16))
                for i in range(2)]
            wu_sb = [pc.enter_context(
                nc.sbuf_tensor(f"wu{i}_{b}", [128, NCH, 128], F16))
                for i in range(2)]
            wd_sb = [pc.enter_context(
                nc.sbuf_tensor(f"wd{i}_{b}", [128, NHC, 128], F16))
                for i in range(2)]
            hf = pc.enter_context(nc.sbuf_tensor(f"hf_{b}", [128, TGT], F32))
            sg_sb = pc.enter_context(nc.sbuf_tensor(f"sg_sb_{b}", [128, TGT], F32))
            hh = pc.enter_context(nc.sbuf_tensor(f"hh_{b}", [128, NHC, TGT], F16))
            hl = pc.enter_context(nc.sbuf_tensor(f"hl_{b}", [128, NHC, TGT], F16))
            g_sb = pc.enter_context(nc.sbuf_tensor(f"g_sb_{b}", [128, NCH, TGT],
                                                   F32))
            bufx = pc.enter_context(nc.sbuf_tensor(f"bufx_{b}", [128, NSB, D],
                                                   F32))
            s_sb = pc.enter_context(nc.sbuf_tensor(f"s_sb_{b}", [128, NCH, TGT],
                                                   F32))
            if phases[2]:
                phase_b2(b, xnT2d, xh, xl, wg_sb, wu_sb, wd_sb, hf, sg_sb, hh,
                         hl, g_sb, bufx, s_sb, hf, out_sb)
        em.barrier()

    em.finalize(nc, stack)
    return nc


_PROG_CACHE = {}


def _get_program(S, T, HID):
    key = (S, T, HID)
    if key not in _PROG_CACHE:
        _PROG_CACHE[key] = build_program(S, T, HID)
    return _PROG_CACHE[key]


def kernel(L0_attn_x, L0_attn_w_q, L0_attn_w_k, L0_attn_w_v, L0_attn_g_q,
           L0_attn_g_k, L0_attn_w_o, L0_attn_b_o, L0_attn_gamma, L0_attn_beta,
           L0_ffn_w_gate, L0_ffn_w_up, L0_ffn_w_down):
    x = np.ascontiguousarray(np.asarray(L0_attn_x, dtype=np.float32))
    B, T, D_ = x.shape
    assert D_ == D
    n_cores = 8
    S = B // n_cores
    HID = L0_ffn_w_gate.shape[0]
    cn = _mk_consts(np.asarray(L0_attn_b_o, np.float32),
                    np.asarray(L0_attn_gamma, np.float32),
                    np.asarray(L0_attn_beta, np.float32),
                    np.asarray(L0_attn_g_q, np.float32),
                    np.asarray(L0_attn_g_k, np.float32))
    wgT = np.ascontiguousarray(np.asarray(L0_ffn_w_gate, np.int32).T)
    wuT = np.ascontiguousarray(np.asarray(L0_ffn_w_up, np.int32).T)
    wdT = np.ascontiguousarray(np.asarray(L0_ffn_w_down, np.int32).T)
    base = {
        "wq": np.ascontiguousarray(np.asarray(L0_attn_w_q, np.float32)),
        "wk": np.ascontiguousarray(np.asarray(L0_attn_w_k, np.float32)),
        "wv": np.ascontiguousarray(np.asarray(L0_attn_w_v, np.float32)),
        "wo": np.ascontiguousarray(np.asarray(L0_attn_w_o, np.float32)),
        "wgT": wgT, "wuT": wuT, "wdT": wdT, "cn": cn,
    }
    in_maps = [dict(base, x=np.ascontiguousarray(x[S * i:S * (i + 1)]))
               for i in range(n_cores)]
    nc = _get_program(S, T, HID)
    res = run_bass_kernel_spmd(nc, in_maps, core_ids=list(range(n_cores)))
    return np.concatenate([r["out"] for r in res.results], axis=0)


# version 24
# speedup vs baseline: 1.0821x; 1.0194x over previous
import contextlib
import numpy as np
import concourse.bass as bass
import concourse.mybir as mybir
from concourse.bass_utils import run_bass_kernel_spmd

F32 = mybir.dt.float32
BF16 = mybir.dt.bfloat16
F16 = mybir.dt.float16
I32 = mybir.dt.int32
AF = mybir.ActivationFunctionType
ALU = mybir.AluOpType
AX = mybir.AxisListType

ENGS = ("sync", "scalar", "vector", "tensor", "gpsimd")
IN_ORDER = {"tensor"}

D = 1024
H = 16
DH = 64
NCH = 8
TGT = 512
NSB = TGT // 128
S127 = 1.0 / 127.0


NPOOL = 12
DMA_QUEUES = ("sync", "gpsimd")
OPCAP = None


class _Op:
    __slots__ = ("eng", "fn", "dma", "idx", "need_inc", "deps", "incs", "post",
                 "waits", "tag", "semk", "slot", "defer", "gkey")


class Em:

    def __init__(self):
        self.ops = []
        self.last_w = {}
        self.last_r = {}
        self.pending_barrier = None
        self.barrier_seen = set()
        self.dma_count = {q: 0 for q in DMA_QUEUES}
        self.slot_last = {q: {} for q in DMA_QUEUES}
        self.open_groups = {}
        self.in_open = set()
        self.last_closed_tensor = None

    def _dep(self, d, gkey=None):
        if d.idx in self.in_open:
            if gkey is not None and d.gkey == gkey:
                return None
            sub = self.last_closed_tensor
            assert sub is not None and sub.idx > d.idx, (
                "open-group dep with no later observable PE op")
            return sub.idx
        return d.idx

    def op(self, eng, fn, reads=(), writes=(), dma=False, tag="", group=None,
           gclose=False):
        if OPCAP is not None and len(self.ops) >= OPCAP:
            d = _Op(); d.idx = -1; d.defer = None; d.gkey = None
            return d
        o = _Op()
        o.eng = eng
        o.fn = fn
        o.dma = dma
        o.tag = tag
        o.idx = len(self.ops)
        o.need_inc = dma
        o.slot = None
        o.defer = None
        o.gkey = group
        if eng == "tensor":
            if group is not None:
                chain = self.open_groups.setdefault(group, [])
                chain.append(o)
                if gclose:
                    for m in chain[:-1]:
                        m.defer = o.idx
                        self.in_open.discard(m.idx)
                    del self.open_groups[group]
                    self.last_closed_tensor = o
                else:
                    self.in_open.add(o.idx)
            else:
                self.last_closed_tensor = o
        deps = set()
        if self.pending_barrier is not None and eng not in self.barrier_seen:
            deps.update(self.pending_barrier)
            self.barrier_seen.add(eng)
        for r in reads:
            w = self.last_w.get(r)
            if w is not None:
                deps.add(self._dep(w, group))
        for r in writes:
            w = self.last_w.get(r)
            if w is not None:
                deps.add(self._dep(w, group))
            for rd in self.last_r.get(r, {}).values():
                deps.add(self._dep(rd, group))
        if dma:
            n = self.dma_count[eng]
            self.dma_count[eng] = n + 1
            o.slot = n % NPOOL
            prev = self.slot_last[eng].get(o.slot)
            if prev is not None:
                deps.add(prev.idx)
            self.slot_last[eng][o.slot] = o
        deps.discard(None)
        deps.discard(o.idx)
        o.deps = sorted(d for d in deps if d < o.idx)
        self.ops.append(o)
        for r in writes:
            self.last_w[r] = o
            self.last_r[r] = {}
        for r in reads:
            self.last_r.setdefault(r, {})[eng] = o
        return o

    def barrier(self):
        targets = []
        last_by_eng = {}
        for o in self.ops:
            last_by_eng[o.eng] = o
        for e, lo in last_by_eng.items():
            if e in DMA_QUEUES:
                targets.extend(p.idx for p in self.slot_last[e].values())
            else:
                targets.append(lo.idx)
        self.pending_barrier = sorted(set(targets))
        self.barrier_seen = set()

    def finalize(self, nc, stack):
        assert not self.open_groups, f"unclosed groups: {list(self.open_groups)}"

        def fin(idx):
            while self.ops[idx].defer is not None:
                idx = self.ops[idx].defer
            return idx

        for o in self.ops:
            o.deps = sorted({fin(d) for d in o.deps})
        for o in self.ops:
            for di in o.deps:
                d = self.ops[di]
                if d.eng == o.eng and o.eng in IN_ORDER:
                    continue
                d.need_inc = True
        semk_all = [("eng", e) for e in ENGS]
        for q in DMA_QUEUES:
            semk_all += [("slot", q, i) for i in range(NPOOL)]
        sems = {k: stack.enter_context(
            nc.semaphore(("sem_%s" % k[1]) if k[0] == "eng"
                         else "sem_%s_s%d" % (k[1], k[2])))
            for k in semk_all}
        cnt = {k: 0 for k in semk_all}
        for o in self.ops:
            o.semk = ("slot", o.eng, o.slot) if o.dma else ("eng", o.eng)
            o.incs = (16 if o.dma else 1) if o.need_inc else 0
            cnt[o.semk] += o.incs
            o.post = cnt[o.semk]
        per_eng = {e: [] for e in ENGS}
        for o in self.ops:
            per_eng[o.eng].append(o)
        for e in ENGS:
            hi = {}
            for o in per_eng[e]:
                need = {}
                for di in o.deps:
                    d = self.ops[di]
                    if d.eng == e and e in IN_ORDER:
                        continue
                    assert d.need_inc and d.post > 0
                    if d.post > hi.get(d.semk, 0):
                        need[d.semk] = max(need.get(d.semk, 0), d.post)
                o.waits = [(sems[k], v) for k, v in need.items()]
                if e != "tensor":
                    for k, v in need.items():
                        hi[k] = v
        block = stack.enter_context(nc.Block())

        def mk(e, items):
            def body(eng):
                for o in items:
                    for (sem, val) in o.waits:
                        eng.wait_ge(sem, val)
                    inst = o.fn(eng)
                    if o.incs:
                        if inst is None:
                            inst = eng.nop(nofuse=True)
                        inst.then_inc(sems[o.semk], o.incs)
                for k in semk_all:
                    if (k[1] == e) and cnt[k] > 0:
                        eng.wait_ge(sems[k], cnt[k])
            return body

        for e in ENGS:
            if per_eng[e]:
                getattr(block, e)(mk(e, per_eng[e]))
        return sems, cnt


C_IDENT = 0
C_GQBLK = 128
C_GKBLK = 130
C_BLK2 = 132
C_ONES = 134
C_TANHB = 135
C_EPSLN = 136
C_TANHB2 = 137
C_BO = 138
C_GAMMA = 146
C_BETA = 154
C_SEL1 = 162
C_SEL05 = 290
CC = 512


def _mk_consts(b_o, gamma, beta, g_q, g_k):
    cn = np.zeros((128, CC), np.float32)
    cn[:, C_IDENT:C_IDENT + 128] = np.eye(128, dtype=np.float32)
    cn[0:64, C_GQBLK] = g_q
    cn[64:128, C_GQBLK + 1] = g_q
    cn[0:64, C_GKBLK] = g_k
    cn[64:128, C_GKBLK + 1] = g_k
    cn[0:64, C_BLK2] = 1.0
    cn[64:128, C_BLK2 + 1] = 1.0
    cn[:, C_ONES] = 1.0
    cn[:, C_TANHB] = -4.05
    b_exact = -(15.0 * 0.27)
    cn[:, C_TANHB2] = np.float32(b_exact - np.float64(np.float32(-4.05)))
    cn[:, C_EPSLN] = 1e-5
    cn[:, C_BO:C_BO + 8] = b_o.reshape(8, 128).T
    cn[:, C_GAMMA:C_GAMMA + 8] = gamma.reshape(8, 128).T
    cn[:, C_BETA:C_BETA + 8] = beta.reshape(8, 128).T
    cn[0, C_SEL1:C_SEL1 + 64] = 1.0
    cn[1, C_SEL1 + 64:C_SEL1 + 128] = 1.0
    cn[0, C_SEL05:C_SEL05 + 64] = 0.5
    cn[1, C_SEL05 + 64:C_SEL05 + 128] = 0.5
    return cn


def build_program(S, T, HID, phases=(1, 1, 1)):
    NTG = T // TGT
    NKT = T // 128
    NHC = HID // 128
    nc = bass.Bass("TRN2", target_bir_lowering=False)
    stack = contextlib.ExitStack()
    nc._keepalive_stack = stack

    x_hbm = nc.dram_tensor("x", [S, T, D], F32, kind="ExternalInput").ap()
    wq_h = nc.dram_tensor("wq", [D, D], F32, kind="ExternalInput").ap()
    wk_h = nc.dram_tensor("wk", [D, D], F32, kind="ExternalInput").ap()
    wv_h = nc.dram_tensor("wv", [D, D], F32, kind="ExternalInput").ap()
    wo_h = nc.dram_tensor("wo", [D, D], F32, kind="ExternalInput").ap()
    wgT_h = nc.dram_tensor("wgT", [D, HID], I32, kind="ExternalInput").ap()
    wuT_h = nc.dram_tensor("wuT", [D, HID], I32, kind="ExternalInput").ap()
    wdT_h = nc.dram_tensor("wdT", [HID, D], I32, kind="ExternalInput").ap()
    cn_h = nc.dram_tensor("cn", [128, CC], F32, kind="ExternalInput").ap()
    out_h = nc.dram_tensor("out", [S, T, D], F32, kind="ExternalOutput").ap()
    sc_xnormT = nc.dram_tensor("sc_xnormT", [S, D, T], F32).ap()
    sc_xnT = nc.dram_tensor("sc_xnT", [S, D, T], F32).ap()
    sc_vp = nc.dram_tensor("sc_vp", [S, T, D], F32).ap()
    sc_k = nc.dram_tensor("sc_k", [S, D, T], F32).ap()
    sc_s = nc.dram_tensor("sc_s", [S, D, T], F32).ap()
    sc_wg = nc.dram_tensor("sc_wg", [NHC, 128, NCH * 128], F16).ap()
    sc_wu = nc.dram_tensor("sc_wu", [NHC, 128, NCH * 128], F16).ap()
    sc_wd = nc.dram_tensor("sc_wd", [NCH, 128, NHC * 128], F16).ap()

    em = Em()

    cn = stack.enter_context(nc.sbuf_tensor("cn_sb", [128, CC], F32))
    c_all = stack.enter_context(nc.sbuf_tensor("c_all", [128, 8], F32))
    ps = stack.enter_context(nc.psum_tensor("ps", [128, 4096], F32))

    def PS(b, o=0, w=512):
        return ps[:, 512 * b + o: 512 * b + o + w]

    ident = cn[:, C_IDENT:C_IDENT + 128]
    gqblk = cn[:, C_GQBLK:C_GQBLK + 2]
    gkblk = cn[:, C_GKBLK:C_GKBLK + 2]
    ones = cn[:, C_ONES:C_ONES + 1]
    tanhb = cn[:, C_TANHB:C_TANHB + 1]
    epsln = cn[:, C_EPSLN:C_EPSLN + 1]
    sel1 = cn[0:2, C_SEL1:C_SEL1 + 128]
    sel05 = cn[0:2, C_SEL05:C_SEL05 + 128]

    em.op("sync", lambda e: e.dma_start(out=cn[:], in_=cn_h[:, :]),
          writes=["cn"], dma=True, tag="cn_load")
    em.op("vector", lambda e: e.tensor_add(
        out=tanhb2[:], in0=cn[:, C_TANHB:C_TANHB + 1],
        in1=cn[:, C_TANHB2:C_TANHB2 + 1]),
        reads=["cn"], writes=["tanhb2"])

    nmu = stack.enter_context(nc.sbuf_tensor("nmu", [128, NSB], F32))
    var = stack.enter_context(nc.sbuf_tensor("var", [128, NSB], F32))
    rs = stack.enter_context(nc.sbuf_tensor("rs", [128, NSB], F32))
    sstile = stack.enter_context(nc.sbuf_tensor("sstile", [128, 64], F32))
    gdot = stack.enter_context(nc.sbuf_tensor("gdot", [128, 64], F32))
    gk_tm = stack.enter_context(nc.sbuf_tensor("gk_tm", [128, 64], F32))
    gq_tm = stack.enter_context(nc.sbuf_tensor("gq_tm", [128, 64], F32))
    n2t = stack.enter_context(nc.sbuf_tensor("n2t", [2, TGT], F32))
    tanhb2 = stack.enter_context(nc.sbuf_tensor("tanhb2", [128, 1], F32))

    def layer_norm(xin, xout, sqbuf, key_in, key_out, key_sq, affine=False):
        em.op("vector", lambda e: e.reduce_sum(out=nmu[:, 0:NSB], in_=xin[:],
                                               axis=AX.X, negate=True),
              reads=[key_in], writes=["nmu"])
        em.op("vector", lambda e: e.tensor_scalar_mul(out=nmu[:, 0:NSB],
                                                      in0=nmu[:, 0:NSB],
                                                      scalar1=1.0 / D),
              reads=["nmu"], writes=["nmu"])
        for s in range(NSB):
            em.op("scalar", lambda e, s=s: e.activation(
                out=xout[:, s, :], in_=xin[:, s, :], func=AF.Identity,
                bias=nmu[:, s:s + 1], scale=1.0),
                reads=[key_in, "nmu"], writes=[key_out])
        em.op("vector", lambda e: e.tensor_mul(out=sqbuf[:], in0=xout[:],
                                               in1=xout[:]),
              reads=[key_out], writes=[key_sq])
        em.op("vector", lambda e: e.reduce_sum(out=var[:, 0:NSB], in_=sqbuf[:],
                                               axis=AX.X),
              reads=[key_sq], writes=["var"])
        em.op("vector", lambda e: e.tensor_scalar_mul(out=var[:, 0:NSB],
                                                      in0=var[:, 0:NSB],
                                                      scalar1=1.0 / D),
              reads=["var"], writes=["var"])
        em.op("scalar", lambda e: e.activation(out=rs[:, 0:NSB],
                                               in_=var[:, 0:NSB],
                                               func=AF.Sqrt,
                                               bias=epsln, scale=1.0),
              reads=["var", "cn"], writes=["rs"])
        em.op("vector", lambda e: e.reciprocal(out=rs[:, 0:NSB],
                                               in_=rs[:, 0:NSB]),
              reads=["rs"], writes=["rs"])
        for s in range(NSB):
            em.op("scalar", lambda e, s=s: e.activation(
                out=xout[:, s, :], in_=xout[:, s, :], func=AF.Identity,
                bias=0.0, scale=rs[:, s:s + 1]),
                reads=[key_out, "rs"], writes=[key_out])

    def layer_norm2(xin, xout, key_in, key_out):
        em.op("vector", lambda e: e.reduce_sum(out=nmu[:, 0:NSB], in_=xin[:],
                                               axis=AX.X, negate=True),
              reads=[key_in], writes=["nmu"])
        em.op("vector", lambda e: e.tensor_scalar_mul(out=nmu[:, 0:NSB],
                                                      in0=nmu[:, 0:NSB],
                                                      scalar1=1.0 / D),
              reads=["nmu"], writes=["nmu"])
        em.op("vector", lambda e: e.tensor_mul(out=xout[:], in0=xin[:],
                                               in1=xin[:]),
              reads=[key_in], writes=[key_out])
        em.op("vector", lambda e: e.reduce_sum(out=var[:, 0:NSB], in_=xout[:],
                                               axis=AX.X),
              reads=[key_out], writes=["var"])
        em.op("vector", lambda e: e.tensor_mul(out=rs[:, 0:NSB],
                                               in0=nmu[:, 0:NSB],
                                               in1=nmu[:, 0:NSB]),
              reads=["nmu"], writes=["rs"])
        em.op("vector", lambda e: e.tensor_scalar_mul(out=var[:, 0:NSB],
                                                      in0=var[:, 0:NSB],
                                                      scalar1=1.0 / D),
              reads=["var"], writes=["var"])
        em.op("vector", lambda e: e.tensor_sub(out=var[:, 0:NSB],
                                               in0=var[:, 0:NSB],
                                               in1=rs[:, 0:NSB]),
              reads=["var", "rs"], writes=["var"])
        em.op("scalar", lambda e: e.activation(out=rs[:, 0:NSB],
                                               in_=var[:, 0:NSB],
                                               func=AF.Sqrt,
                                               bias=epsln, scale=1.0),
              reads=["var", "cn"], writes=["rs"])
        em.op("vector", lambda e: e.reciprocal(out=rs[:, 0:NSB],
                                               in_=rs[:, 0:NSB]),
              reads=["rs"], writes=["rs"])
        for s in range(NSB):
            em.op("scalar", lambda e, s=s: e.activation(
                out=xout[:, s, :], in_=xin[:, s, :], func=AF.Identity,
                bias=nmu[:, s:s + 1], scale=1.0),
                reads=[key_in, "nmu"], writes=[key_out])
        for s in range(NSB):
            em.op("scalar", lambda e, s=s: e.activation(
                out=xout[:, s, :], in_=xout[:, s, :], func=AF.Identity,
                bias=0.0, scale=rs[:, s:s + 1]),
                reads=[key_out, "rs"], writes=[key_out])

    def phase_a(b, bufAd, bufBd, xnT, xnormT, wk_sb, wv_sb, kT_tg,
                sq_sb, v_tg, rn2):
        em.op("vector", lambda e: e.memset(c_all[:], 0.0), writes=["c_all"])

        def ld_weights():
            em.op("sync", lambda e: e.dma_start(
                out=wk_sb[:],
                in_=wk_h[:, :].rearrange("(c p) d -> p c d", p=128)),
                  writes=["wk_sb"], dma=True, tag="wk")
            em.op("sync", lambda e: e.dma_start(
                out=wv_sb[:],
                in_=wv_h[:, :].rearrange("(c p) d -> p c d", p=128)),
                  writes=["wv_sb"], dma=True, tag="wv")

        def pre_load(g):
            gp = g % 2
            bufA = bufAd[gp]
            kA = f"bufA{gp}"
            t0 = TGT * g
            srcx = x_hbm[b, t0:t0 + TGT, :].rearrange("(s p) d -> p s d",
                                                      p=128)
            em.op("sync", lambda e, srcx=srcx, bufA=bufA: e.dma_start(
                out=bufA[:], in_=srcx),
                  writes=[kA], dma=True, tag="xload")

        def pre_ln1(g):
            gp = g % 2
            bufA, bufB = bufAd[gp], bufBd[gp]
            kA, kB = f"bufA{gp}", f"bufB{gp}"
            layer_norm(bufA, bufB, bufA, kA, kB, kA)

        def pre_ln2(g):
            gp = g % 2
            bufA, bufB = bufAd[gp], bufBd[gp]
            kA, kB = f"bufA{gp}", f"bufB{gp}"
            layer_norm2(bufB, bufA, kB, kA)

        def main1(g):
            gp = g % 2
            bufA, bufB = bufAd[gp], bufBd[gp]
            kA, kB = f"bufA{gp}", f"bufB{gp}"
            t0 = TGT * g
            for c in range(NCH):
                tb = 3 * (c % 2)
                for s in range(NSB):
                    em.op("tensor", lambda e, c=c, s=s, bufB=bufB, tb=tb:
                          e.transpose(
                        PS(tb, 128 * s, 128), bufB[:, s, 128 * c:128 * c + 128],
                        ident),
                        reads=[kB, "cn"], writes=[f"ps{tb}"])
                em.op("vector", lambda e, c=c, tb=tb: e.tensor_copy(
                    out=xnT[:, c, :], in_=PS(tb)),
                    reads=[f"ps{tb}"], writes=["xnT"])
            dst = sc_xnT[b, :, t0:t0 + TGT].rearrange("(c p) t -> p c t",
                                                      p=128)
            em.op("gpsimd", lambda e, dst=dst: e.dma_start(out=dst,
                                                           in_=xnT[:]),
                  reads=["xnT"], dma=True)
            for c in range(NCH):
                tb = 3 * (c % 2)
                for s in range(NSB):
                    em.op("tensor", lambda e, c=c, s=s, bufA=bufA, tb=tb:
                          e.transpose(
                        PS(tb, 128 * s, 128), bufA[:, s, 128 * c:128 * c + 128],
                        ident),
                        reads=[kA, "cn"], writes=[f"ps{tb}"])
                em.op("scalar", lambda e, c=c, tb=tb: e.activation(
                    out=xnormT[:, c, :], in_=PS(tb), func=AF.Identity,
                    scale=cn[:, C_GAMMA + c:C_GAMMA + c + 1],
                    bias=cn[:, C_BETA + c:C_BETA + c + 1]),
                    reads=[f"ps{tb}", "cn"], writes=["xnormT"])
            dst = sc_xnormT[b, :, t0:t0 + TGT].rearrange("(c p) t -> p c t",
                                                         p=128)
            em.op("gpsimd", lambda e, dst=dst: e.dma_start(out=dst,
                                                           in_=xnormT[:]),
                  reads=["xnormT"], dma=True)
            def chainA(j):
                em.op("scalar", lambda e, j=j: e.activation(
                    out=sq_sb[:], in_=kT_tg[:, j, :], func=AF.Square),
                    reads=["kT_tg"], writes=["sq_sb"])
                em.op("tensor", lambda e, j=j: e.matmul(
                    ps[0:2, 512 * 3:512 * 3 + TGT], cn[:, C_BLK2:C_BLK2 + 2],
                    sq_sb[:], start=True, stop=True),
                    reads=["sq_sb", "cn"], writes=["ps3"])
                em.op("scalar", lambda e, j=j: e.activation(
                    out=n2t[:, :], in_=ps[0:2, 512 * 3:512 * 3 + TGT],
                    func=AF.Sqrt),
                    reads=["ps3"], writes=["n2t"])

            def chainB(j):
                em.op("vector", lambda e: e.tensor_single_scalar(
                    out=n2t[:, :], in_=n2t[:, :], scalar=1e-12, op=ALU.max),
                    reads=["n2t"], writes=["n2t"])
                em.op("vector", lambda e, j=j: e.reciprocal(
                    out=rn2[:, :], in_=n2t[:, :]),
                    reads=["n2t"], writes=["rn2"])
                em.op("tensor", lambda e, j=j: e.matmul(
                    PS(4), sel1, rn2[:, :], start=True, stop=True),
                    reads=["rn2", "cn"], writes=["ps4"])
                em.op("vector", lambda e, j=j: e.tensor_mul(
                    out=kT_tg[:, j, :], in0=kT_tg[:, j, :],
                    in1=PS(4)),
                    reads=["ps4", "kT_tg"], writes=["kT_tg"])
                for s in range(NSB):
                    em.op("tensor", lambda e, j=j, s=s: e.matmul(
                        ps[:, 512 * 5 + 16 * s + 2 * j:
                           512 * 5 + 16 * s + 2 * j + 2],
                        kT_tg[:, j, 128 * s:128 * s + 128], gkblk,
                        start=True, stop=True),
                        reads=["kT_tg", "cn"], writes=["ps5"])

            for j in range(NCH):
                pb = 1 + (j % 2)
                for i in range(NCH):
                    em.op("tensor", lambda e, i=i, j=j, pb=pb: e.matmul(
                        PS(pb), wk_sb[:, i, 128 * j:128 * j + 128],
                        xnormT[:, i, :], start=(i == 0), stop=(i == NCH - 1)),
                        reads=["wk_sb", "xnormT"], writes=[f"ps{pb}"],
                        group=f"kp{g}_{j}", gclose=(i == NCH - 1))
                if j >= 2:
                    chainB(j - 2)
                if j >= 1:
                    chainA(j - 1)
                em.op("vector", lambda e, j=j, pb=pb: e.tensor_copy(
                    out=kT_tg[:, j, :], in_=PS(pb)),
                    reads=[f"ps{pb}"], writes=["kT_tg"])
            chainB(NCH - 2)
            chainA(NCH - 1)
            chainB(NCH - 1)
            dst = sc_k[b, :, t0:t0 + TGT].rearrange("(c p) t -> p c t",
                                                    p=128)
            em.op("gpsimd", lambda e, dst=dst: e.dma_start(out=dst,
                                                           in_=kT_tg[:]),
                  reads=["kT_tg"], dma=True, tag="kstore")
            em.op("scalar", lambda e: e.activation(
                out=gdot[:, :], in_=ps[:, 512 * 5:512 * 5 + 64],
                func=AF.Sigmoid),
                reads=["ps5"], writes=["gdot"])
            em.op("vector", lambda e: e.tensor_scalar(
                out=gk_tm[:, :], in0=gdot[:, :], scalar1=0.75, scalar2=0.25,
                op0=ALU.mult, op1=ALU.add),
                reads=["gdot"], writes=["gk_tm"])

        def main2(g, lnhook=None):
            gp = g % 2
            t0 = TGT * g
            VB = (1, 2, 6, 7)
            for s in range(NSB):
                for jh in range(2):
                    pb = VB[(2 * s + jh) % 4]
                    for i in range(NCH):
                        em.op("tensor", lambda e, i=i, s=s, jh=jh, pb=pb:
                              e.matmul(PS(pb),
                                       xnT[:, i, 128 * s:128 * s + 128],
                                       wv_sb[:, i, 512 * jh:512 * jh + 512],
                                       start=(i == 0), stop=(i == NCH - 1)),
                              reads=["wv_sb", "xnT"], writes=[f"ps{pb}"],
                              group=f"vp{g}_{s}_{jh}", gclose=(i == NCH - 1))
                    for hh in range(8):
                        h = 8 * jh + hh
                        em.op("vector", lambda e, s=s, jh=jh, hh=hh, h=h,
                              pb=pb: e.tensor_scalar_mul(
                                  out=v_tg[:, s, 512 * jh + 64 * hh:
                                           512 * jh + 64 * hh + 64],
                                  in0=PS(pb, 64 * hh, 64),
                                  scalar1=gk_tm[:, 16 * s + h:16 * s + h + 1]),
                              reads=[f"ps{pb}", "gk_tm"], writes=["v_tg"])
            if lnhook is not None:
                lnhook()
            for s in range(NSB):
                for p in range(8):
                    em.op("tensor", lambda e, s=s, p=p: e.matmul(
                        ps[:, 512 * 4 + p:512 * 4 + p + 1],
                        v_tg[:, s, 128 * p:128 * p + 128], ones,
                        start=True, stop=True),
                        reads=["v_tg", "cn"], writes=["ps4"])
                em.op("vector", lambda e, s=s: e.tensor_add(
                    out=c_all[:], in0=c_all[:], in1=ps[:, 512 * 4:512 * 4 + 8]),
                    reads=["ps4", "c_all"], writes=["c_all"])
            dst = sc_vp[b, t0:t0 + TGT, :].rearrange("(s p) d -> p s d", p=128)
            em.op("gpsimd", lambda e, dst=dst: e.dma_start(out=dst,
                                                           in_=v_tg[:]),
                  reads=["v_tg"], dma=True)

        pre_load(0)
        ld_weights()
        pre_ln1(0)
        pre_ln2(0)
        for g in range(NTG):
            if g + 1 < NTG:
                pre_load(g + 1)
            main1(g)
            if g + 1 < NTG:
                pre_ln1(g + 1)
                main2(g, lnhook=lambda g=g: pre_ln2(g + 1))
            else:
                main2(g)

    def phase_b1(b, xnormT2, qstage, sqq, qnT, wq_sb, wo_sb, kc_sb, vp_sb,
                 t_sb, f1, fusedT, sT_st, gq_hm):
        em.op("sync", lambda e: e.dma_start(
            out=wq_sb[:], in_=wq_h[:, :].rearrange("(c p) d -> p c d", p=128)),
              writes=["wq_sb"], dma=True, tag="wq")
        src0 = sc_xnormT[b, :, 0:TGT].rearrange("(c p) t -> p c t", p=128)
        em.op("sync", lambda e, src=src0: e.dma_start(out=xnormT2[0][:],
                                                      in_=src),
              writes=["xn0"], dma=True)
        em.op("sync", lambda e: e.dma_start(
            out=wo_sb[:], in_=wo_h[:, :].rearrange("(c p) d -> p c d", p=128)),
              writes=["wo_sb"], dma=True, tag="wo")
        for qt in range(NTG):
            q0 = TGT * qt
            xp = qt % 2
            xnormT = xnormT2[xp]
            xkey = f"xn{xp}"
            if qt + 1 < NTG:
                srcn = sc_xnormT[b, :, q0 + TGT:q0 + 2 * TGT].rearrange(
                    "(c p) t -> p c t", p=128)
                em.op("gpsimd", lambda e, src=srcn, xq=(qt + 1) % 2:
                      e.dma_start(out=xnormT2[xq][:], in_=src),
                      writes=[f"xn{(qt + 1) % 2}"], dma=True)
            for s in range(NSB):
                for jh in range(2):
                    pb = jh
                    for i in range(NCH):
                        em.op("tensor", lambda e, i=i, s=s, jh=jh, pb=pb,
                              xnormT=xnormT:
                              e.matmul(PS(pb),
                                       xnormT[:, i, 128 * s:128 * s + 128],
                                       wq_sb[:, i, 512 * jh:512 * jh + 512],
                                       start=(i == 0), stop=(i == NCH - 1)),
                              reads=["wq_sb", xkey], writes=[f"ps{pb}"],
                              group=f"qp{qt}_{s}_{jh}", gclose=(i == NCH - 1))
                    em.op("vector", lambda e, s=s, jh=jh, pb=pb: e.tensor_copy(
                        out=qstage[:, s, 512 * jh:512 * jh + 512], in_=PS(pb)),
                        reads=[f"ps{pb}"], writes=["qf"])
                    em.op("scalar", lambda e, s=s, jh=jh: e.activation(
                        out=sqq[:, 512 * jh:512 * jh + 512],
                        in_=qstage[:, s, 512 * jh:512 * jh + 512],
                        func=AF.Square),
                        reads=["qf"], writes=["sqq"])
                em.op("vector", lambda e, s=s: e.reduce_sum(
                    out=sstile[:, 16 * s:16 * s + 16],
                    in_=sqq[:].rearrange("p (h d) -> p h d", d=DH), axis=AX.X),
                    reads=["sqq"], writes=["sstile"])
            em.op("scalar", lambda e: e.activation(out=sstile[:, :],
                                                   in_=sstile[:, :],
                                                   func=AF.Sqrt),
                  reads=["sstile"], writes=["sstile"])
            em.op("vector", lambda e: e.tensor_single_scalar(
                out=sstile[:, :], in_=sstile[:, :], scalar=1e-12, op=ALU.max),
                reads=["sstile"], writes=["sstile"])
            em.op("vector", lambda e: e.reciprocal(out=sstile[:, :],
                                                   in_=sstile[:, :]),
                  reads=["sstile"], writes=["sstile"])
            for s in range(NSB):
                em.op("vector", lambda e, s=s: e.tensor_mul(
                    out=qstage[:, s, :].rearrange("p (h d) -> p h d", d=DH),
                    in0=qstage[:, s, :].rearrange("p (h d) -> p h d", d=DH),
                    in1=sstile[:, 16 * s:16 * s + 16].unsqueeze(2)
                        .broadcast_to([128, 16, DH])),
                    reads=["qf", "sstile"], writes=["qf"])
            for c in range(NCH):
                tb = 2 + (c % 2)
                for s in range(NSB):
                    em.op("tensor", lambda e, c=c, s=s, tb=tb: e.transpose(
                        PS(tb, 128 * s, 128),
                        qstage[:, s, 128 * c:128 * c + 128], ident),
                        reads=["qf", "cn"], writes=[f"ps{tb}"])
                if c % 2 == 0:
                    em.op("scalar", lambda e, c=c, tb=tb: e.activation(
                        out=qnT[:, c, :], in_=PS(tb), func=AF.Identity),
                        reads=[f"ps{tb}"], writes=["qnT"])
                else:
                    em.op("vector", lambda e, c=c, tb=tb: e.tensor_copy(
                        out=qnT[:, c, :], in_=PS(tb)),
                        reads=[f"ps{tb}"], writes=["qnT"])
            for s in range(NSB):
                for c in range(NCH):
                    em.op("tensor", lambda e, c=c, s=s: e.matmul(
                        ps[:, 512 * 3 + 16 * s + 2 * c:
                           512 * 3 + 16 * s + 2 * c + 2],
                        qnT[:, c, 128 * s:128 * s + 128], gqblk,
                        start=True, stop=True),
                        reads=["qnT", "cn"], writes=["ps3"])
            em.op("scalar", lambda e: e.activation(
                out=gdot[:, :], in_=ps[:, 512 * 3:512 * 3 + 64],
                func=AF.Sigmoid),
                reads=["ps3"], writes=["gdot"])
            em.op("vector", lambda e: e.tensor_scalar(
                out=gq_tm[:, :], in0=gdot[:, :], scalar1=0.75, scalar2=0.25,
                op0=ALU.mult, op1=ALU.add),
                reads=["gdot"], writes=["gq_tm"])
            for c in range(NCH):
                for s in range(NSB):
                    em.op("tensor", lambda e, c=c, s=s: e.transpose(
                        ps[0:2, 512 * 3 + 128 * s:512 * 3 + 128 * s + 128],
                        gq_tm[:, 16 * s + 2 * c:16 * s + 2 * c + 2], ident),
                        reads=["gq_tm", "cn"], writes=["ps3"])
                em.op("vector", lambda e, c=c: e.tensor_copy(
                    out=gq_hm[c][:, :], in_=ps[0:2, 512 * 3:512 * 3 + TGT]),
                    reads=["ps3"], writes=[f"gq_hm{c}"])
            for c in range(NCH):
                cp = c % 2
                src = sc_vp[b, :, 128 * c:128 * c + 128].rearrange(
                    "(kt p) d -> p kt d", p=128)
                em.op("sync", lambda e, src=src, cp=cp: e.dma_start(
                    out=vp_sb[cp][:], in_=src),
                    writes=[f"vp{cp}"], dma=True, tag="vp")
                em.op("sync", lambda e, c=c, cp=cp: e.dma_start(
                    out=kc_sb[cp][:], in_=sc_k[b, 128 * c:128 * c + 128, :]),
                    writes=[f"kc{cp}"], dma=True, tag="kc")

                def align(kt):
                    alb = 2 + 2 * (kt % 2)
                    em.op("tensor", lambda e, kt=kt, alb=alb, cp=cp, c=c:
                          e.matmul(
                        PS(alb), kc_sb[cp][0:64, 128 * kt:128 * kt + 128],
                        qnT[0:64, c, :], start=True, stop=True),
                        reads=[f"kc{cp}", "qnT"], writes=[f"ps{alb}"])
                    em.op("tensor", lambda e, kt=kt, alb=alb, cp=cp, c=c:
                          e.matmul(
                        PS(alb + 1), kc_sb[cp][64:128, 128 * kt:128 * kt + 128],
                        qnT[64:128, c, :], start=True, stop=True,
                        tile_position=(64, 0)),
                        reads=[f"kc{cp}", "qnT"], writes=[f"ps{alb + 1}"])

                align(0)
                for kt in range(NKT):
                    kp = kt % 2
                    alb = 2 + 2 * kp
                    em.op("scalar", lambda e, kp=kp, alb=alb: e.activation(
                        out=t_sb[kp][:], in_=ps[:, 512 * alb:512 * alb + 1024],
                        func=AF.Tanh, scale=15.0, bias=tanhb2[:, 0:1]),
                        reads=[f"ps{alb}", f"ps{alb + 1}", "tanhb2"],
                        writes=[f"t{kp}"])
                    if kt + 1 < NKT:
                        align(kt + 1)
                    em.op("tensor", lambda e, kt=kt, kp=kp, cp=cp: e.matmul(
                        ps[0:64, 512 * 6:512 * 6 + 512],
                        vp_sb[cp][:, kt, 0:64], t_sb[kp][:, 0:512],
                        start=(kt == 0), stop=(kt == NKT - 1)),
                        reads=[f"vp{cp}", f"t{kp}"], writes=["ps6"],
                        group=f"fuA{qt}_{c}", gclose=(kt == NKT - 1))
                    em.op("tensor", lambda e, kt=kt, kp=kp, cp=cp: e.matmul(
                        ps[64:128, 512 * 7:512 * 7 + 512],
                        vp_sb[cp][:, kt, 64:128], t_sb[kp][:, 512:1024],
                        start=(kt == 0), stop=(kt == NKT - 1),
                        tile_position=(0, 64)),
                        reads=[f"vp{cp}", f"t{kp}"], writes=["ps7"],
                        group=f"fuB{qt}_{c}", gclose=(kt == NKT - 1))
                em.op("tensor", lambda e, c=c: e.matmul(
                    PS(2), sel05, gq_hm[c][:, :], start=True, stop=True),
                    reads=[f"gq_hm{c}", "cn"], writes=["ps2"])
                em.op("vector", lambda e: e.tensor_copy(
                    out=f1[:], in_=PS(2)),
                    reads=["ps2"], writes=["f1"])
                em.op("vector", lambda e, c=c: e.scalar_tensor_tensor(
                    out=fusedT[0:64, c, :], in0=ps[0:64, 512 * 6:512 * 6 + 512],
                    scalar=c_all[0:64, c:c + 1], in1=f1[0:64, :],
                    op0=ALU.add, op1=ALU.mult),
                    reads=["ps6", "f1", "c_all"], writes=["qf"])
                em.op("vector", lambda e, c=c: e.scalar_tensor_tensor(
                    out=fusedT[64:128, c, :], in0=ps[64:128, 512 * 7:512 * 7 + 512],
                    scalar=c_all[64:128, c:c + 1], in1=f1[64:128, :],
                    op0=ALU.add, op1=ALU.mult),
                    reads=["ps7", "f1", "c_all"], writes=["qf"])
            for j in range(NCH):
                pb = j % 2
                for i in range(NCH):
                    em.op("tensor", lambda e, i=i, j=j, pb=pb: e.matmul(
                        PS(pb), wo_sb[:, i, 128 * j:128 * j + 128],
                        fusedT[:, i, :], start=(i == 0), stop=(i == NCH - 1)),
                        reads=["wo_sb", "qf"], writes=[f"ps{pb}"],
                        group=f"wo{qt}_{j}", gclose=(i == NCH - 1))
                sp = j % 2
                em.op("scalar", lambda e, j=j, pb=pb, sp=sp: e.activation(
                    out=sT_st[sp][:], in_=PS(pb), func=AF.Identity,
                    bias=cn[:, C_BO + j:C_BO + j + 1]),
                    reads=[f"ps{pb}", "cn"], writes=[f"sT{sp}"])
                dst = sc_s[b, 128 * j:128 * j + 128, q0:q0 + TGT]
                em.op("gpsimd", lambda e, dst=dst, sp=sp: e.dma_start(
                    out=dst, in_=sT_st[sp][:]),
                    reads=[f"sT{sp}"], dma=True)

    def phase_b2(b, xnT2d, xh, xl, wg_sb, wu_sb, wd_sb, hf, sg_sb, hh, hl,
                 g_sb, bufx, s_sb, prod, out_sb):
        NHC_l = NHC
        src0 = sc_xnT[b, :, 0:TGT].rearrange("(c p) t -> p c t", p=128)
        em.op("gpsimd", lambda e, src=src0: e.dma_start(out=xnT2d[0][:],
                                                        in_=src),
              writes=["xt0"], dma=True)

        def split(qt):
            xp = qt % 2
            xnT2 = xnT2d[xp]
            xtkey = f"xt{xp}"
            for c in range(NCH):
                em.op("vector", lambda e, c=c, xnT2=xnT2: e.tensor_copy(
                    out=xh[:, c, :], in_=xnT2[:, c, :]),
                    reads=[xtkey], writes=["xh"])
                em.op("vector", lambda e, c=c, xnT2=xnT2: e.tensor_sub(
                    out=xl[:, c, :], in0=xnT2[:, c, :], in1=xh[:, c, :]),
                    reads=[xtkey, "xh"], writes=["xl"])

        split(0)
        for qt in range(NTG):
            q0 = TGT * qt
            if qt + 1 < NTG:
                srcn = sc_xnT[b, :, q0 + TGT:q0 + 2 * TGT].rearrange(
                    "(c p) t -> p c t", p=128)
                em.op("gpsimd", lambda e, src=srcn, xq=(qt + 1) % 2:
                      e.dma_start(out=xnT2d[xq][:], in_=src),
                      writes=[f"xt{(qt + 1) % 2}"], dma=True)
            for hc in range(NHC_l):
                wp = hc % 2
                em.op("sync", lambda e, hc=hc, wp=wp: e.dma_start(
                    out=wg_sb[wp][:],
                    in_=sc_wg[hc, :, :].rearrange("p (c h) -> p c h", h=128)),
                    writes=[f"wg{wp}"], dma=True, tag="wg")
                em.op("sync", lambda e, hc=hc, wp=wp: e.dma_start(
                    out=wu_sb[wp][:],
                    in_=sc_wu[hc, :, :].rearrange("p (c h) -> p c h", h=128)),
                    writes=[f"wu{wp}"], dma=True, tag="wu")

                for i in range(NCH):
                    em.op("tensor", lambda e, i=i, wp=wp: e.matmul(
                        PS(0), wg_sb[wp][:, i, :], xh[:, i, :],
                        start=(i == 0), stop=False),
                        reads=[f"wg{wp}", "xh"], writes=["ps0"],
                        group=f"ga{qt}_{hc}")
                    em.op("tensor", lambda e, i=i, wp=wp: e.matmul(
                        PS(0), wg_sb[wp][:, i, :], xl[:, i, :],
                        start=False, stop=(i == NCH - 1)),
                        reads=[f"wg{wp}", "xl"], writes=["ps0"],
                        group=f"ga{qt}_{hc}", gclose=(i == NCH - 1))
                for i in range(NCH):
                    em.op("tensor", lambda e, i=i, wp=wp: e.matmul(
                        PS(1), wu_sb[wp][:, i, :], xh[:, i, :],
                        start=(i == 0), stop=False),
                        reads=[f"wu{wp}", "xh"], writes=["ps1"],
                        group=f"up{qt}_{hc}")
                    em.op("tensor", lambda e, i=i, wp=wp: e.matmul(
                        PS(1), wu_sb[wp][:, i, :], xl[:, i, :],
                        start=False, stop=(i == NCH - 1)),
                        reads=[f"wu{wp}", "xl"], writes=["ps1"],
                        group=f"up{qt}_{hc}", gclose=(i == NCH - 1))
                em.op("scalar", lambda e: e.activation(
                    out=sg_sb[:], in_=PS(0), func=AF.Silu, scale=S127),
                    reads=["ps0"], writes=["sg_sb"])
                em.op("vector", lambda e: e.scalar_tensor_tensor(
                    out=hf[:], in0=PS(1), scalar=S127, in1=sg_sb[:],
                    op0=ALU.mult, op1=ALU.mult),
                    reads=["ps1", "sg_sb"], writes=["hf"])
                em.op("vector", lambda e, hc=hc: e.tensor_copy(
                    out=hh[:, hc, :], in_=hf[:]),
                    reads=["hf"], writes=["hh"])
                em.op("vector", lambda e, hc=hc: e.tensor_sub(
                    out=hl[:, hc, :], in0=hf[:], in1=hh[:, hc, :]),
                    reads=["hf", "hh"], writes=["hl"])
            if qt + 1 < NTG:
                split(qt + 1)
            for j in range(NCH):
                pb = 2 + (j % 2)
                wp = j % 2
                em.op("sync", lambda e, j=j, wp=wp: e.dma_start(
                    out=wd_sb[wp][:],
                    in_=sc_wd[j, :, :].rearrange("p (c d) -> p c d", d=128)),
                    writes=[f"wd{wp}"], dma=True, tag="wd")
                for hc in range(NHC_l):
                    em.op("tensor", lambda e, hc=hc, j=j, pb=pb, wp=wp:
                          e.matmul(
                        PS(pb), wd_sb[wp][:, hc, :],
                        hh[:, hc, :], start=(hc == 0), stop=False),
                        reads=[f"wd{wp}", "hh"], writes=[f"ps{pb}"],
                        group=f"dn{qt}_{j}")
                    em.op("tensor", lambda e, hc=hc, j=j, pb=pb, wp=wp:
                          e.matmul(
                        PS(pb), wd_sb[wp][:, hc, :],
                        hl[:, hc, :], start=False, stop=(hc == NHC_l - 1)),
                        reads=[f"wd{wp}", "hl"], writes=[f"ps{pb}"],
                        group=f"dn{qt}_{j}", gclose=(hc == NHC_l - 1))
                em.op("scalar", lambda e, j=j, pb=pb: e.activation(
                    out=g_sb[:, j, :], in_=PS(pb), func=AF.Silu, scale=S127),
                    reads=[f"ps{pb}"], writes=["g_sb"])
            src = x_hbm[b, q0:q0 + TGT, :].rearrange("(s p) d -> p s d", p=128)
            em.op("sync", lambda e, src=src: e.dma_start(out=bufx[:], in_=src),
                  writes=["bufx"], dma=True)
            src = sc_s[b, :, q0:q0 + TGT].rearrange("(c p) t -> p c t", p=128)
            em.op("sync", lambda e, src=src: e.dma_start(out=s_sb[:], in_=src),
                  writes=["s_sb"], dma=True)
            for j in range(NCH):
                em.op("vector", lambda e, j=j: e.tensor_mul(
                    out=s_sb[:, j, :], in0=s_sb[:, j, :], in1=g_sb[:, j, :]),
                    reads=["g_sb", "s_sb"], writes=["s_sb"])
            for j in range(NCH):
                tb = (4, 5, 6, 7)[j % 4]
                for s in range(NSB):
                    em.op("tensor", lambda e, j=j, s=s, tb=tb: e.transpose(
                        PS(tb, 128 * s, 128), s_sb[:, j, 128 * s:128 * s + 128],
                        ident),
                        reads=["s_sb", "cn"], writes=[f"ps{tb}"])
                for s in range(NSB):
                    em.op("vector", lambda e, j=j, s=s, tb=tb: e.tensor_add(
                        out=out_sb[:, s, 128 * j:128 * j + 128],
                        in0=PS(tb, 128 * s, 128),
                        in1=bufx[:, s, 128 * j:128 * j + 128]),
                        reads=[f"ps{tb}", "bufx"], writes=["out_sb"])
            dst = out_h[b, q0:q0 + TGT, :].rearrange("(s p) d -> p s d", p=128)
            em.op("gpsimd", lambda e, dst=dst: e.dma_start(out=dst,
                                                           in_=out_sb[:]),
                  reads=["out_sb"], dma=True, tag="out")

    with contextlib.ExitStack() as pp:
        wbuf = [pp.enter_context(nc.sbuf_tensor(f"wrep{i}", [128, HID], F16))
                for i in range(2)]
        for c in range(NCH):
            wp = c % 2
            for (wsrc, wdst, tag) in ((wgT_h, sc_wg, "rg"), (wuT_h, sc_wu, "ru")):
                em.op("gpsimd", lambda e, c=c, wp=wp, wsrc=wsrc: e.dma_start(
                    out=wbuf[wp][:],
                    in_=wsrc[128 * c:128 * c + 128, :]),
                    writes=[f"wrep{wp}"], dma=True, tag=tag)
                dst = wdst[:, :, 128 * c:128 * c + 128].rearrange(
                    "h p j -> p h j")
                em.op("gpsimd", lambda e, dst=dst, wp=wp: e.dma_start(
                    out=dst, in_=wbuf[wp][:].rearrange("p (h j) -> p h j",
                                                       j=128)),
                    reads=[f"wrep{wp}"], dma=True, tag=tag)
        wdbuf = [pp.enter_context(nc.sbuf_tensor(f"wdrep{i}", [128, D], F16))
                 for i in range(2)]
        for c in range(NHC):
            wp = c % 2
            em.op("gpsimd", lambda e, c=c, wp=wp: e.dma_start(
                out=wdbuf[wp][:],
                in_=wdT_h[128 * c:128 * c + 128, :]),
                writes=[f"wdrep{wp}"], dma=True, tag="rd")
            dst = sc_wd[:, :, 128 * c:128 * c + 128].rearrange(
                "j p d -> p j d")
            em.op("gpsimd", lambda e, dst=dst, wp=wp: e.dma_start(
                out=dst, in_=wdbuf[wp][:].rearrange("p (j d) -> p j d",
                                                    d=128)),
                reads=[f"wdrep{wp}"], dma=True, tag="rd")
    em.barrier()
    for b in range(S):
        if True:
            with contextlib.ExitStack() as pa:
                bufAd = [pa.enter_context(
                    nc.sbuf_tensor(f"bufA{i}_{b}", [128, NSB, D], F32))
                    for i in range(2)]
                bufBd = [pa.enter_context(
                    nc.sbuf_tensor(f"bufB{i}_{b}", [128, NSB, D], F32))
                    for i in range(2)]
                xnT = pa.enter_context(nc.sbuf_tensor(f"xnT_{b}", [128, NCH, TGT],
                                                      F32))
                xnormT = pa.enter_context(
                    nc.sbuf_tensor(f"xnormT_{b}", [128, NCH, TGT], F32))
                wk_sb = pa.enter_context(
                    nc.sbuf_tensor(f"wk_sb_{b}", [128, NCH, D], F32))
                wv_sb = pa.enter_context(
                    nc.sbuf_tensor(f"wv_sb_{b}", [128, NCH, D], F32))
                kT_tg = pa.enter_context(
                    nc.sbuf_tensor(f"kT_tg_{b}", [128, NCH, TGT], F32))
                sq_sb = pa.enter_context(nc.sbuf_tensor(f"sq_sb_{b}", [128, TGT],
                                                        F32))
                v_tg = pa.enter_context(nc.sbuf_tensor(f"v_tg_{b}", [128, NSB, D],
                                                       F32))
                rn2 = pa.enter_context(nc.sbuf_tensor(f"rn2_{b}", [2, TGT],
                                                      F32))
                if phases[0]:
                    phase_a(b, bufAd, bufBd, xnT, xnormT, wk_sb, wv_sb,
                            kT_tg, sq_sb, v_tg, rn2)
            em.barrier()
            with contextlib.ExitStack() as pb:
                xnormT2 = [pb.enter_context(
                    nc.sbuf_tensor(f"xn{i}_{b}", [128, NCH, TGT], F32))
                    for i in range(2)]
                qf = pb.enter_context(
                    nc.sbuf_tensor(f"qf_{b}", [128, NSB * D], F32))
                qstage = qf[:].rearrange("p (s c) -> p s c", s=NSB)
                sqq = pb.enter_context(nc.sbuf_tensor(f"sqq_{b}", [128, D], F32))
                qnT = pb.enter_context(nc.sbuf_tensor(f"qnT_{b}", [128, NCH, TGT],
                                                      F32))
                wq_sb = pb.enter_context(
                    nc.sbuf_tensor(f"wq_sb_{b}", [128, NCH, D], F32))
                wo_sb = pb.enter_context(
                    nc.sbuf_tensor(f"wo_sb_{b}", [128, NCH, D], F32))
                kc_sb = [pb.enter_context(
                    nc.sbuf_tensor(f"kc{i}_{b}", [128, T], F32))
                    for i in range(2)]
                vp_sb = [pb.enter_context(
                    nc.sbuf_tensor(f"vp{i}_{b}", [128, NKT, 128], F32))
                    for i in range(2)]
                t_sb = [pb.enter_context(
                    nc.sbuf_tensor(f"t{i}_{b}", [128, 1024], F32))
                    for i in range(2)]
                f1 = pb.enter_context(nc.sbuf_tensor(f"f1_{b}", [128, TGT], F32))
                fusedT = qf[:].rearrange("p (c t) -> p c t", c=NCH)
                sT_st = [pb.enter_context(
                    nc.sbuf_tensor(f"sT{i}_{b}", [128, TGT], F32))
                    for i in range(2)]
                gq_hm = [pb.enter_context(
                    nc.sbuf_tensor(f"gq_hm{c}_{b}", [2, TGT], F32))
                    for c in range(NCH)]
                if phases[1]:
                    phase_b1(b, xnormT2, qstage, sqq, qnT, wq_sb, wo_sb, kc_sb,
                             vp_sb, t_sb, f1, fusedT, sT_st, gq_hm)
            em.barrier()
        with contextlib.ExitStack() as pc:
            xnT2d = [pc.enter_context(
                nc.sbuf_tensor(f"xt{i}_{b}", [128, NCH, TGT], F32))
                for i in range(2)]
            out_sb = pc.enter_context(
                nc.sbuf_tensor(f"out_sb_{b}", [128, NSB, D], F32))
            xh = pc.enter_context(nc.sbuf_tensor(f"xh_b2_{b}", [128, NCH, TGT],
                                                 F16))
            xl = pc.enter_context(nc.sbuf_tensor(f"xl_b2_{b}", [128, NCH, TGT],
                                                 F16))
            wg_sb = [pc.enter_context(
                nc.sbuf_tensor(f"wg{i}_{b}", [128, NCH, 128], F16))
                for i in range(2)]
            wu_sb = [pc.enter_context(
                nc.sbuf_tensor(f"wu{i}_{b}", [128, NCH, 128], F16))
                for i in range(2)]
            wd_sb = [pc.enter_context(
                nc.sbuf_tensor(f"wd{i}_{b}", [128, NHC, 128], F16))
                for i in range(2)]
            hf = pc.enter_context(nc.sbuf_tensor(f"hf_{b}", [128, TGT], F32))
            sg_sb = pc.enter_context(nc.sbuf_tensor(f"sg_sb_{b}", [128, TGT], F32))
            hh = pc.enter_context(nc.sbuf_tensor(f"hh_{b}", [128, NHC, TGT], F16))
            hl = pc.enter_context(nc.sbuf_tensor(f"hl_{b}", [128, NHC, TGT], F16))
            g_sb = pc.enter_context(nc.sbuf_tensor(f"g_sb_{b}", [128, NCH, TGT],
                                                   F32))
            bufx = pc.enter_context(nc.sbuf_tensor(f"bufx_{b}", [128, NSB, D],
                                                   F32))
            s_sb = pc.enter_context(nc.sbuf_tensor(f"s_sb_{b}", [128, NCH, TGT],
                                                   F32))
            if phases[2]:
                phase_b2(b, xnT2d, xh, xl, wg_sb, wu_sb, wd_sb, hf, sg_sb, hh,
                         hl, g_sb, bufx, s_sb, hf, out_sb)
        em.barrier()

    em.finalize(nc, stack)
    return nc


_PROG_CACHE = {}


def _get_program(S, T, HID):
    key = (S, T, HID)
    if key not in _PROG_CACHE:
        _PROG_CACHE[key] = build_program(S, T, HID)
    return _PROG_CACHE[key]


def kernel(L0_attn_x, L0_attn_w_q, L0_attn_w_k, L0_attn_w_v, L0_attn_g_q,
           L0_attn_g_k, L0_attn_w_o, L0_attn_b_o, L0_attn_gamma, L0_attn_beta,
           L0_ffn_w_gate, L0_ffn_w_up, L0_ffn_w_down):
    x = np.ascontiguousarray(np.asarray(L0_attn_x, dtype=np.float32))
    B, T, D_ = x.shape
    assert D_ == D
    n_cores = 8
    S = B // n_cores
    HID = L0_ffn_w_gate.shape[0]
    cn = _mk_consts(np.asarray(L0_attn_b_o, np.float32),
                    np.asarray(L0_attn_gamma, np.float32),
                    np.asarray(L0_attn_beta, np.float32),
                    np.asarray(L0_attn_g_q, np.float32),
                    np.asarray(L0_attn_g_k, np.float32))
    wgT = np.ascontiguousarray(np.asarray(L0_ffn_w_gate, np.int32).T)
    wuT = np.ascontiguousarray(np.asarray(L0_ffn_w_up, np.int32).T)
    wdT = np.ascontiguousarray(np.asarray(L0_ffn_w_down, np.int32).T)
    base = {
        "wq": np.ascontiguousarray(np.asarray(L0_attn_w_q, np.float32)),
        "wk": np.ascontiguousarray(np.asarray(L0_attn_w_k, np.float32)),
        "wv": np.ascontiguousarray(np.asarray(L0_attn_w_v, np.float32)),
        "wo": np.ascontiguousarray(np.asarray(L0_attn_w_o, np.float32)),
        "wgT": wgT, "wuT": wuT, "wdT": wdT, "cn": cn,
    }
    in_maps = [dict(base, x=np.ascontiguousarray(x[S * i:S * (i + 1)]))
               for i in range(n_cores)]
    nc = _get_program(S, T, HID)
    res = run_bass_kernel_spmd(nc, in_maps, core_ids=list(range(n_cores)))
    return np.concatenate([r["out"] for r in res.results], axis=0)


# version 25
# speedup vs baseline: 1.0911x; 1.0083x over previous
import contextlib
import numpy as np
import concourse.bass as bass
import concourse.mybir as mybir
from concourse.bass_utils import run_bass_kernel_spmd

F32 = mybir.dt.float32
BF16 = mybir.dt.bfloat16
F16 = mybir.dt.float16
I32 = mybir.dt.int32
AF = mybir.ActivationFunctionType
ALU = mybir.AluOpType
AX = mybir.AxisListType

ENGS = ("sync", "scalar", "vector", "tensor", "gpsimd")
IN_ORDER = {"tensor"}

D = 1024
H = 16
DH = 64
NCH = 8
TGT = 512
NSB = TGT // 128
S127 = 1.0 / 127.0


NPOOL = 12
DMA_QUEUES = ("sync", "gpsimd")
OPCAP = None


class _Op:
    __slots__ = ("eng", "fn", "dma", "idx", "need_inc", "deps", "incs", "post",
                 "waits", "tag", "semk", "slot", "defer", "gkey")


class Em:

    def __init__(self):
        self.ops = []
        self.last_w = {}
        self.last_r = {}
        self.pending_barrier = None
        self.barrier_seen = set()
        self.dma_count = {q: 0 for q in DMA_QUEUES}
        self.slot_last = {q: {} for q in DMA_QUEUES}
        self.open_groups = {}
        self.in_open = set()
        self.last_closed_tensor = None

    def _dep(self, d, gkey=None):
        if d.idx in self.in_open:
            if gkey is not None and d.gkey == gkey:
                return None
            sub = self.last_closed_tensor
            assert sub is not None and sub.idx > d.idx, (
                "open-group dep with no later observable PE op")
            return sub.idx
        return d.idx

    def op(self, eng, fn, reads=(), writes=(), dma=False, tag="", group=None,
           gclose=False):
        if OPCAP is not None and len(self.ops) >= OPCAP:
            d = _Op(); d.idx = -1; d.defer = None; d.gkey = None
            return d
        o = _Op()
        o.eng = eng
        o.fn = fn
        o.dma = dma
        o.tag = tag
        o.idx = len(self.ops)
        o.need_inc = dma
        o.slot = None
        o.defer = None
        o.gkey = group
        if eng == "tensor":
            if group is not None:
                chain = self.open_groups.setdefault(group, [])
                chain.append(o)
                if gclose:
                    for m in chain[:-1]:
                        m.defer = o.idx
                        self.in_open.discard(m.idx)
                    del self.open_groups[group]
                    self.last_closed_tensor = o
                else:
                    self.in_open.add(o.idx)
            else:
                self.last_closed_tensor = o
        deps = set()
        if self.pending_barrier is not None and eng not in self.barrier_seen:
            deps.update(self.pending_barrier)
            self.barrier_seen.add(eng)
        for r in reads:
            w = self.last_w.get(r)
            if w is not None:
                deps.add(self._dep(w, group))
        for r in writes:
            w = self.last_w.get(r)
            if w is not None:
                deps.add(self._dep(w, group))
            for rd in self.last_r.get(r, {}).values():
                deps.add(self._dep(rd, group))
        if dma:
            n = self.dma_count[eng]
            self.dma_count[eng] = n + 1
            o.slot = n % NPOOL
            prev = self.slot_last[eng].get(o.slot)
            if prev is not None:
                deps.add(prev.idx)
            self.slot_last[eng][o.slot] = o
        deps.discard(None)
        deps.discard(o.idx)
        o.deps = sorted(d for d in deps if d < o.idx)
        self.ops.append(o)
        for r in writes:
            self.last_w[r] = o
            self.last_r[r] = {}
        for r in reads:
            self.last_r.setdefault(r, {})[eng] = o
        return o

    def barrier(self):
        targets = []
        last_by_eng = {}
        for o in self.ops:
            last_by_eng[o.eng] = o
        for e, lo in last_by_eng.items():
            if e in DMA_QUEUES:
                targets.extend(p.idx for p in self.slot_last[e].values())
            else:
                targets.append(lo.idx)
        self.pending_barrier = sorted(set(targets))
        self.barrier_seen = set()

    def finalize(self, nc, stack):
        assert not self.open_groups, f"unclosed groups: {list(self.open_groups)}"

        def fin(idx):
            while self.ops[idx].defer is not None:
                idx = self.ops[idx].defer
            return idx

        for o in self.ops:
            o.deps = sorted({fin(d) for d in o.deps})
        for o in self.ops:
            for di in o.deps:
                d = self.ops[di]
                if d.eng == o.eng and o.eng in IN_ORDER:
                    continue
                d.need_inc = True
        semk_all = [("eng", e) for e in ENGS]
        for q in DMA_QUEUES:
            semk_all += [("slot", q, i) for i in range(NPOOL)]
        sems = {k: stack.enter_context(
            nc.semaphore(("sem_%s" % k[1]) if k[0] == "eng"
                         else "sem_%s_s%d" % (k[1], k[2])))
            for k in semk_all}
        cnt = {k: 0 for k in semk_all}
        for o in self.ops:
            o.semk = ("slot", o.eng, o.slot) if o.dma else ("eng", o.eng)
            o.incs = (16 if o.dma else 1) if o.need_inc else 0
            cnt[o.semk] += o.incs
            o.post = cnt[o.semk]
        per_eng = {e: [] for e in ENGS}
        for o in self.ops:
            per_eng[o.eng].append(o)
        for e in ENGS:
            hi = {}
            for o in per_eng[e]:
                need = {}
                for di in o.deps:
                    d = self.ops[di]
                    if d.eng == e and e in IN_ORDER:
                        continue
                    assert d.need_inc and d.post > 0
                    if d.post > hi.get(d.semk, 0):
                        need[d.semk] = max(need.get(d.semk, 0), d.post)
                o.waits = [(sems[k], v) for k, v in need.items()]
                if e != "tensor":
                    for k, v in need.items():
                        hi[k] = v
        block = stack.enter_context(nc.Block())

        def mk(e, items):
            def body(eng):
                for o in items:
                    for (sem, val) in o.waits:
                        eng.wait_ge(sem, val)
                    inst = o.fn(eng)
                    if o.incs:
                        if inst is None:
                            inst = eng.nop(nofuse=True)
                        inst.then_inc(sems[o.semk], o.incs)
                for k in semk_all:
                    if (k[1] == e) and cnt[k] > 0:
                        eng.wait_ge(sems[k], cnt[k])
            return body

        for e in ENGS:
            if per_eng[e]:
                getattr(block, e)(mk(e, per_eng[e]))
        return sems, cnt


C_IDENT = 0
C_GQBLK = 128
C_GKBLK = 130
C_BLK2 = 132
C_ONES = 134
C_TANHB = 135
C_EPSLN = 136
C_TANHB2 = 137
C_BO = 138
C_GAMMA = 146
C_BETA = 154
C_SEL1 = 162
C_SEL05 = 290
CC = 512


def _mk_consts(b_o, gamma, beta, g_q, g_k):
    cn = np.zeros((128, CC), np.float32)
    cn[:, C_IDENT:C_IDENT + 128] = np.eye(128, dtype=np.float32)
    cn[0:64, C_GQBLK] = g_q
    cn[64:128, C_GQBLK + 1] = g_q
    cn[0:64, C_GKBLK] = g_k
    cn[64:128, C_GKBLK + 1] = g_k
    cn[0:64, C_BLK2] = 1.0
    cn[64:128, C_BLK2 + 1] = 1.0
    cn[:, C_ONES] = 1.0
    cn[:, C_TANHB] = -4.05
    b_exact = -(15.0 * 0.27)
    cn[:, C_TANHB2] = np.float32(b_exact - np.float64(np.float32(-4.05)))
    cn[:, C_EPSLN] = 1e-5
    cn[:, C_BO:C_BO + 8] = b_o.reshape(8, 128).T
    cn[:, C_GAMMA:C_GAMMA + 8] = gamma.reshape(8, 128).T
    cn[:, C_BETA:C_BETA + 8] = beta.reshape(8, 128).T
    cn[0, C_SEL1:C_SEL1 + 64] = 1.0
    cn[1, C_SEL1 + 64:C_SEL1 + 128] = 1.0
    cn[0, C_SEL05:C_SEL05 + 64] = 0.5
    cn[1, C_SEL05 + 64:C_SEL05 + 128] = 0.5
    return cn


def build_program(S, T, HID, phases=(1, 1, 1)):
    NTG = T // TGT
    NKT = T // 128
    NHC = HID // 128
    nc = bass.Bass("TRN2", target_bir_lowering=False)
    stack = contextlib.ExitStack()
    nc._keepalive_stack = stack

    x_hbm = nc.dram_tensor("x", [S, T, D], F32, kind="ExternalInput").ap()
    wq_h = nc.dram_tensor("wq", [D, D], F32, kind="ExternalInput").ap()
    wk_h = nc.dram_tensor("wk", [D, D], F32, kind="ExternalInput").ap()
    wv_h = nc.dram_tensor("wv", [D, D], F32, kind="ExternalInput").ap()
    wo_h = nc.dram_tensor("wo", [D, D], F32, kind="ExternalInput").ap()
    wgT_h = nc.dram_tensor("wgT", [D, HID], I32, kind="ExternalInput").ap()
    wuT_h = nc.dram_tensor("wuT", [D, HID], I32, kind="ExternalInput").ap()
    wdT_h = nc.dram_tensor("wdT", [HID, D], I32, kind="ExternalInput").ap()
    cn_h = nc.dram_tensor("cn", [128, CC], F32, kind="ExternalInput").ap()
    out_h = nc.dram_tensor("out", [S, T, D], F32, kind="ExternalOutput").ap()
    sc_xnormT = nc.dram_tensor("sc_xnormT", [S, D, T], F32).ap()
    sc_xnT = nc.dram_tensor("sc_xnT", [S, D, T], F32).ap()
    sc_vp = nc.dram_tensor("sc_vp", [S, T, D], F32).ap()
    sc_k = nc.dram_tensor("sc_k", [S, D, T], F32).ap()
    sc_s = nc.dram_tensor("sc_s", [S, D, T], F32).ap()
    sc_wg = nc.dram_tensor("sc_wg", [NHC, 128, NCH * 128], F16).ap()
    sc_wu = nc.dram_tensor("sc_wu", [NHC, 128, NCH * 128], F16).ap()
    sc_wd = nc.dram_tensor("sc_wd", [NCH, 128, NHC * 128], F16).ap()

    em = Em()

    cn = stack.enter_context(nc.sbuf_tensor("cn_sb", [128, CC], F32))
    c_all = stack.enter_context(nc.sbuf_tensor("c_all", [128, 8], F32))
    ps = stack.enter_context(nc.psum_tensor("ps", [128, 4096], F32))

    def PS(b, o=0, w=512):
        return ps[:, 512 * b + o: 512 * b + o + w]

    ident = cn[:, C_IDENT:C_IDENT + 128]
    gqblk = cn[:, C_GQBLK:C_GQBLK + 2]
    gkblk = cn[:, C_GKBLK:C_GKBLK + 2]
    ones = cn[:, C_ONES:C_ONES + 1]
    tanhb = cn[:, C_TANHB:C_TANHB + 1]
    epsln = cn[:, C_EPSLN:C_EPSLN + 1]
    sel1 = cn[0:2, C_SEL1:C_SEL1 + 128]
    sel05 = cn[0:2, C_SEL05:C_SEL05 + 128]

    em.op("sync", lambda e: e.dma_start(out=cn[:], in_=cn_h[:, :]),
          writes=["cn"], dma=True, tag="cn_load")
    em.op("vector", lambda e: e.tensor_add(
        out=tanhb2[:], in0=cn[:, C_TANHB:C_TANHB + 1],
        in1=cn[:, C_TANHB2:C_TANHB2 + 1]),
        reads=["cn"], writes=["tanhb2"])

    nmu = stack.enter_context(nc.sbuf_tensor("nmu", [128, NSB], F32))
    var = stack.enter_context(nc.sbuf_tensor("var", [128, NSB], F32))
    rs = stack.enter_context(nc.sbuf_tensor("rs", [128, NSB], F32))
    sstile = stack.enter_context(nc.sbuf_tensor("sstile", [128, 64], F32))
    gdot = stack.enter_context(nc.sbuf_tensor("gdot", [128, 64], F32))
    gk_tm = stack.enter_context(nc.sbuf_tensor("gk_tm", [128, 64], F32))
    gq_tm = stack.enter_context(nc.sbuf_tensor("gq_tm", [128, 64], F32))
    n2t = stack.enter_context(nc.sbuf_tensor("n2t", [2, TGT], F32))
    tanhb2 = stack.enter_context(nc.sbuf_tensor("tanhb2", [128, 1], F32))

    def layer_norm(xin, xout, sqbuf, key_in, key_out, key_sq, affine=False):
        em.op("vector", lambda e: e.reduce_sum(out=nmu[:, 0:NSB], in_=xin[:],
                                               axis=AX.X, negate=True),
              reads=[key_in], writes=["nmu"])
        em.op("vector", lambda e: e.tensor_scalar_mul(out=nmu[:, 0:NSB],
                                                      in0=nmu[:, 0:NSB],
                                                      scalar1=1.0 / D),
              reads=["nmu"], writes=["nmu"])
        for s in range(NSB):
            em.op("scalar", lambda e, s=s: e.activation(
                out=xout[:, s, :], in_=xin[:, s, :], func=AF.Identity,
                bias=nmu[:, s:s + 1], scale=1.0),
                reads=[key_in, "nmu"], writes=[key_out])
        em.op("vector", lambda e: e.tensor_mul(out=sqbuf[:], in0=xout[:],
                                               in1=xout[:]),
              reads=[key_out], writes=[key_sq])
        em.op("vector", lambda e: e.reduce_sum(out=var[:, 0:NSB], in_=sqbuf[:],
                                               axis=AX.X),
              reads=[key_sq], writes=["var"])
        em.op("vector", lambda e: e.tensor_scalar_mul(out=var[:, 0:NSB],
                                                      in0=var[:, 0:NSB],
                                                      scalar1=1.0 / D),
              reads=["var"], writes=["var"])
        em.op("scalar", lambda e: e.activation(out=rs[:, 0:NSB],
                                               in_=var[:, 0:NSB],
                                               func=AF.Sqrt,
                                               bias=epsln, scale=1.0),
              reads=["var", "cn"], writes=["rs"])
        em.op("vector", lambda e: e.reciprocal(out=rs[:, 0:NSB],
                                               in_=rs[:, 0:NSB]),
              reads=["rs"], writes=["rs"])
        for s in range(NSB):
            em.op("scalar", lambda e, s=s: e.activation(
                out=xout[:, s, :], in_=xout[:, s, :], func=AF.Identity,
                bias=0.0, scale=rs[:, s:s + 1]),
                reads=[key_out, "rs"], writes=[key_out])

    def layer_norm2(xin, xout, key_in, key_out):
        em.op("vector", lambda e: e.reduce_sum(out=nmu[:, 0:NSB], in_=xin[:],
                                               axis=AX.X, negate=True),
              reads=[key_in], writes=["nmu"])
        em.op("vector", lambda e: e.tensor_scalar_mul(out=nmu[:, 0:NSB],
                                                      in0=nmu[:, 0:NSB],
                                                      scalar1=1.0 / D),
              reads=["nmu"], writes=["nmu"])
        em.op("vector", lambda e: e.tensor_mul(out=xout[:], in0=xin[:],
                                               in1=xin[:]),
              reads=[key_in], writes=[key_out])
        em.op("vector", lambda e: e.reduce_sum(out=var[:, 0:NSB], in_=xout[:],
                                               axis=AX.X),
              reads=[key_out], writes=["var"])
        em.op("vector", lambda e: e.tensor_mul(out=rs[:, 0:NSB],
                                               in0=nmu[:, 0:NSB],
                                               in1=nmu[:, 0:NSB]),
              reads=["nmu"], writes=["rs"])
        em.op("vector", lambda e: e.tensor_scalar_mul(out=var[:, 0:NSB],
                                                      in0=var[:, 0:NSB],
                                                      scalar1=1.0 / D),
              reads=["var"], writes=["var"])
        em.op("vector", lambda e: e.tensor_sub(out=var[:, 0:NSB],
                                               in0=var[:, 0:NSB],
                                               in1=rs[:, 0:NSB]),
              reads=["var", "rs"], writes=["var"])
        em.op("scalar", lambda e: e.activation(out=rs[:, 0:NSB],
                                               in_=var[:, 0:NSB],
                                               func=AF.Sqrt,
                                               bias=epsln, scale=1.0),
              reads=["var", "cn"], writes=["rs"])
        em.op("vector", lambda e: e.reciprocal(out=rs[:, 0:NSB],
                                               in_=rs[:, 0:NSB]),
              reads=["rs"], writes=["rs"])
        for s in range(NSB):
            em.op("scalar", lambda e, s=s: e.activation(
                out=xout[:, s, :], in_=xin[:, s, :], func=AF.Identity,
                bias=nmu[:, s:s + 1], scale=1.0),
                reads=[key_in, "nmu"], writes=[key_out])
        for s in range(NSB):
            em.op("scalar", lambda e, s=s: e.activation(
                out=xout[:, s, :], in_=xout[:, s, :], func=AF.Identity,
                bias=0.0, scale=rs[:, s:s + 1]),
                reads=[key_out, "rs"], writes=[key_out])

    def phase_a(b, bufAd, bufBd, xnT, xnormT, wk_sb, wv_sb, kT_tg,
                sq_sb, v_tg, rn2):
        em.op("vector", lambda e: e.memset(c_all[:], 0.0), writes=["c_all"])

        def ld_weights():
            em.op("sync", lambda e: e.dma_start(
                out=wk_sb[:],
                in_=wk_h[:, :].rearrange("(c p) d -> p c d", p=128)),
                  writes=["wk_sb"], dma=True, tag="wk")
            em.op("sync", lambda e: e.dma_start(
                out=wv_sb[:],
                in_=wv_h[:, :].rearrange("(c p) d -> p c d", p=128)),
                  writes=["wv_sb"], dma=True, tag="wv")

        def pre_load(g):
            gp = g % 2
            bufA = bufAd[gp]
            kA = f"bufA{gp}"
            t0 = TGT * g
            srcx = x_hbm[b, t0:t0 + TGT, :].rearrange("(s p) d -> p s d",
                                                      p=128)
            em.op("sync", lambda e, srcx=srcx, bufA=bufA: e.dma_start(
                out=bufA[:], in_=srcx),
                  writes=[kA], dma=True, tag="xload")

        def pre_ln1(g):
            gp = g % 2
            bufA, bufB = bufAd[gp], bufBd[gp]
            kA, kB = f"bufA{gp}", f"bufB{gp}"
            layer_norm(bufA, bufB, bufA, kA, kB, kA)

        def pre_ln2(g):
            gp = g % 2
            bufA, bufB = bufAd[gp], bufBd[gp]
            kA, kB = f"bufA{gp}", f"bufB{gp}"
            layer_norm2(bufB, bufA, kB, kA)

        def main1(g):
            gp = g % 2
            bufA, bufB = bufAd[gp], bufBd[gp]
            kA, kB = f"bufA{gp}", f"bufB{gp}"
            t0 = TGT * g
            for c in range(NCH):
                tb = 3 * (c % 2)
                for s in range(NSB):
                    em.op("tensor", lambda e, c=c, s=s, bufB=bufB, tb=tb:
                          e.transpose(
                        PS(tb, 128 * s, 128), bufB[:, s, 128 * c:128 * c + 128],
                        ident),
                        reads=[kB, "cn"], writes=[f"ps{tb}"])
                em.op("vector", lambda e, c=c, tb=tb: e.tensor_copy(
                    out=xnT[:, c, :], in_=PS(tb)),
                    reads=[f"ps{tb}"], writes=["xnT"])
            dst = sc_xnT[b, :, t0:t0 + TGT].rearrange("(c p) t -> p c t",
                                                      p=128)
            em.op("gpsimd", lambda e, dst=dst: e.dma_start(out=dst,
                                                           in_=xnT[:]),
                  reads=["xnT"], dma=True)
            for c in range(NCH):
                tb = 3 * (c % 2)
                for s in range(NSB):
                    em.op("tensor", lambda e, c=c, s=s, bufA=bufA, tb=tb:
                          e.transpose(
                        PS(tb, 128 * s, 128), bufA[:, s, 128 * c:128 * c + 128],
                        ident),
                        reads=[kA, "cn"], writes=[f"ps{tb}"])
                em.op("scalar", lambda e, c=c, tb=tb: e.activation(
                    out=xnormT[:, c, :], in_=PS(tb), func=AF.Identity,
                    scale=cn[:, C_GAMMA + c:C_GAMMA + c + 1],
                    bias=cn[:, C_BETA + c:C_BETA + c + 1]),
                    reads=[f"ps{tb}", "cn"], writes=["xnormT"])
            dst = sc_xnormT[b, :, t0:t0 + TGT].rearrange("(c p) t -> p c t",
                                                         p=128)
            em.op("gpsimd", lambda e, dst=dst: e.dma_start(out=dst,
                                                           in_=xnormT[:]),
                  reads=["xnormT"], dma=True)
            def chainA(j):
                em.op("scalar", lambda e, j=j: e.activation(
                    out=sq_sb[:], in_=kT_tg[:, j, :], func=AF.Square),
                    reads=["kT_tg"], writes=["sq_sb"])
                em.op("tensor", lambda e, j=j: e.matmul(
                    ps[0:2, 512 * 3:512 * 3 + TGT], cn[:, C_BLK2:C_BLK2 + 2],
                    sq_sb[:], start=True, stop=True),
                    reads=["sq_sb", "cn"], writes=["ps3"])
                em.op("scalar", lambda e, j=j: e.activation(
                    out=n2t[:, :], in_=ps[0:2, 512 * 3:512 * 3 + TGT],
                    func=AF.Sqrt),
                    reads=["ps3"], writes=["n2t"])

            def chainB(j):
                em.op("vector", lambda e: e.tensor_single_scalar(
                    out=n2t[:, :], in_=n2t[:, :], scalar=1e-12, op=ALU.max),
                    reads=["n2t"], writes=["n2t"])
                em.op("vector", lambda e, j=j: e.reciprocal(
                    out=rn2[:, :], in_=n2t[:, :]),
                    reads=["n2t"], writes=["rn2"])
                em.op("tensor", lambda e, j=j: e.matmul(
                    PS(4), sel1, rn2[:, :], start=True, stop=True),
                    reads=["rn2", "cn"], writes=["ps4"])
                em.op("vector", lambda e, j=j: e.tensor_mul(
                    out=kT_tg[:, j, :], in0=kT_tg[:, j, :],
                    in1=PS(4)),
                    reads=["ps4", "kT_tg"], writes=["kT_tg"])
                for s in range(NSB):
                    em.op("tensor", lambda e, j=j, s=s: e.matmul(
                        ps[:, 512 * 5 + 16 * s + 2 * j:
                           512 * 5 + 16 * s + 2 * j + 2],
                        kT_tg[:, j, 128 * s:128 * s + 128], gkblk,
                        start=True, stop=True),
                        reads=["kT_tg", "cn"], writes=["ps5"])

            for j in range(NCH):
                pb = 1 + (j % 2)
                for i in range(NCH):
                    em.op("tensor", lambda e, i=i, j=j, pb=pb: e.matmul(
                        PS(pb), wk_sb[:, i, 128 * j:128 * j + 128],
                        xnormT[:, i, :], start=(i == 0), stop=(i == NCH - 1)),
                        reads=["wk_sb", "xnormT"], writes=[f"ps{pb}"],
                        group=f"kp{g}_{j}", gclose=(i == NCH - 1))
                if j >= 2:
                    chainB(j - 2)
                if j >= 1:
                    chainA(j - 1)
                em.op("vector", lambda e, j=j, pb=pb: e.tensor_copy(
                    out=kT_tg[:, j, :], in_=PS(pb)),
                    reads=[f"ps{pb}"], writes=["kT_tg"])
            chainB(NCH - 2)
            chainA(NCH - 1)
            chainB(NCH - 1)
            dst = sc_k[b, :, t0:t0 + TGT].rearrange("(c p) t -> p c t",
                                                    p=128)
            em.op("gpsimd", lambda e, dst=dst: e.dma_start(out=dst,
                                                           in_=kT_tg[:]),
                  reads=["kT_tg"], dma=True, tag="kstore")
            em.op("scalar", lambda e: e.activation(
                out=gdot[:, :], in_=ps[:, 512 * 5:512 * 5 + 64],
                func=AF.Sigmoid),
                reads=["ps5"], writes=["gdot"])
            em.op("vector", lambda e: e.tensor_scalar(
                out=gk_tm[:, :], in0=gdot[:, :], scalar1=0.75, scalar2=0.25,
                op0=ALU.mult, op1=ALU.add),
                reads=["gdot"], writes=["gk_tm"])

        def main2(g, lnhook=None):
            gp = g % 2
            t0 = TGT * g
            VB = (1, 2, 6, 7)
            for s in range(NSB):
                for jh in range(2):
                    pb = VB[(2 * s + jh) % 4]
                    for i in range(NCH):
                        em.op("tensor", lambda e, i=i, s=s, jh=jh, pb=pb:
                              e.matmul(PS(pb),
                                       xnT[:, i, 128 * s:128 * s + 128],
                                       wv_sb[:, i, 512 * jh:512 * jh + 512],
                                       start=(i == 0), stop=(i == NCH - 1)),
                              reads=["wv_sb", "xnT"], writes=[f"ps{pb}"],
                              group=f"vp{g}_{s}_{jh}", gclose=(i == NCH - 1))
                    for hh in range(8):
                        h = 8 * jh + hh
                        em.op("vector", lambda e, s=s, jh=jh, hh=hh, h=h,
                              pb=pb: e.tensor_scalar_mul(
                                  out=v_tg[:, s, 512 * jh + 64 * hh:
                                           512 * jh + 64 * hh + 64],
                                  in0=PS(pb, 64 * hh, 64),
                                  scalar1=gk_tm[:, 16 * s + h:16 * s + h + 1]),
                              reads=[f"ps{pb}", "gk_tm"], writes=["v_tg"])
            if lnhook is not None:
                lnhook()
            for s in range(NSB):
                for p in range(8):
                    em.op("tensor", lambda e, s=s, p=p: e.matmul(
                        ps[:, 512 * 4 + 8 * s + p:512 * 4 + 8 * s + p + 1],
                        v_tg[:, s, 128 * p:128 * p + 128], ones,
                        start=True, stop=True),
                        reads=["v_tg", "cn"], writes=["ps4"])
            for s in range(NSB):
                em.op("vector", lambda e, s=s: e.tensor_add(
                    out=c_all[:], in0=c_all[:],
                    in1=ps[:, 512 * 4 + 8 * s:512 * 4 + 8 * s + 8]),
                    reads=["ps4", "c_all"], writes=["c_all"])
            dst = sc_vp[b, t0:t0 + TGT, :].rearrange("(s p) d -> p s d", p=128)
            em.op("gpsimd", lambda e, dst=dst: e.dma_start(out=dst,
                                                           in_=v_tg[:]),
                  reads=["v_tg"], dma=True)

        pre_load(0)
        ld_weights()
        pre_ln1(0)
        pre_ln2(0)
        for g in range(NTG):
            if g + 1 < NTG:
                pre_load(g + 1)
            main1(g)
            if g + 1 < NTG:
                pre_ln1(g + 1)
                main2(g, lnhook=lambda g=g: pre_ln2(g + 1))
            else:
                main2(g)

    def phase_b1(b, xnormT2, qstage, sqq, qnT, wq_sb, wo_sb, kc_sb, vp_sb,
                 t_sb, f1, fusedT, sT_st, gq_hm):
        em.op("sync", lambda e: e.dma_start(
            out=wq_sb[:], in_=wq_h[:, :].rearrange("(c p) d -> p c d", p=128)),
              writes=["wq_sb"], dma=True, tag="wq")
        src0 = sc_xnormT[b, :, 0:TGT].rearrange("(c p) t -> p c t", p=128)
        em.op("sync", lambda e, src=src0: e.dma_start(out=xnormT2[0][:],
                                                      in_=src),
              writes=["xn0"], dma=True)
        em.op("sync", lambda e: e.dma_start(
            out=wo_sb[:], in_=wo_h[:, :].rearrange("(c p) d -> p c d", p=128)),
              writes=["wo_sb"], dma=True, tag="wo")
        for qt in range(NTG):
            q0 = TGT * qt
            xp = qt % 2
            xnormT = xnormT2[xp]
            xkey = f"xn{xp}"
            if qt + 1 < NTG:
                srcn = sc_xnormT[b, :, q0 + TGT:q0 + 2 * TGT].rearrange(
                    "(c p) t -> p c t", p=128)
                em.op("gpsimd", lambda e, src=srcn, xq=(qt + 1) % 2:
                      e.dma_start(out=xnormT2[xq][:], in_=src),
                      writes=[f"xn{(qt + 1) % 2}"], dma=True)
            for s in range(NSB):
                for jh in range(2):
                    pb = jh
                    for i in range(NCH):
                        em.op("tensor", lambda e, i=i, s=s, jh=jh, pb=pb,
                              xnormT=xnormT:
                              e.matmul(PS(pb),
                                       xnormT[:, i, 128 * s:128 * s + 128],
                                       wq_sb[:, i, 512 * jh:512 * jh + 512],
                                       start=(i == 0), stop=(i == NCH - 1)),
                              reads=["wq_sb", xkey], writes=[f"ps{pb}"],
                              group=f"qp{qt}_{s}_{jh}", gclose=(i == NCH - 1))
                    em.op("vector", lambda e, s=s, jh=jh, pb=pb: e.tensor_copy(
                        out=qstage[:, s, 512 * jh:512 * jh + 512], in_=PS(pb)),
                        reads=[f"ps{pb}"], writes=["qf"])
                    em.op("scalar", lambda e, s=s, jh=jh: e.activation(
                        out=sqq[:, 512 * jh:512 * jh + 512],
                        in_=qstage[:, s, 512 * jh:512 * jh + 512],
                        func=AF.Square),
                        reads=["qf"], writes=["sqq"])
                em.op("vector", lambda e, s=s: e.reduce_sum(
                    out=sstile[:, 16 * s:16 * s + 16],
                    in_=sqq[:].rearrange("p (h d) -> p h d", d=DH), axis=AX.X),
                    reads=["sqq"], writes=["sstile"])
            em.op("scalar", lambda e: e.activation(out=sstile[:, :],
                                                   in_=sstile[:, :],
                                                   func=AF.Sqrt),
                  reads=["sstile"], writes=["sstile"])
            em.op("vector", lambda e: e.tensor_single_scalar(
                out=sstile[:, :], in_=sstile[:, :], scalar=1e-12, op=ALU.max),
                reads=["sstile"], writes=["sstile"])
            em.op("vector", lambda e: e.reciprocal(out=sstile[:, :],
                                                   in_=sstile[:, :]),
                  reads=["sstile"], writes=["sstile"])
            for s in range(NSB):
                em.op("vector", lambda e, s=s: e.tensor_mul(
                    out=qstage[:, s, :].rearrange("p (h d) -> p h d", d=DH),
                    in0=qstage[:, s, :].rearrange("p (h d) -> p h d", d=DH),
                    in1=sstile[:, 16 * s:16 * s + 16].unsqueeze(2)
                        .broadcast_to([128, 16, DH])),
                    reads=["qf", "sstile"], writes=["qf"])
            for c in range(NCH):
                tb = 2 + (c % 2)
                for s in range(NSB):
                    em.op("tensor", lambda e, c=c, s=s, tb=tb: e.transpose(
                        PS(tb, 128 * s, 128),
                        qstage[:, s, 128 * c:128 * c + 128], ident),
                        reads=["qf", "cn"], writes=[f"ps{tb}"])
                if c % 2 == 0:
                    em.op("scalar", lambda e, c=c, tb=tb: e.activation(
                        out=qnT[:, c, :], in_=PS(tb), func=AF.Identity),
                        reads=[f"ps{tb}"], writes=["qnT"])
                else:
                    em.op("vector", lambda e, c=c, tb=tb: e.tensor_copy(
                        out=qnT[:, c, :], in_=PS(tb)),
                        reads=[f"ps{tb}"], writes=["qnT"])
            for s in range(NSB):
                for c in range(NCH):
                    em.op("tensor", lambda e, c=c, s=s: e.matmul(
                        ps[:, 512 * 3 + 16 * s + 2 * c:
                           512 * 3 + 16 * s + 2 * c + 2],
                        qnT[:, c, 128 * s:128 * s + 128], gqblk,
                        start=True, stop=True),
                        reads=["qnT", "cn"], writes=["ps3"])
            em.op("scalar", lambda e: e.activation(
                out=gdot[:, :], in_=ps[:, 512 * 3:512 * 3 + 64],
                func=AF.Sigmoid),
                reads=["ps3"], writes=["gdot"])
            em.op("vector", lambda e: e.tensor_scalar(
                out=gq_tm[:, :], in0=gdot[:, :], scalar1=0.75, scalar2=0.25,
                op0=ALU.mult, op1=ALU.add),
                reads=["gdot"], writes=["gq_tm"])
            for c in range(NCH):
                for s in range(NSB):
                    em.op("tensor", lambda e, c=c, s=s: e.transpose(
                        ps[0:2, 512 * 3 + 128 * s:512 * 3 + 128 * s + 128],
                        gq_tm[:, 16 * s + 2 * c:16 * s + 2 * c + 2], ident),
                        reads=["gq_tm", "cn"], writes=["ps3"])
                em.op("vector", lambda e, c=c: e.tensor_copy(
                    out=gq_hm[c][:, :], in_=ps[0:2, 512 * 3:512 * 3 + TGT]),
                    reads=["ps3"], writes=[f"gq_hm{c}"])
            for c in range(NCH):
                cp = c % 2
                src = sc_vp[b, :, 128 * c:128 * c + 128].rearrange(
                    "(kt p) d -> p kt d", p=128)
                em.op("sync", lambda e, src=src, cp=cp: e.dma_start(
                    out=vp_sb[cp][:], in_=src),
                    writes=[f"vp{cp}"], dma=True, tag="vp")
                em.op("sync", lambda e, c=c, cp=cp: e.dma_start(
                    out=kc_sb[cp][:], in_=sc_k[b, 128 * c:128 * c + 128, :]),
                    writes=[f"kc{cp}"], dma=True, tag="kc")

                def align(kt):
                    alb = 2 + 2 * (kt % 2)
                    em.op("tensor", lambda e, kt=kt, alb=alb, cp=cp, c=c:
                          e.matmul(
                        PS(alb), kc_sb[cp][0:64, 128 * kt:128 * kt + 128],
                        qnT[0:64, c, :], start=True, stop=True),
                        reads=[f"kc{cp}", "qnT"], writes=[f"ps{alb}"])
                    em.op("tensor", lambda e, kt=kt, alb=alb, cp=cp, c=c:
                          e.matmul(
                        PS(alb + 1), kc_sb[cp][64:128, 128 * kt:128 * kt + 128],
                        qnT[64:128, c, :], start=True, stop=True,
                        tile_position=(64, 0)),
                        reads=[f"kc{cp}", "qnT"], writes=[f"ps{alb + 1}"])

                align(0)
                for kt in range(NKT):
                    kp = kt % 2
                    alb = 2 + 2 * kp
                    em.op("scalar", lambda e, kp=kp, alb=alb: e.activation(
                        out=t_sb[kp][:], in_=ps[:, 512 * alb:512 * alb + 1024],
                        func=AF.Tanh, scale=15.0, bias=tanhb2[:, 0:1]),
                        reads=[f"ps{alb}", f"ps{alb + 1}", "tanhb2"],
                        writes=[f"t{kp}"])
                    if kt + 1 < NKT:
                        align(kt + 1)
                    em.op("tensor", lambda e, kt=kt, kp=kp, cp=cp: e.matmul(
                        ps[0:64, 512 * 6:512 * 6 + 512],
                        vp_sb[cp][:, kt, 0:64], t_sb[kp][:, 0:512],
                        start=(kt == 0), stop=(kt == NKT - 1)),
                        reads=[f"vp{cp}", f"t{kp}"], writes=["ps6"],
                        group=f"fuA{qt}_{c}", gclose=(kt == NKT - 1))
                    em.op("tensor", lambda e, kt=kt, kp=kp, cp=cp: e.matmul(
                        ps[64:128, 512 * 7:512 * 7 + 512],
                        vp_sb[cp][:, kt, 64:128], t_sb[kp][:, 512:1024],
                        start=(kt == 0), stop=(kt == NKT - 1),
                        tile_position=(0, 64)),
                        reads=[f"vp{cp}", f"t{kp}"], writes=["ps7"],
                        group=f"fuB{qt}_{c}", gclose=(kt == NKT - 1))
                em.op("tensor", lambda e, c=c: e.matmul(
                    PS(2), sel05, gq_hm[c][:, :], start=True, stop=True),
                    reads=[f"gq_hm{c}", "cn"], writes=["ps2"])
                em.op("vector", lambda e: e.tensor_copy(
                    out=f1[:], in_=PS(2)),
                    reads=["ps2"], writes=["f1"])
                em.op("vector", lambda e, c=c: e.scalar_tensor_tensor(
                    out=fusedT[0:64, c, :], in0=ps[0:64, 512 * 6:512 * 6 + 512],
                    scalar=c_all[0:64, c:c + 1], in1=f1[0:64, :],
                    op0=ALU.add, op1=ALU.mult),
                    reads=["ps6", "f1", "c_all"], writes=["qf"])
                em.op("vector", lambda e, c=c: e.scalar_tensor_tensor(
                    out=fusedT[64:128, c, :], in0=ps[64:128, 512 * 7:512 * 7 + 512],
                    scalar=c_all[64:128, c:c + 1], in1=f1[64:128, :],
                    op0=ALU.add, op1=ALU.mult),
                    reads=["ps7", "f1", "c_all"], writes=["qf"])
            for j in range(NCH):
                pb = j % 2
                for i in range(NCH):
                    em.op("tensor", lambda e, i=i, j=j, pb=pb: e.matmul(
                        PS(pb), wo_sb[:, i, 128 * j:128 * j + 128],
                        fusedT[:, i, :], start=(i == 0), stop=(i == NCH - 1)),
                        reads=["wo_sb", "qf"], writes=[f"ps{pb}"],
                        group=f"wo{qt}_{j}", gclose=(i == NCH - 1))
                sp = j % 2
                em.op("scalar", lambda e, j=j, pb=pb, sp=sp: e.activation(
                    out=sT_st[sp][:], in_=PS(pb), func=AF.Identity,
                    bias=cn[:, C_BO + j:C_BO + j + 1]),
                    reads=[f"ps{pb}", "cn"], writes=[f"sT{sp}"])
                dst = sc_s[b, 128 * j:128 * j + 128, q0:q0 + TGT]
                em.op("gpsimd", lambda e, dst=dst, sp=sp: e.dma_start(
                    out=dst, in_=sT_st[sp][:]),
                    reads=[f"sT{sp}"], dma=True)

    def phase_b2(b, xnT2d, xh, xl, wg_sb, wu_sb, wd_sb, hf, sg_sb, hh, hl,
                 g_sb, bufx, s_sb, prod, out_sb):
        NHC_l = NHC
        src0 = sc_xnT[b, :, 0:TGT].rearrange("(c p) t -> p c t", p=128)
        em.op("gpsimd", lambda e, src=src0: e.dma_start(out=xnT2d[0][:],
                                                        in_=src),
              writes=["xt0"], dma=True)

        def split(qt):
            xp = qt % 2
            xnT2 = xnT2d[xp]
            xtkey = f"xt{xp}"
            for c in range(NCH):
                em.op("vector", lambda e, c=c, xnT2=xnT2: e.tensor_copy(
                    out=xh[:, c, :], in_=xnT2[:, c, :]),
                    reads=[xtkey], writes=["xh"])
                em.op("vector", lambda e, c=c, xnT2=xnT2: e.tensor_sub(
                    out=xl[:, c, :], in0=xnT2[:, c, :], in1=xh[:, c, :]),
                    reads=[xtkey, "xh"], writes=["xl"])

        split(0)
        for qt in range(NTG):
            q0 = TGT * qt
            src = x_hbm[b, q0:q0 + TGT, :].rearrange("(s p) d -> p s d", p=128)
            em.op("gpsimd", lambda e, src=src: e.dma_start(out=bufx[:],
                                                           in_=src),
                  writes=["bufx"], dma=True)
            src = sc_s[b, :, q0:q0 + TGT].rearrange("(c p) t -> p c t", p=128)
            em.op("gpsimd", lambda e, src=src: e.dma_start(out=s_sb[:],
                                                           in_=src),
                  writes=["s_sb"], dma=True)
            if qt + 1 < NTG:
                srcn = sc_xnT[b, :, q0 + TGT:q0 + 2 * TGT].rearrange(
                    "(c p) t -> p c t", p=128)
                em.op("gpsimd", lambda e, src=srcn, xq=(qt + 1) % 2:
                      e.dma_start(out=xnT2d[xq][:], in_=src),
                      writes=[f"xt{(qt + 1) % 2}"], dma=True)
            for hc in range(NHC_l):
                wp = hc % 2
                em.op("sync", lambda e, hc=hc, wp=wp: e.dma_start(
                    out=wg_sb[wp][:],
                    in_=sc_wg[hc, :, :].rearrange("p (c h) -> p c h", h=128)),
                    writes=[f"wg{wp}"], dma=True, tag="wg")
                em.op("sync", lambda e, hc=hc, wp=wp: e.dma_start(
                    out=wu_sb[wp][:],
                    in_=sc_wu[hc, :, :].rearrange("p (c h) -> p c h", h=128)),
                    writes=[f"wu{wp}"], dma=True, tag="wu")

                for i in range(NCH):
                    em.op("tensor", lambda e, i=i, wp=wp: e.matmul(
                        PS(0), wg_sb[wp][:, i, :], xh[:, i, :],
                        start=(i == 0), stop=False),
                        reads=[f"wg{wp}", "xh"], writes=["ps0"],
                        group=f"ga{qt}_{hc}")
                    em.op("tensor", lambda e, i=i, wp=wp: e.matmul(
                        PS(0), wg_sb[wp][:, i, :], xl[:, i, :],
                        start=False, stop=(i == NCH - 1)),
                        reads=[f"wg{wp}", "xl"], writes=["ps0"],
                        group=f"ga{qt}_{hc}", gclose=(i == NCH - 1))
                for i in range(NCH):
                    em.op("tensor", lambda e, i=i, wp=wp: e.matmul(
                        PS(1), wu_sb[wp][:, i, :], xh[:, i, :],
                        start=(i == 0), stop=False),
                        reads=[f"wu{wp}", "xh"], writes=["ps1"],
                        group=f"up{qt}_{hc}")
                    em.op("tensor", lambda e, i=i, wp=wp: e.matmul(
                        PS(1), wu_sb[wp][:, i, :], xl[:, i, :],
                        start=False, stop=(i == NCH - 1)),
                        reads=[f"wu{wp}", "xl"], writes=["ps1"],
                        group=f"up{qt}_{hc}", gclose=(i == NCH - 1))
                em.op("scalar", lambda e: e.activation(
                    out=sg_sb[:], in_=PS(0), func=AF.Silu, scale=S127),
                    reads=["ps0"], writes=["sg_sb"])
                em.op("vector", lambda e: e.scalar_tensor_tensor(
                    out=hf[:], in0=PS(1), scalar=S127, in1=sg_sb[:],
                    op0=ALU.mult, op1=ALU.mult),
                    reads=["ps1", "sg_sb"], writes=["hf"])
                em.op("vector", lambda e, hc=hc: e.tensor_copy(
                    out=hh[:, hc, :], in_=hf[:]),
                    reads=["hf"], writes=["hh"])
                em.op("vector", lambda e, hc=hc: e.tensor_sub(
                    out=hl[:, hc, :], in0=hf[:], in1=hh[:, hc, :]),
                    reads=["hf", "hh"], writes=["hl"])
            if qt + 1 < NTG:
                split(qt + 1)
            for j in range(NCH):
                pb = 2 + (j % 2)
                wp = j % 2
                em.op("sync", lambda e, j=j, wp=wp: e.dma_start(
                    out=wd_sb[wp][:],
                    in_=sc_wd[j, :, :].rearrange("p (c d) -> p c d", d=128)),
                    writes=[f"wd{wp}"], dma=True, tag="wd")
                for hc in range(NHC_l):
                    em.op("tensor", lambda e, hc=hc, j=j, pb=pb, wp=wp:
                          e.matmul(
                        PS(pb), wd_sb[wp][:, hc, :],
                        hh[:, hc, :], start=(hc == 0), stop=False),
                        reads=[f"wd{wp}", "hh"], writes=[f"ps{pb}"],
                        group=f"dn{qt}_{j}")
                    em.op("tensor", lambda e, hc=hc, j=j, pb=pb, wp=wp:
                          e.matmul(
                        PS(pb), wd_sb[wp][:, hc, :],
                        hl[:, hc, :], start=False, stop=(hc == NHC_l - 1)),
                        reads=[f"wd{wp}", "hl"], writes=[f"ps{pb}"],
                        group=f"dn{qt}_{j}", gclose=(hc == NHC_l - 1))
                em.op("scalar", lambda e, j=j, pb=pb: e.activation(
                    out=g_sb[:, j, :], in_=PS(pb), func=AF.Silu, scale=S127),
                    reads=[f"ps{pb}"], writes=["g_sb"])
            for j in range(NCH):
                em.op("vector", lambda e, j=j: e.tensor_mul(
                    out=s_sb[:, j, :], in0=s_sb[:, j, :], in1=g_sb[:, j, :]),
                    reads=["g_sb", "s_sb"], writes=["s_sb"])
            for j in range(NCH):
                tb = (4, 5, 6, 7)[j % 4]
                for s in range(NSB):
                    em.op("tensor", lambda e, j=j, s=s, tb=tb: e.transpose(
                        PS(tb, 128 * s, 128), s_sb[:, j, 128 * s:128 * s + 128],
                        ident),
                        reads=["s_sb", "cn"], writes=[f"ps{tb}"])
                for s in range(NSB):
                    em.op("vector", lambda e, j=j, s=s, tb=tb: e.tensor_add(
                        out=out_sb[:, s, 128 * j:128 * j + 128],
                        in0=PS(tb, 128 * s, 128),
                        in1=bufx[:, s, 128 * j:128 * j + 128]),
                        reads=[f"ps{tb}", "bufx"], writes=["out_sb"])
            dst = out_h[b, q0:q0 + TGT, :].rearrange("(s p) d -> p s d", p=128)
            em.op("gpsimd", lambda e, dst=dst: e.dma_start(out=dst,
                                                           in_=out_sb[:]),
                  reads=["out_sb"], dma=True, tag="out")

    with contextlib.ExitStack() as pp:
        wbuf = [pp.enter_context(nc.sbuf_tensor(f"wrep{i}", [128, HID], F16))
                for i in range(2)]
        for c in range(NCH):
            wp = c % 2
            for (wsrc, wdst, tag) in ((wgT_h, sc_wg, "rg"), (wuT_h, sc_wu, "ru")):
                em.op("gpsimd", lambda e, c=c, wp=wp, wsrc=wsrc: e.dma_start(
                    out=wbuf[wp][:],
                    in_=wsrc[128 * c:128 * c + 128, :]),
                    writes=[f"wrep{wp}"], dma=True, tag=tag)
                dst = wdst[:, :, 128 * c:128 * c + 128].rearrange(
                    "h p j -> p h j")
                em.op("gpsimd", lambda e, dst=dst, wp=wp: e.dma_start(
                    out=dst, in_=wbuf[wp][:].rearrange("p (h j) -> p h j",
                                                       j=128)),
                    reads=[f"wrep{wp}"], dma=True, tag=tag)
        wdbuf = [pp.enter_context(nc.sbuf_tensor(f"wdrep{i}", [128, D], F16))
                 for i in range(2)]
        for c in range(NHC):
            wp = c % 2
            em.op("gpsimd", lambda e, c=c, wp=wp: e.dma_start(
                out=wdbuf[wp][:],
                in_=wdT_h[128 * c:128 * c + 128, :]),
                writes=[f"wdrep{wp}"], dma=True, tag="rd")
            dst = sc_wd[:, :, 128 * c:128 * c + 128].rearrange(
                "j p d -> p j d")
            em.op("gpsimd", lambda e, dst=dst, wp=wp: e.dma_start(
                out=dst, in_=wdbuf[wp][:].rearrange("p (j d) -> p j d",
                                                    d=128)),
                reads=[f"wdrep{wp}"], dma=True, tag="rd")
    em.barrier()
    for b in range(S):
        if True:
            with contextlib.ExitStack() as pa:
                bufAd = [pa.enter_context(
                    nc.sbuf_tensor(f"bufA{i}_{b}", [128, NSB, D], F32))
                    for i in range(2)]
                bufBd = [pa.enter_context(
                    nc.sbuf_tensor(f"bufB{i}_{b}", [128, NSB, D], F32))
                    for i in range(2)]
                xnT = pa.enter_context(nc.sbuf_tensor(f"xnT_{b}", [128, NCH, TGT],
                                                      F32))
                xnormT = pa.enter_context(
                    nc.sbuf_tensor(f"xnormT_{b}", [128, NCH, TGT], F32))
                wk_sb = pa.enter_context(
                    nc.sbuf_tensor(f"wk_sb_{b}", [128, NCH, D], F32))
                wv_sb = pa.enter_context(
                    nc.sbuf_tensor(f"wv_sb_{b}", [128, NCH, D], F32))
                kT_tg = pa.enter_context(
                    nc.sbuf_tensor(f"kT_tg_{b}", [128, NCH, TGT], F32))
                sq_sb = pa.enter_context(nc.sbuf_tensor(f"sq_sb_{b}", [128, TGT],
                                                        F32))
                v_tg = pa.enter_context(nc.sbuf_tensor(f"v_tg_{b}", [128, NSB, D],
                                                       F32))
                rn2 = pa.enter_context(nc.sbuf_tensor(f"rn2_{b}", [2, TGT],
                                                      F32))
                if phases[0]:
                    phase_a(b, bufAd, bufBd, xnT, xnormT, wk_sb, wv_sb,
                            kT_tg, sq_sb, v_tg, rn2)
            em.barrier()
            with contextlib.ExitStack() as pb:
                xnormT2 = [pb.enter_context(
                    nc.sbuf_tensor(f"xn{i}_{b}", [128, NCH, TGT], F32))
                    for i in range(2)]
                qf = pb.enter_context(
                    nc.sbuf_tensor(f"qf_{b}", [128, NSB * D], F32))
                qstage = qf[:].rearrange("p (s c) -> p s c", s=NSB)
                sqq = pb.enter_context(nc.sbuf_tensor(f"sqq_{b}", [128, D], F32))
                qnT = pb.enter_context(nc.sbuf_tensor(f"qnT_{b}", [128, NCH, TGT],
                                                      F32))
                wq_sb = pb.enter_context(
                    nc.sbuf_tensor(f"wq_sb_{b}", [128, NCH, D], F32))
                wo_sb = pb.enter_context(
                    nc.sbuf_tensor(f"wo_sb_{b}", [128, NCH, D], F32))
                kc_sb = [pb.enter_context(
                    nc.sbuf_tensor(f"kc{i}_{b}", [128, T], F32))
                    for i in range(2)]
                vp_sb = [pb.enter_context(
                    nc.sbuf_tensor(f"vp{i}_{b}", [128, NKT, 128], F32))
                    for i in range(2)]
                t_sb = [pb.enter_context(
                    nc.sbuf_tensor(f"t{i}_{b}", [128, 1024], F32))
                    for i in range(2)]
                f1 = pb.enter_context(nc.sbuf_tensor(f"f1_{b}", [128, TGT], F32))
                fusedT = qf[:].rearrange("p (c t) -> p c t", c=NCH)
                sT_st = [pb.enter_context(
                    nc.sbuf_tensor(f"sT{i}_{b}", [128, TGT], F32))
                    for i in range(2)]
                gq_hm = [pb.enter_context(
                    nc.sbuf_tensor(f"gq_hm{c}_{b}", [2, TGT], F32))
                    for c in range(NCH)]
                if phases[1]:
                    phase_b1(b, xnormT2, qstage, sqq, qnT, wq_sb, wo_sb, kc_sb,
                             vp_sb, t_sb, f1, fusedT, sT_st, gq_hm)
            em.barrier()
        with contextlib.ExitStack() as pc:
            xnT2d = [pc.enter_context(
                nc.sbuf_tensor(f"xt{i}_{b}", [128, NCH, TGT], F32))
                for i in range(2)]
            out_sb = pc.enter_context(
                nc.sbuf_tensor(f"out_sb_{b}", [128, NSB, D], F32))
            xh = pc.enter_context(nc.sbuf_tensor(f"xh_b2_{b}", [128, NCH, TGT],
                                                 F16))
            xl = pc.enter_context(nc.sbuf_tensor(f"xl_b2_{b}", [128, NCH, TGT],
                                                 F16))
            wg_sb = [pc.enter_context(
                nc.sbuf_tensor(f"wg{i}_{b}", [128, NCH, 128], F16))
                for i in range(2)]
            wu_sb = [pc.enter_context(
                nc.sbuf_tensor(f"wu{i}_{b}", [128, NCH, 128], F16))
                for i in range(2)]
            wd_sb = [pc.enter_context(
                nc.sbuf_tensor(f"wd{i}_{b}", [128, NHC, 128], F16))
                for i in range(2)]
            hf = pc.enter_context(nc.sbuf_tensor(f"hf_{b}", [128, TGT], F32))
            sg_sb = pc.enter_context(nc.sbuf_tensor(f"sg_sb_{b}", [128, TGT], F32))
            hh = pc.enter_context(nc.sbuf_tensor(f"hh_{b}", [128, NHC, TGT], F16))
            hl = pc.enter_context(nc.sbuf_tensor(f"hl_{b}", [128, NHC, TGT], F16))
            g_sb = pc.enter_context(nc.sbuf_tensor(f"g_sb_{b}", [128, NCH, TGT],
                                                   F32))
            bufx = pc.enter_context(nc.sbuf_tensor(f"bufx_{b}", [128, NSB, D],
                                                   F32))
            s_sb = pc.enter_context(nc.sbuf_tensor(f"s_sb_{b}", [128, NCH, TGT],
                                                   F32))
            if phases[2]:
                phase_b2(b, xnT2d, xh, xl, wg_sb, wu_sb, wd_sb, hf, sg_sb, hh,
                         hl, g_sb, bufx, s_sb, hf, out_sb)
        em.barrier()

    em.finalize(nc, stack)
    return nc


_PROG_CACHE = {}


def _get_program(S, T, HID):
    key = (S, T, HID)
    if key not in _PROG_CACHE:
        _PROG_CACHE[key] = build_program(S, T, HID)
    return _PROG_CACHE[key]


def kernel(L0_attn_x, L0_attn_w_q, L0_attn_w_k, L0_attn_w_v, L0_attn_g_q,
           L0_attn_g_k, L0_attn_w_o, L0_attn_b_o, L0_attn_gamma, L0_attn_beta,
           L0_ffn_w_gate, L0_ffn_w_up, L0_ffn_w_down):
    x = np.ascontiguousarray(np.asarray(L0_attn_x, dtype=np.float32))
    B, T, D_ = x.shape
    assert D_ == D
    n_cores = 8
    S = B // n_cores
    HID = L0_ffn_w_gate.shape[0]
    cn = _mk_consts(np.asarray(L0_attn_b_o, np.float32),
                    np.asarray(L0_attn_gamma, np.float32),
                    np.asarray(L0_attn_beta, np.float32),
                    np.asarray(L0_attn_g_q, np.float32),
                    np.asarray(L0_attn_g_k, np.float32))
    wgT = np.ascontiguousarray(np.asarray(L0_ffn_w_gate, np.int32).T)
    wuT = np.ascontiguousarray(np.asarray(L0_ffn_w_up, np.int32).T)
    wdT = np.ascontiguousarray(np.asarray(L0_ffn_w_down, np.int32).T)
    base = {
        "wq": np.ascontiguousarray(np.asarray(L0_attn_w_q, np.float32)),
        "wk": np.ascontiguousarray(np.asarray(L0_attn_w_k, np.float32)),
        "wv": np.ascontiguousarray(np.asarray(L0_attn_w_v, np.float32)),
        "wo": np.ascontiguousarray(np.asarray(L0_attn_w_o, np.float32)),
        "wgT": wgT, "wuT": wuT, "wdT": wdT, "cn": cn,
    }
    in_maps = [dict(base, x=np.ascontiguousarray(x[S * i:S * (i + 1)]))
               for i in range(n_cores)]
    nc = _get_program(S, T, HID)
    res = run_bass_kernel_spmd(nc, in_maps, core_ids=list(range(n_cores)))
    return np.concatenate([r["out"] for r in res.results], axis=0)


# version 28 (final)
# speedup vs baseline: 1.1165x; 1.0233x over previous
import contextlib
import numpy as np
import concourse.bass as bass
import concourse.mybir as mybir
from concourse.bass_utils import run_bass_kernel_spmd

F32 = mybir.dt.float32
BF16 = mybir.dt.bfloat16
F16 = mybir.dt.float16
I32 = mybir.dt.int32
AF = mybir.ActivationFunctionType
ALU = mybir.AluOpType
AX = mybir.AxisListType

ENGS = ("sync", "scalar", "vector", "tensor", "gpsimd")
IN_ORDER = {"tensor"}

D = 1024
H = 16
DH = 64
NCH = 8
TGT = 512
NSB = TGT // 128
S127 = 1.0 / 127.0


NPOOL = 12
DMA_QUEUES = ("sync", "gpsimd")
OPCAP = None


class _Op:
    __slots__ = ("eng", "fn", "dma", "idx", "need_inc", "deps", "incs", "post",
                 "waits", "tag", "semk", "slot", "defer", "gkey")


class Em:

    def __init__(self):
        self.ops = []
        self.last_w = {}
        self.last_r = {}
        self.pending_barrier = None
        self.barrier_seen = set()
        self.dma_count = {q: 0 for q in DMA_QUEUES}
        self.slot_last = {q: {} for q in DMA_QUEUES}
        self.open_groups = {}
        self.in_open = set()
        self.last_closed_tensor = None

    def _dep(self, d, gkey=None):
        if d.idx in self.in_open:
            if gkey is not None and d.gkey == gkey:
                return None
            sub = self.last_closed_tensor
            assert sub is not None and sub.idx > d.idx, (
                "open-group dep with no later observable PE op")
            return sub.idx
        return d.idx

    def op(self, eng, fn, reads=(), writes=(), dma=False, tag="", group=None,
           gclose=False):
        if OPCAP is not None and len(self.ops) >= OPCAP:
            d = _Op(); d.idx = -1; d.defer = None; d.gkey = None
            return d
        o = _Op()
        o.eng = eng
        o.fn = fn
        o.dma = dma
        o.tag = tag
        o.idx = len(self.ops)
        o.need_inc = dma
        o.slot = None
        o.defer = None
        o.gkey = group
        if eng == "tensor":
            if group is not None:
                chain = self.open_groups.setdefault(group, [])
                chain.append(o)
                if gclose:
                    for m in chain[:-1]:
                        m.defer = o.idx
                        self.in_open.discard(m.idx)
                    del self.open_groups[group]
                    self.last_closed_tensor = o
                else:
                    self.in_open.add(o.idx)
            else:
                self.last_closed_tensor = o
        deps = set()
        if self.pending_barrier is not None and eng not in self.barrier_seen:
            deps.update(self.pending_barrier)
            self.barrier_seen.add(eng)
        for r in reads:
            w = self.last_w.get(r)
            if w is not None:
                deps.add(self._dep(w, group))
        for r in writes:
            w = self.last_w.get(r)
            if w is not None:
                deps.add(self._dep(w, group))
            for rd in self.last_r.get(r, {}).values():
                deps.add(self._dep(rd, group))
        if dma:
            n = self.dma_count[eng]
            self.dma_count[eng] = n + 1
            o.slot = n % NPOOL
            prev = self.slot_last[eng].get(o.slot)
            if prev is not None:
                deps.add(prev.idx)
            self.slot_last[eng][o.slot] = o
        deps.discard(None)
        deps.discard(o.idx)
        o.deps = sorted(d for d in deps if d < o.idx)
        self.ops.append(o)
        for r in writes:
            self.last_w[r] = o
            self.last_r[r] = {}
        for r in reads:
            self.last_r.setdefault(r, {})[eng] = o
        return o

    def barrier(self):
        targets = []
        last_by_eng = {}
        for o in self.ops:
            last_by_eng[o.eng] = o
        for e, lo in last_by_eng.items():
            if e in DMA_QUEUES:
                targets.extend(p.idx for p in self.slot_last[e].values())
            else:
                targets.append(lo.idx)
        self.pending_barrier = sorted(set(targets))
        self.barrier_seen = set()

    def finalize(self, nc, stack):
        assert not self.open_groups, f"unclosed groups: {list(self.open_groups)}"

        def fin(idx):
            while self.ops[idx].defer is not None:
                idx = self.ops[idx].defer
            return idx

        for o in self.ops:
            o.deps = sorted({fin(d) for d in o.deps})
        for o in self.ops:
            for di in o.deps:
                d = self.ops[di]
                if d.eng == o.eng and o.eng in IN_ORDER:
                    continue
                d.need_inc = True
        semk_all = [("eng", e) for e in ENGS]
        for q in DMA_QUEUES:
            semk_all += [("slot", q, i) for i in range(NPOOL)]
        sems = {k: stack.enter_context(
            nc.semaphore(("sem_%s" % k[1]) if k[0] == "eng"
                         else "sem_%s_s%d" % (k[1], k[2])))
            for k in semk_all}
        cnt = {k: 0 for k in semk_all}
        for o in self.ops:
            o.semk = ("slot", o.eng, o.slot) if o.dma else ("eng", o.eng)
            o.incs = (16 if o.dma else 1) if o.need_inc else 0
            cnt[o.semk] += o.incs
            o.post = cnt[o.semk]
        per_eng = {e: [] for e in ENGS}
        for o in self.ops:
            per_eng[o.eng].append(o)
        for e in ENGS:
            hi = {}
            for o in per_eng[e]:
                need = {}
                for di in o.deps:
                    d = self.ops[di]
                    if d.eng == e and e in IN_ORDER:
                        continue
                    assert d.need_inc and d.post > 0
                    if d.post > hi.get(d.semk, 0):
                        need[d.semk] = max(need.get(d.semk, 0), d.post)
                o.waits = [(sems[k], v) for k, v in need.items()]
                if e != "tensor":
                    for k, v in need.items():
                        hi[k] = v
        block = stack.enter_context(nc.Block())

        def mk(e, items):
            def body(eng):
                for o in items:
                    for (sem, val) in o.waits:
                        eng.wait_ge(sem, val)
                    inst = o.fn(eng)
                    if o.incs:
                        if inst is None:
                            inst = eng.nop(nofuse=True)
                        inst.then_inc(sems[o.semk], o.incs)
                for k in semk_all:
                    if (k[1] == e) and cnt[k] > 0:
                        eng.wait_ge(sems[k], cnt[k])
            return body

        for e in ENGS:
            if per_eng[e]:
                getattr(block, e)(mk(e, per_eng[e]))
        return sems, cnt


C_IDENT = 0
C_GQBLK = 128
C_GKBLK = 130
C_BLK2 = 132
C_ONES = 134
C_TANHB = 135
C_EPSLN = 136
C_TANHB2 = 137
C_BO = 138
C_GAMMA = 146
C_BETA = 154
C_SEL1 = 162
C_SEL05 = 290
CC = 512


def _mk_consts(b_o, gamma, beta, g_q, g_k):
    cn = np.zeros((128, CC), np.float32)
    cn[:, C_IDENT:C_IDENT + 128] = np.eye(128, dtype=np.float32)
    cn[0:64, C_GQBLK] = g_q
    cn[64:128, C_GQBLK + 1] = g_q
    cn[0:64, C_GKBLK] = g_k
    cn[64:128, C_GKBLK + 1] = g_k
    cn[0:64, C_BLK2] = 1.0
    cn[64:128, C_BLK2 + 1] = 1.0
    cn[:, C_ONES] = 1.0
    cn[:, C_TANHB] = -4.05
    b_exact = -(15.0 * 0.27)
    cn[:, C_TANHB2] = np.float32(b_exact - np.float64(np.float32(-4.05)))
    cn[:, C_EPSLN] = 1e-5
    cn[:, C_BO:C_BO + 8] = b_o.reshape(8, 128).T
    cn[:, C_GAMMA:C_GAMMA + 8] = gamma.reshape(8, 128).T
    cn[:, C_BETA:C_BETA + 8] = beta.reshape(8, 128).T
    cn[0, C_SEL1:C_SEL1 + 64] = 1.0
    cn[1, C_SEL1 + 64:C_SEL1 + 128] = 1.0
    cn[0, C_SEL05:C_SEL05 + 64] = 0.5
    cn[1, C_SEL05 + 64:C_SEL05 + 128] = 0.5
    return cn


def build_program(S, T, HID, phases=(1, 1, 1)):
    NTG = T // TGT
    NKT = T // 128
    NHC = HID // 128
    nc = bass.Bass("TRN2", target_bir_lowering=False)
    stack = contextlib.ExitStack()
    nc._keepalive_stack = stack

    x_hbm = nc.dram_tensor("x", [S, T, D], F32, kind="ExternalInput").ap()
    wq_h = nc.dram_tensor("wq", [D, D], F32, kind="ExternalInput").ap()
    wk_h = nc.dram_tensor("wk", [D, D], F32, kind="ExternalInput").ap()
    wv_h = nc.dram_tensor("wv", [D, D], F32, kind="ExternalInput").ap()
    wo_h = nc.dram_tensor("wo", [D, D], F32, kind="ExternalInput").ap()
    wgT_h = nc.dram_tensor("wgT", [D, HID], I32, kind="ExternalInput").ap()
    wuT_h = nc.dram_tensor("wuT", [D, HID], I32, kind="ExternalInput").ap()
    wdT_h = nc.dram_tensor("wdT", [HID, D], I32, kind="ExternalInput").ap()
    cn_h = nc.dram_tensor("cn", [128, CC], F32, kind="ExternalInput").ap()
    out_h = nc.dram_tensor("out", [S, T, D], F32, kind="ExternalOutput").ap()
    sc_xnormT = nc.dram_tensor("sc_xnormT", [S, D, T], F32).ap()
    sc_xnT = nc.dram_tensor("sc_xnT", [S, D, T], F32).ap()
    sc_vp = nc.dram_tensor("sc_vp", [S, T, D], F32).ap()
    sc_k = nc.dram_tensor("sc_k", [S, D, T], F32).ap()
    sc_s = nc.dram_tensor("sc_s", [S, D, T], F32).ap()
    sc_wg = nc.dram_tensor("sc_wg", [NHC, 128, NCH * 128], F16).ap()
    sc_wu = nc.dram_tensor("sc_wu", [NHC, 128, NCH * 128], F16).ap()
    sc_wd = nc.dram_tensor("sc_wd", [NCH, 128, NHC * 128], F16).ap()

    em = Em()

    cn = stack.enter_context(nc.sbuf_tensor("cn_sb", [128, CC], F32))
    c_all = stack.enter_context(nc.sbuf_tensor("c_all", [128, 8], F32))
    ps = stack.enter_context(nc.psum_tensor("ps", [128, 4096], F32))

    def PS(b, o=0, w=512):
        return ps[:, 512 * b + o: 512 * b + o + w]

    def warm(n, bank):
        for i in range(n):
            em.op("tensor", lambda e, bank=bank: e.matmul(
                PS(bank, 0, 128), ident, ident, start=True, stop=True),
                reads=["cn"], writes=[f"ps{bank}"], tag="warm")

    ident = cn[:, C_IDENT:C_IDENT + 128]
    gqblk = cn[:, C_GQBLK:C_GQBLK + 2]
    gkblk = cn[:, C_GKBLK:C_GKBLK + 2]
    ones = cn[:, C_ONES:C_ONES + 1]
    tanhb = cn[:, C_TANHB:C_TANHB + 1]
    epsln = cn[:, C_EPSLN:C_EPSLN + 1]
    sel1 = cn[0:2, C_SEL1:C_SEL1 + 128]
    sel05 = cn[0:2, C_SEL05:C_SEL05 + 128]

    em.op("sync", lambda e: e.dma_start(out=cn[:], in_=cn_h[:, :]),
          writes=["cn"], dma=True, tag="cn_load")
    em.op("vector", lambda e: e.tensor_add(
        out=tanhb2[:], in0=cn[:, C_TANHB:C_TANHB + 1],
        in1=cn[:, C_TANHB2:C_TANHB2 + 1]),
        reads=["cn"], writes=["tanhb2"])

    nmu = stack.enter_context(nc.sbuf_tensor("nmu", [128, NSB], F32))
    var = stack.enter_context(nc.sbuf_tensor("var", [128, NSB], F32))
    rs = stack.enter_context(nc.sbuf_tensor("rs", [128, NSB], F32))
    sstile = stack.enter_context(nc.sbuf_tensor("sstile", [128, 64], F32))
    gdot = stack.enter_context(nc.sbuf_tensor("gdot", [128, 64], F32))
    gk_tm = stack.enter_context(nc.sbuf_tensor("gk_tm", [128, 64], F32))
    gq_tm = stack.enter_context(nc.sbuf_tensor("gq_tm", [128, 64], F32))
    n2t = stack.enter_context(nc.sbuf_tensor("n2t", [2, 2, TGT], F32))
    tanhb2 = stack.enter_context(nc.sbuf_tensor("tanhb2", [128, 1], F32))

    def layer_norm(xin, xout, sqbuf, key_in, key_out, key_sq, affine=False):
        em.op("vector", lambda e: e.reduce_sum(out=nmu[:, 0:NSB], in_=xin[:],
                                               axis=AX.X, negate=True),
              reads=[key_in], writes=["nmu"])
        em.op("vector", lambda e: e.tensor_scalar_mul(out=nmu[:, 0:NSB],
                                                      in0=nmu[:, 0:NSB],
                                                      scalar1=1.0 / D),
              reads=["nmu"], writes=["nmu"])
        for s in range(NSB):
            em.op("scalar", lambda e, s=s: e.activation(
                out=xout[:, s, :], in_=xin[:, s, :], func=AF.Identity,
                bias=nmu[:, s:s + 1], scale=1.0),
                reads=[key_in, "nmu"], writes=[key_out])
        em.op("vector", lambda e: e.tensor_mul(out=sqbuf[:], in0=xout[:],
                                               in1=xout[:]),
              reads=[key_out], writes=[key_sq])
        em.op("vector", lambda e: e.reduce_sum(out=var[:, 0:NSB], in_=sqbuf[:],
                                               axis=AX.X),
              reads=[key_sq], writes=["var"])
        em.op("vector", lambda e: e.tensor_scalar_mul(out=var[:, 0:NSB],
                                                      in0=var[:, 0:NSB],
                                                      scalar1=1.0 / D),
              reads=["var"], writes=["var"])
        em.op("scalar", lambda e: e.activation(out=rs[:, 0:NSB],
                                               in_=var[:, 0:NSB],
                                               func=AF.Sqrt,
                                               bias=epsln, scale=1.0),
              reads=["var", "cn"], writes=["rs"])
        em.op("vector", lambda e: e.reciprocal(out=rs[:, 0:NSB],
                                               in_=rs[:, 0:NSB]),
              reads=["rs"], writes=["rs"])
        for s in range(NSB):
            em.op("scalar", lambda e, s=s: e.activation(
                out=xout[:, s, :], in_=xout[:, s, :], func=AF.Identity,
                bias=0.0, scale=rs[:, s:s + 1]),
                reads=[key_out, "rs"], writes=[key_out])

    def layer_norm2(xin, xout, key_in, key_out):
        em.op("vector", lambda e: e.reduce_sum(out=nmu[:, 0:NSB], in_=xin[:],
                                               axis=AX.X, negate=True),
              reads=[key_in], writes=["nmu"])
        em.op("vector", lambda e: e.tensor_scalar_mul(out=nmu[:, 0:NSB],
                                                      in0=nmu[:, 0:NSB],
                                                      scalar1=1.0 / D),
              reads=["nmu"], writes=["nmu"])
        em.op("vector", lambda e: e.tensor_mul(out=xout[:], in0=xin[:],
                                               in1=xin[:]),
              reads=[key_in], writes=[key_out])
        em.op("vector", lambda e: e.reduce_sum(out=var[:, 0:NSB], in_=xout[:],
                                               axis=AX.X),
              reads=[key_out], writes=["var"])
        em.op("vector", lambda e: e.tensor_mul(out=rs[:, 0:NSB],
                                               in0=nmu[:, 0:NSB],
                                               in1=nmu[:, 0:NSB]),
              reads=["nmu"], writes=["rs"])
        em.op("vector", lambda e: e.tensor_scalar_mul(out=var[:, 0:NSB],
                                                      in0=var[:, 0:NSB],
                                                      scalar1=1.0 / D),
              reads=["var"], writes=["var"])
        em.op("vector", lambda e: e.tensor_sub(out=var[:, 0:NSB],
                                               in0=var[:, 0:NSB],
                                               in1=rs[:, 0:NSB]),
              reads=["var", "rs"], writes=["var"])
        em.op("scalar", lambda e: e.activation(out=rs[:, 0:NSB],
                                               in_=var[:, 0:NSB],
                                               func=AF.Sqrt,
                                               bias=epsln, scale=1.0),
              reads=["var", "cn"], writes=["rs"])
        em.op("vector", lambda e: e.reciprocal(out=rs[:, 0:NSB],
                                               in_=rs[:, 0:NSB]),
              reads=["rs"], writes=["rs"])
        for s in range(NSB):
            em.op("scalar", lambda e, s=s: e.activation(
                out=xout[:, s, :], in_=xin[:, s, :], func=AF.Identity,
                bias=nmu[:, s:s + 1], scale=1.0),
                reads=[key_in, "nmu"], writes=[key_out])
        for s in range(NSB):
            em.op("scalar", lambda e, s=s: e.activation(
                out=xout[:, s, :], in_=xout[:, s, :], func=AF.Identity,
                bias=0.0, scale=rs[:, s:s + 1]),
                reads=[key_out, "rs"], writes=[key_out])

    def phase_a(b, bufAd, bufBd, xnT, xnormT, wk_sb, wv_sb, kT_tg,
                sq_sb, v_tg, rn2):
        em.op("vector", lambda e: e.memset(c_all[:], 0.0), writes=["c_all"])
        warm(55 if b == 0 else 190, 5)

        def ld_weights():
            em.op("sync", lambda e: e.dma_start(
                out=wk_sb[:],
                in_=wk_h[:, :].rearrange("(c p) d -> p c d", p=128)),
                  writes=["wk_sb"], dma=True, tag="wk")
            em.op("sync", lambda e: e.dma_start(
                out=wv_sb[:],
                in_=wv_h[:, :].rearrange("(c p) d -> p c d", p=128)),
                  writes=["wv_sb"], dma=True, tag="wv")

        def pre_load(g):
            gp = g % 2
            bufA = bufAd[gp]
            kA = f"bufA{gp}"
            t0 = TGT * g
            srcx = x_hbm[b, t0:t0 + TGT, :].rearrange("(s p) d -> p s d",
                                                      p=128)
            em.op("sync", lambda e, srcx=srcx, bufA=bufA: e.dma_start(
                out=bufA[:], in_=srcx),
                  writes=[kA], dma=True, tag="xload")

        def pre_ln1(g):
            gp = g % 2
            bufA, bufB = bufAd[gp], bufBd[gp]
            kA, kB = f"bufA{gp}", f"bufB{gp}"
            layer_norm(bufA, bufB, bufA, kA, kB, kA)

        def pre_ln2(g):
            gp = g % 2
            bufA, bufB = bufAd[gp], bufBd[gp]
            kA, kB = f"bufA{gp}", f"bufB{gp}"
            layer_norm2(bufB, bufA, kB, kA)

        def main1(g):
            gp = g % 2
            bufA, bufB = bufAd[gp], bufBd[gp]
            kA, kB = f"bufA{gp}", f"bufB{gp}"
            t0 = TGT * g
            for c in range(NCH):
                tb = 3 * (c % 2)
                for s in range(NSB):
                    em.op("tensor", lambda e, c=c, s=s, bufB=bufB, tb=tb:
                          e.transpose(
                        PS(tb, 128 * s, 128), bufB[:, s, 128 * c:128 * c + 128],
                        ident),
                        reads=[kB, "cn"], writes=[f"ps{tb}"])
                em.op("vector", lambda e, c=c, tb=tb: e.tensor_copy(
                    out=xnT[:, c, :], in_=PS(tb)),
                    reads=[f"ps{tb}"], writes=["xnT"])
            dst = sc_xnT[b, :, t0:t0 + TGT].rearrange("(c p) t -> p c t",
                                                      p=128)
            em.op("gpsimd", lambda e, dst=dst: e.dma_start(out=dst,
                                                           in_=xnT[:]),
                  reads=["xnT"], dma=True)
            for c in range(NCH):
                tb = 3 * (c % 2)
                for s in range(NSB):
                    em.op("tensor", lambda e, c=c, s=s, bufA=bufA, tb=tb:
                          e.transpose(
                        PS(tb, 128 * s, 128), bufA[:, s, 128 * c:128 * c + 128],
                        ident),
                        reads=[kA, "cn"], writes=[f"ps{tb}"])
                em.op("scalar", lambda e, c=c, tb=tb: e.activation(
                    out=xnormT[:, c, :], in_=PS(tb), func=AF.Identity,
                    scale=cn[:, C_GAMMA + c:C_GAMMA + c + 1],
                    bias=cn[:, C_BETA + c:C_BETA + c + 1]),
                    reads=[f"ps{tb}", "cn"], writes=["xnormT"])
            dst = sc_xnormT[b, :, t0:t0 + TGT].rearrange("(c p) t -> p c t",
                                                         p=128)
            em.op("gpsimd", lambda e, dst=dst: e.dma_start(out=dst,
                                                           in_=xnormT[:]),
                  reads=["xnormT"], dma=True)
            def chainA(j):
                em.op("scalar", lambda e, j=j: e.activation(
                    out=sq_sb[:], in_=kT_tg[:, j, :], func=AF.Square),
                    reads=["kT_tg"], writes=["sq_sb"])
                em.op("tensor", lambda e, j=j: e.matmul(
                    ps[0:2, 512 * 3:512 * 3 + TGT], cn[:, C_BLK2:C_BLK2 + 2],
                    sq_sb[:], start=True, stop=True),
                    reads=["sq_sb", "cn"], writes=["ps3"])
                em.op("scalar", lambda e, j=j: e.activation(
                    out=n2t[:, j % 2, :], in_=ps[0:2, 512 * 3:512 * 3 + TGT],
                    func=AF.Sqrt),
                    reads=["ps3"], writes=[f"n2t{j % 2}"])

            def chainB(j):
                em.op("vector", lambda e, j=j: e.tensor_single_scalar(
                    out=n2t[:, j % 2, :], in_=n2t[:, j % 2, :], scalar=1e-12,
                    op=ALU.max),
                    reads=[f"n2t{j % 2}"], writes=[f"n2t{j % 2}"])
                em.op("vector", lambda e, j=j: e.reciprocal(
                    out=rn2[:, :], in_=n2t[:, j % 2, :]),
                    reads=[f"n2t{j % 2}"], writes=["rn2"])
                em.op("tensor", lambda e, j=j: e.matmul(
                    PS(4), sel1, rn2[:, :], start=True, stop=True),
                    reads=["rn2", "cn"], writes=["ps4"])
                em.op("vector", lambda e, j=j: e.tensor_mul(
                    out=kT_tg[:, j, :], in0=kT_tg[:, j, :],
                    in1=PS(4)),
                    reads=["ps4", "kT_tg"], writes=["kT_tg"])
                for s in range(NSB):
                    em.op("tensor", lambda e, j=j, s=s: e.matmul(
                        ps[:, 512 * 5 + 16 * s + 2 * j:
                           512 * 5 + 16 * s + 2 * j + 2],
                        kT_tg[:, j, 128 * s:128 * s + 128], gkblk,
                        start=True, stop=True),
                        reads=["kT_tg", "cn"], writes=["ps5"])

            for j in range(NCH):
                pb = 1 + (j % 2)
                for i in range(NCH):
                    em.op("tensor", lambda e, i=i, j=j, pb=pb: e.matmul(
                        PS(pb), wk_sb[:, i, 128 * j:128 * j + 128],
                        xnormT[:, i, :], start=(i == 0), stop=(i == NCH - 1)),
                        reads=["wk_sb", "xnormT"], writes=[f"ps{pb}"],
                        group=f"kp{g}_{j}", gclose=(i == NCH - 1))
                if j >= 3:
                    chainB(j - 3)
                if j >= 1:
                    chainA(j - 1)
                em.op("vector", lambda e, j=j, pb=pb: e.tensor_copy(
                    out=kT_tg[:, j, :], in_=PS(pb)),
                    reads=[f"ps{pb}"], writes=["kT_tg"])
            chainB(NCH - 3)
            chainA(NCH - 1)
            chainB(NCH - 2)
            chainB(NCH - 1)
            dst = sc_k[b, :, t0:t0 + TGT].rearrange("(c p) t -> p c t",
                                                    p=128)
            em.op("gpsimd", lambda e, dst=dst: e.dma_start(out=dst,
                                                           in_=kT_tg[:]),
                  reads=["kT_tg"], dma=True, tag="kstore")
            em.op("scalar", lambda e: e.activation(
                out=gdot[:, :], in_=ps[:, 512 * 5:512 * 5 + 64],
                func=AF.Sigmoid),
                reads=["ps5"], writes=["gdot"])
            em.op("vector", lambda e: e.tensor_scalar(
                out=gk_tm[:, :], in0=gdot[:, :], scalar1=0.75, scalar2=0.25,
                op0=ALU.mult, op1=ALU.add),
                reads=["gdot"], writes=["gk_tm"])

        def main2(g, lnhook=None):
            gp = g % 2
            t0 = TGT * g
            VB = (1, 2, 6, 7)
            for s in range(NSB):
                for jh in range(2):
                    pb = VB[(2 * s + jh) % 4]
                    for i in range(NCH):
                        em.op("tensor", lambda e, i=i, s=s, jh=jh, pb=pb:
                              e.matmul(PS(pb),
                                       xnT[:, i, 128 * s:128 * s + 128],
                                       wv_sb[:, i, 512 * jh:512 * jh + 512],
                                       start=(i == 0), stop=(i == NCH - 1)),
                              reads=["wv_sb", "xnT"], writes=[f"ps{pb}"],
                              group=f"vp{g}_{s}_{jh}", gclose=(i == NCH - 1))
                    for hh in range(8):
                        h = 8 * jh + hh
                        em.op("vector", lambda e, s=s, jh=jh, hh=hh, h=h,
                              pb=pb: e.tensor_scalar_mul(
                                  out=v_tg[:, s, 512 * jh + 64 * hh:
                                           512 * jh + 64 * hh + 64],
                                  in0=PS(pb, 64 * hh, 64),
                                  scalar1=gk_tm[:, 16 * s + h:16 * s + h + 1]),
                              reads=[f"ps{pb}", "gk_tm"],
                              writes=[f"v_tg{s}"])
            if lnhook is not None:
                lnhook()
            for s in range(NSB):
                for p in range(8):
                    em.op("tensor", lambda e, s=s, p=p: e.matmul(
                        ps[:, 512 * 4 + 8 * s + p:512 * 4 + 8 * s + p + 1],
                        v_tg[:, s, 128 * p:128 * p + 128], ones,
                        start=True, stop=True),
                        reads=[f"v_tg{s}", "cn"], writes=["ps4"])
            for s in range(NSB):
                em.op("vector", lambda e, s=s: e.tensor_add(
                    out=c_all[:], in0=c_all[:],
                    in1=ps[:, 512 * 4 + 8 * s:512 * 4 + 8 * s + 8]),
                    reads=["ps4", "c_all"], writes=["c_all"])
            dst = sc_vp[b, t0:t0 + TGT, :].rearrange("(s p) d -> p s d", p=128)
            em.op("gpsimd", lambda e, dst=dst: e.dma_start(out=dst,
                                                           in_=v_tg[:]),
                  reads=[f"v_tg{s}" for s in range(NSB)], dma=True)

        pre_load(0)
        ld_weights()
        pre_ln1(0)
        pre_ln2(0)
        for g in range(NTG):
            if g + 1 < NTG:
                pre_load(g + 1)
            main1(g)
            if g + 1 < NTG:
                pre_ln1(g + 1)
                main2(g, lnhook=lambda g=g: pre_ln2(g + 1))
            else:
                main2(g)

    def phase_b1(b, xnormT2, qstage, sqq, qnT, wq_sb, wo_sb, kc_sb, vp_sb,
                 t_sb, f1, fusedT, sT_st, gq_hm):
        em.op("sync", lambda e: e.dma_start(
            out=wq_sb[:], in_=wq_h[:, :].rearrange("(c p) d -> p c d", p=128)),
              writes=["wq_sb"], dma=True, tag="wq")
        src0 = sc_xnormT[b, :, 0:TGT].rearrange("(c p) t -> p c t", p=128)
        em.op("sync", lambda e, src=src0: e.dma_start(out=xnormT2[0][:],
                                                      in_=src),
              writes=["xn0"], dma=True)
        em.op("sync", lambda e: e.dma_start(
            out=wo_sb[:], in_=wo_h[:, :].rearrange("(c p) d -> p c d", p=128)),
              writes=["wo_sb"], dma=True, tag="wo")
        warm(115, 6)
        for qt in range(NTG):
            q0 = TGT * qt
            xp = qt % 2
            xnormT = xnormT2[xp]
            xkey = f"xn{xp}"
            if qt + 1 < NTG:
                srcn = sc_xnormT[b, :, q0 + TGT:q0 + 2 * TGT].rearrange(
                    "(c p) t -> p c t", p=128)
                em.op("gpsimd", lambda e, src=srcn, xq=(qt + 1) % 2:
                      e.dma_start(out=xnormT2[xq][:], in_=src),
                      writes=[f"xn{(qt + 1) % 2}"], dma=True)
            for s in range(NSB):
                for jh in range(2):
                    pb = jh
                    for i in range(NCH):
                        em.op("tensor", lambda e, i=i, s=s, jh=jh, pb=pb,
                              xnormT=xnormT:
                              e.matmul(PS(pb),
                                       xnormT[:, i, 128 * s:128 * s + 128],
                                       wq_sb[:, i, 512 * jh:512 * jh + 512],
                                       start=(i == 0), stop=(i == NCH - 1)),
                              reads=["wq_sb", xkey], writes=[f"ps{pb}"],
                              group=f"qp{qt}_{s}_{jh}", gclose=(i == NCH - 1))
                    em.op("vector", lambda e, s=s, jh=jh, pb=pb: e.tensor_copy(
                        out=qstage[:, s, 512 * jh:512 * jh + 512], in_=PS(pb)),
                        reads=[f"ps{pb}"], writes=["qf"])
                    em.op("scalar", lambda e, s=s, jh=jh: e.activation(
                        out=sqq[:, 512 * jh:512 * jh + 512],
                        in_=qstage[:, s, 512 * jh:512 * jh + 512],
                        func=AF.Square),
                        reads=["qf"], writes=["sqq"])
                em.op("vector", lambda e, s=s: e.reduce_sum(
                    out=sstile[:, 16 * s:16 * s + 16],
                    in_=sqq[:].rearrange("p (h d) -> p h d", d=DH), axis=AX.X),
                    reads=["sqq"], writes=["sstile"])
                em.op("scalar", lambda e, s=s: e.activation(
                    out=sstile[:, 16 * s:16 * s + 16],
                    in_=sstile[:, 16 * s:16 * s + 16], func=AF.Sqrt),
                    reads=["sstile"], writes=["sstile"])
                em.op("vector", lambda e, s=s: e.tensor_single_scalar(
                    out=sstile[:, 16 * s:16 * s + 16],
                    in_=sstile[:, 16 * s:16 * s + 16], scalar=1e-12,
                    op=ALU.max),
                    reads=["sstile"], writes=["sstile"])
                em.op("vector", lambda e, s=s: e.reciprocal(
                    out=sstile[:, 16 * s:16 * s + 16],
                    in_=sstile[:, 16 * s:16 * s + 16]),
                    reads=["sstile"], writes=["sstile"])
                em.op("vector", lambda e, s=s: e.tensor_mul(
                    out=qstage[:, s, :].rearrange("p (h d) -> p h d", d=DH),
                    in0=qstage[:, s, :].rearrange("p (h d) -> p h d", d=DH),
                    in1=sstile[:, 16 * s:16 * s + 16].unsqueeze(2)
                        .broadcast_to([128, 16, DH])),
                    reads=["qf", "sstile"], writes=["qf"])
            for c in range(NCH):
                tb = 2 + (c % 2)
                for s in range(NSB):
                    em.op("tensor", lambda e, c=c, s=s, tb=tb: e.transpose(
                        PS(tb, 128 * s, 128),
                        qstage[:, s, 128 * c:128 * c + 128], ident),
                        reads=["qf", "cn"], writes=[f"ps{tb}"])
                if c % 2 == 0:
                    em.op("scalar", lambda e, c=c, tb=tb: e.activation(
                        out=qnT[:, c, :], in_=PS(tb), func=AF.Identity),
                        reads=[f"ps{tb}"], writes=["qnT"])
                else:
                    em.op("vector", lambda e, c=c, tb=tb: e.tensor_copy(
                        out=qnT[:, c, :], in_=PS(tb)),
                        reads=[f"ps{tb}"], writes=["qnT"])
            for s in range(NSB):
                for c in range(NCH):
                    em.op("tensor", lambda e, c=c, s=s: e.matmul(
                        ps[:, 512 * 3 + 16 * s + 2 * c:
                           512 * 3 + 16 * s + 2 * c + 2],
                        qnT[:, c, 128 * s:128 * s + 128], gqblk,
                        start=True, stop=True),
                        reads=["qnT", "cn"], writes=["ps3"])
            em.op("scalar", lambda e: e.activation(
                out=gdot[:, :], in_=ps[:, 512 * 3:512 * 3 + 64],
                func=AF.Sigmoid),
                reads=["ps3"], writes=["gdot"])
            em.op("vector", lambda e: e.tensor_scalar(
                out=gq_tm[:, :], in0=gdot[:, :], scalar1=0.75, scalar2=0.25,
                op0=ALU.mult, op1=ALU.add),
                reads=["gdot"], writes=["gq_tm"])
            for c in range(NCH):
                for s in range(NSB):
                    em.op("tensor", lambda e, c=c, s=s: e.transpose(
                        ps[0:2, 512 * 3 + 128 * s:512 * 3 + 128 * s + 128],
                        gq_tm[:, 16 * s + 2 * c:16 * s + 2 * c + 2], ident),
                        reads=["gq_tm", "cn"], writes=["ps3"])
                em.op("vector", lambda e, c=c: e.tensor_copy(
                    out=gq_hm[c][:, :], in_=ps[0:2, 512 * 3:512 * 3 + TGT]),
                    reads=["ps3"], writes=[f"gq_hm{c}"])
            def dma_c(c):
                cp = c % 2
                src = sc_vp[b, :, 128 * c:128 * c + 128].rearrange(
                    "(kt p) d -> p kt d", p=128)
                em.op("sync", lambda e, src=src, cp=cp: e.dma_start(
                    out=vp_sb[cp][:], in_=src),
                    writes=[f"vp{cp}"], dma=True, tag="vp")
                em.op("sync", lambda e, c=c, cp=cp: e.dma_start(
                    out=kc_sb[cp][:], in_=sc_k[b, 128 * c:128 * c + 128, :]),
                    writes=[f"kc{cp}"], dma=True, tag="kc")

            def align(kt, c, cp):
                alb = 2 + 2 * (kt % 2)
                em.op("tensor", lambda e, kt=kt, alb=alb, cp=cp, c=c:
                      e.matmul(
                    PS(alb), kc_sb[cp][0:64, 128 * kt:128 * kt + 128],
                    qnT[0:64, c, :], start=True, stop=True),
                    reads=[f"kc{cp}", "qnT"], writes=[f"ps{alb}"])
                em.op("tensor", lambda e, kt=kt, alb=alb, cp=cp, c=c:
                      e.matmul(
                    PS(alb + 1), kc_sb[cp][64:128, 128 * kt:128 * kt + 128],
                    qnT[64:128, c, :], start=True, stop=True,
                    tile_position=(64, 0)),
                    reads=[f"kc{cp}", "qnT"], writes=[f"ps{alb + 1}"])

            def tanh0():
                em.op("scalar", lambda e: e.activation(
                    out=t_sb[0][:], in_=ps[:, 512 * 2:512 * 2 + 1024],
                    func=AF.Tanh, scale=15.0, bias=tanhb2[:, 0:1]),
                    reads=["ps2", "ps3", "tanhb2"],
                    writes=["t0"])

            dma_c(0)
            align(0, 0, 0)
            tanh0()
            for c in range(NCH):
                cp = c % 2
                if c + 1 < NCH:
                    dma_c(c + 1)
                for kt in range(NKT):
                    kp = kt % 2
                    alb = 2 + 2 * kp
                    if kt > 0:
                        em.op("scalar", lambda e, kp=kp, alb=alb:
                              e.activation(
                            out=t_sb[kp][:],
                            in_=ps[:, 512 * alb:512 * alb + 1024],
                            func=AF.Tanh, scale=15.0, bias=tanhb2[:, 0:1]),
                            reads=[f"ps{alb}", f"ps{alb + 1}", "tanhb2"],
                            writes=[f"t{kp}"])
                    if kt + 1 < NKT:
                        align(kt + 1, c, cp)
                    em.op("tensor", lambda e, kt=kt, kp=kp, cp=cp: e.matmul(
                        ps[0:64, 512 * 6:512 * 6 + 512],
                        vp_sb[cp][:, kt, 0:64], t_sb[kp][:, 0:512],
                        start=(kt == 0), stop=(kt == NKT - 1)),
                        reads=[f"vp{cp}", f"t{kp}"], writes=["ps6"],
                        group=f"fuA{qt}_{c}", gclose=(kt == NKT - 1))
                    em.op("tensor", lambda e, kt=kt, kp=kp, cp=cp: e.matmul(
                        ps[64:128, 512 * 7:512 * 7 + 512],
                        vp_sb[cp][:, kt, 64:128], t_sb[kp][:, 512:1024],
                        start=(kt == 0), stop=(kt == NKT - 1),
                        tile_position=(0, 64)),
                        reads=[f"vp{cp}", f"t{kp}"], writes=["ps7"],
                        group=f"fuB{qt}_{c}", gclose=(kt == NKT - 1))
                if c + 1 < NCH:
                    align(0, c + 1, (c + 1) % 2)
                    tanh0()
                em.op("tensor", lambda e, c=c: e.matmul(
                    PS(0), sel05, gq_hm[c][:, :], start=True, stop=True),
                    reads=[f"gq_hm{c}", "cn"], writes=["ps0"])
                em.op("vector", lambda e: e.tensor_copy(
                    out=f1[:], in_=PS(0)),
                    reads=["ps0"], writes=["f1"])
                em.op("vector", lambda e, c=c: e.scalar_tensor_tensor(
                    out=fusedT[0:64, c, :], in0=ps[0:64, 512 * 6:512 * 6 + 512],
                    scalar=c_all[0:64, c:c + 1], in1=f1[0:64, :],
                    op0=ALU.add, op1=ALU.mult),
                    reads=["ps6", "f1", "c_all"], writes=["qf"])
                em.op("vector", lambda e, c=c: e.scalar_tensor_tensor(
                    out=fusedT[64:128, c, :], in0=ps[64:128, 512 * 7:512 * 7 + 512],
                    scalar=c_all[64:128, c:c + 1], in1=f1[64:128, :],
                    op0=ALU.add, op1=ALU.mult),
                    reads=["ps7", "f1", "c_all"], writes=["qf"])
            for j in range(NCH):
                pb = j % 2
                for i in range(NCH):
                    em.op("tensor", lambda e, i=i, j=j, pb=pb: e.matmul(
                        PS(pb), wo_sb[:, i, 128 * j:128 * j + 128],
                        fusedT[:, i, :], start=(i == 0), stop=(i == NCH - 1)),
                        reads=["wo_sb", "qf"], writes=[f"ps{pb}"],
                        group=f"wo{qt}_{j}", gclose=(i == NCH - 1))
                sp = j % 2
                em.op("scalar", lambda e, j=j, pb=pb, sp=sp: e.activation(
                    out=sT_st[sp][:], in_=PS(pb), func=AF.Identity,
                    bias=cn[:, C_BO + j:C_BO + j + 1]),
                    reads=[f"ps{pb}", "cn"], writes=[f"sT{sp}"])
                dst = sc_s[b, 128 * j:128 * j + 128, q0:q0 + TGT]
                em.op("gpsimd", lambda e, dst=dst, sp=sp: e.dma_start(
                    out=dst, in_=sT_st[sp][:]),
                    reads=[f"sT{sp}"], dma=True)

    def phase_b2(b, xnT2d, xh, xl, wg_sb, wu_sb, wd_sb, hf, sg_sb, hh, hl,
                 g_sb, bufx, s_sb, prod, out_sb):
        NHC_l = NHC
        src0 = sc_xnT[b, :, 0:TGT].rearrange("(c p) t -> p c t", p=128)
        em.op("gpsimd", lambda e, src=src0: e.dma_start(out=xnT2d[0][:],
                                                        in_=src),
              writes=["xt0"], dma=True)
        warm(85, 6)

        def split(qt):
            xp = qt % 2
            xnT2 = xnT2d[xp]
            xtkey = f"xt{xp}"
            for c in range(NCH):
                em.op("vector", lambda e, c=c, xnT2=xnT2: e.tensor_copy(
                    out=xh[:, c, :], in_=xnT2[:, c, :]),
                    reads=[xtkey], writes=["xh"])
                em.op("vector", lambda e, c=c, xnT2=xnT2: e.tensor_sub(
                    out=xl[:, c, :], in0=xnT2[:, c, :], in1=xh[:, c, :]),
                    reads=[xtkey, "xh"], writes=["xl"])

        split(0)
        for qt in range(NTG):
            q0 = TGT * qt
            src = x_hbm[b, q0:q0 + TGT, :].rearrange("(s p) d -> p s d", p=128)
            em.op("gpsimd", lambda e, src=src: e.dma_start(out=bufx[:],
                                                           in_=src),
                  writes=["bufx"], dma=True)
            src = sc_s[b, :, q0:q0 + TGT].rearrange("(c p) t -> p c t", p=128)
            em.op("gpsimd", lambda e, src=src: e.dma_start(out=s_sb[:],
                                                           in_=src),
                  writes=["s_sb"], dma=True)
            if qt + 1 < NTG:
                srcn = sc_xnT[b, :, q0 + TGT:q0 + 2 * TGT].rearrange(
                    "(c p) t -> p c t", p=128)
                em.op("gpsimd", lambda e, src=srcn, xq=(qt + 1) % 2:
                      e.dma_start(out=xnT2d[xq][:], in_=src),
                      writes=[f"xt{(qt + 1) % 2}"], dma=True)
            for hc in range(NHC_l):
                wp = hc % 3
                em.op("sync", lambda e, hc=hc, wp=wp: e.dma_start(
                    out=wg_sb[wp][:],
                    in_=sc_wg[hc, :, :].rearrange("p (c h) -> p c h", h=128)),
                    writes=[f"wg{wp}"], dma=True, tag="wg")
                em.op("sync", lambda e, hc=hc, wp=wp: e.dma_start(
                    out=wu_sb[wp][:],
                    in_=sc_wu[hc, :, :].rearrange("p (c h) -> p c h", h=128)),
                    writes=[f"wu{wp}"], dma=True, tag="wu")

                for i in range(NCH):
                    em.op("tensor", lambda e, i=i, wp=wp: e.matmul(
                        PS(0), wg_sb[wp][:, i, :], xh[:, i, :],
                        start=(i == 0), stop=False),
                        reads=[f"wg{wp}", "xh"], writes=["ps0"],
                        group=f"ga{qt}_{hc}")
                    em.op("tensor", lambda e, i=i, wp=wp: e.matmul(
                        PS(0), wg_sb[wp][:, i, :], xl[:, i, :],
                        start=False, stop=(i == NCH - 1)),
                        reads=[f"wg{wp}", "xl"], writes=["ps0"],
                        group=f"ga{qt}_{hc}", gclose=(i == NCH - 1))
                for i in range(NCH):
                    em.op("tensor", lambda e, i=i, wp=wp: e.matmul(
                        PS(1), wu_sb[wp][:, i, :], xh[:, i, :],
                        start=(i == 0), stop=False),
                        reads=[f"wu{wp}", "xh"], writes=["ps1"],
                        group=f"up{qt}_{hc}")
                    em.op("tensor", lambda e, i=i, wp=wp: e.matmul(
                        PS(1), wu_sb[wp][:, i, :], xl[:, i, :],
                        start=False, stop=(i == NCH - 1)),
                        reads=[f"wu{wp}", "xl"], writes=["ps1"],
                        group=f"up{qt}_{hc}", gclose=(i == NCH - 1))
                em.op("scalar", lambda e: e.activation(
                    out=sg_sb[:], in_=PS(0), func=AF.Silu, scale=S127),
                    reads=["ps0"], writes=["sg_sb"])
                em.op("vector", lambda e: e.scalar_tensor_tensor(
                    out=hf[:], in0=PS(1), scalar=S127, in1=sg_sb[:],
                    op0=ALU.mult, op1=ALU.mult),
                    reads=["ps1", "sg_sb"], writes=["hf"])
                em.op("vector", lambda e, hc=hc: e.tensor_copy(
                    out=hh[:, hc, :], in_=hf[:]),
                    reads=["hf"], writes=["hh"])
                em.op("vector", lambda e, hc=hc: e.tensor_sub(
                    out=hl[:, hc, :], in0=hf[:], in1=hh[:, hc, :]),
                    reads=["hf", "hh"], writes=["hl"])
            if qt + 1 < NTG:
                split(qt + 1)
            for j in range(NCH):
                pb = 2 + (j % 2)
                wp = j % 3
                em.op("sync", lambda e, j=j, wp=wp: e.dma_start(
                    out=wd_sb[wp][:],
                    in_=sc_wd[j, :, :].rearrange("p (c d) -> p c d", d=128)),
                    writes=[f"wd{wp}"], dma=True, tag="wd")
                for hc in range(NHC_l):
                    em.op("tensor", lambda e, hc=hc, j=j, pb=pb, wp=wp:
                          e.matmul(
                        PS(pb), wd_sb[wp][:, hc, :],
                        hh[:, hc, :], start=(hc == 0), stop=False),
                        reads=[f"wd{wp}", "hh"], writes=[f"ps{pb}"],
                        group=f"dn{qt}_{j}")
                    em.op("tensor", lambda e, hc=hc, j=j, pb=pb, wp=wp:
                          e.matmul(
                        PS(pb), wd_sb[wp][:, hc, :],
                        hl[:, hc, :], start=False, stop=(hc == NHC_l - 1)),
                        reads=[f"wd{wp}", "hl"], writes=[f"ps{pb}"],
                        group=f"dn{qt}_{j}", gclose=(hc == NHC_l - 1))
                em.op("scalar", lambda e, j=j, pb=pb: e.activation(
                    out=g_sb[:, j, :], in_=PS(pb), func=AF.Silu, scale=S127),
                    reads=[f"ps{pb}"], writes=["g_sb"])
                em.op("vector", lambda e, j=j: e.tensor_mul(
                    out=s_sb[:, j, :], in0=s_sb[:, j, :], in1=g_sb[:, j, :]),
                    reads=["g_sb", "s_sb"], writes=["s_sb"])
            for j in range(NCH):
                tb = (4, 5, 6, 7)[j % 4]
                for s in range(NSB):
                    em.op("tensor", lambda e, j=j, s=s, tb=tb: e.transpose(
                        PS(tb, 128 * s, 128), s_sb[:, j, 128 * s:128 * s + 128],
                        ident),
                        reads=["s_sb", "cn"], writes=[f"ps{tb}"])
                for s in range(NSB):
                    em.op("vector", lambda e, j=j, s=s, tb=tb: e.tensor_add(
                        out=out_sb[:, s, 128 * j:128 * j + 128],
                        in0=PS(tb, 128 * s, 128),
                        in1=bufx[:, s, 128 * j:128 * j + 128]),
                        reads=[f"ps{tb}", "bufx"], writes=["out_sb"])
            dst = out_h[b, q0:q0 + TGT, :].rearrange("(s p) d -> p s d", p=128)
            em.op("gpsimd", lambda e, dst=dst: e.dma_start(out=dst,
                                                           in_=out_sb[:]),
                  reads=["out_sb"], dma=True, tag="out")

    with contextlib.ExitStack() as pp:
        wbuf = [pp.enter_context(nc.sbuf_tensor(f"wrep{i}", [128, HID], F16))
                for i in range(2)]
        for c in range(NCH):
            wp = c % 2
            for (wsrc, wdst, tag) in ((wgT_h, sc_wg, "rg"), (wuT_h, sc_wu, "ru")):
                em.op("gpsimd", lambda e, c=c, wp=wp, wsrc=wsrc: e.dma_start(
                    out=wbuf[wp][:],
                    in_=wsrc[128 * c:128 * c + 128, :]),
                    writes=[f"wrep{wp}"], dma=True, tag=tag)
                dst = wdst[:, :, 128 * c:128 * c + 128].rearrange(
                    "h p j -> p h j")
                em.op("gpsimd", lambda e, dst=dst, wp=wp: e.dma_start(
                    out=dst, in_=wbuf[wp][:].rearrange("p (h j) -> p h j",
                                                       j=128)),
                    reads=[f"wrep{wp}"], dma=True, tag=tag)
        wdbuf = [pp.enter_context(nc.sbuf_tensor(f"wdrep{i}", [128, D], F16))
                 for i in range(2)]
        for c in range(NHC):
            wp = c % 2
            em.op("gpsimd", lambda e, c=c, wp=wp: e.dma_start(
                out=wdbuf[wp][:],
                in_=wdT_h[128 * c:128 * c + 128, :]),
                writes=[f"wdrep{wp}"], dma=True, tag="rd")
            dst = sc_wd[:, :, 128 * c:128 * c + 128].rearrange(
                "j p d -> p j d")
            em.op("gpsimd", lambda e, dst=dst, wp=wp: e.dma_start(
                out=dst, in_=wdbuf[wp][:].rearrange("p (j d) -> p j d",
                                                    d=128)),
                reads=[f"wdrep{wp}"], dma=True, tag="rd")
    em.barrier()
    for b in range(S):
        if True:
            with contextlib.ExitStack() as pa:
                bufAd = [pa.enter_context(
                    nc.sbuf_tensor(f"bufA{i}_{b}", [128, NSB, D], F32))
                    for i in range(2)]
                bufBd = [pa.enter_context(
                    nc.sbuf_tensor(f"bufB{i}_{b}", [128, NSB, D], F32))
                    for i in range(2)]
                xnT = pa.enter_context(nc.sbuf_tensor(f"xnT_{b}", [128, NCH, TGT],
                                                      F32))
                xnormT = pa.enter_context(
                    nc.sbuf_tensor(f"xnormT_{b}", [128, NCH, TGT], F32))
                wk_sb = pa.enter_context(
                    nc.sbuf_tensor(f"wk_sb_{b}", [128, NCH, D], F32))
                wv_sb = pa.enter_context(
                    nc.sbuf_tensor(f"wv_sb_{b}", [128, NCH, D], F32))
                kT_tg = pa.enter_context(
                    nc.sbuf_tensor(f"kT_tg_{b}", [128, NCH, TGT], F32))
                sq_sb = pa.enter_context(nc.sbuf_tensor(f"sq_sb_{b}", [128, TGT],
                                                        F32))
                v_tg = pa.enter_context(nc.sbuf_tensor(f"v_tg_{b}", [128, NSB, D],
                                                       F32))
                rn2 = pa.enter_context(nc.sbuf_tensor(f"rn2_{b}", [2, TGT],
                                                      F32))
                if phases[0]:
                    phase_a(b, bufAd, bufBd, xnT, xnormT, wk_sb, wv_sb,
                            kT_tg, sq_sb, v_tg, rn2)
            em.barrier()
            with contextlib.ExitStack() as pb:
                xnormT2 = [pb.enter_context(
                    nc.sbuf_tensor(f"xn{i}_{b}", [128, NCH, TGT], F32))
                    for i in range(2)]
                qf = pb.enter_context(
                    nc.sbuf_tensor(f"qf_{b}", [128, NSB * D], F32))
                qstage = qf[:].rearrange("p (s c) -> p s c", s=NSB)
                sqq = pb.enter_context(nc.sbuf_tensor(f"sqq_{b}", [128, D], F32))
                qnT = pb.enter_context(nc.sbuf_tensor(f"qnT_{b}", [128, NCH, TGT],
                                                      F32))
                wq_sb = pb.enter_context(
                    nc.sbuf_tensor(f"wq_sb_{b}", [128, NCH, D], F32))
                wo_sb = pb.enter_context(
                    nc.sbuf_tensor(f"wo_sb_{b}", [128, NCH, D], F32))
                kc_sb = [pb.enter_context(
                    nc.sbuf_tensor(f"kc{i}_{b}", [128, T], F32))
                    for i in range(2)]
                vp_sb = [pb.enter_context(
                    nc.sbuf_tensor(f"vp{i}_{b}", [128, NKT, 128], F32))
                    for i in range(2)]
                t_sb = [pb.enter_context(
                    nc.sbuf_tensor(f"t{i}_{b}", [128, 1024], F32))
                    for i in range(2)]
                f1 = pb.enter_context(nc.sbuf_tensor(f"f1_{b}", [128, TGT], F32))
                fusedT = qf[:].rearrange("p (c t) -> p c t", c=NCH)
                sT_st = [pb.enter_context(
                    nc.sbuf_tensor(f"sT{i}_{b}", [128, TGT], F32))
                    for i in range(2)]
                gq_hm = [pb.enter_context(
                    nc.sbuf_tensor(f"gq_hm{c}_{b}", [2, TGT], F32))
                    for c in range(NCH)]
                if phases[1]:
                    phase_b1(b, xnormT2, qstage, sqq, qnT, wq_sb, wo_sb, kc_sb,
                             vp_sb, t_sb, f1, fusedT, sT_st, gq_hm)
            em.barrier()
        with contextlib.ExitStack() as pc:
            xnT2d = [pc.enter_context(
                nc.sbuf_tensor(f"xt{i}_{b}", [128, NCH, TGT], F32))
                for i in range(2)]
            out_sb = pc.enter_context(
                nc.sbuf_tensor(f"out_sb_{b}", [128, NSB, D], F32))
            xh = pc.enter_context(nc.sbuf_tensor(f"xh_b2_{b}", [128, NCH, TGT],
                                                 F16))
            xl = pc.enter_context(nc.sbuf_tensor(f"xl_b2_{b}", [128, NCH, TGT],
                                                 F16))
            wg_sb = [pc.enter_context(
                nc.sbuf_tensor(f"wg{i}_{b}", [128, NCH, 128], F16))
                for i in range(3)]
            wu_sb = [pc.enter_context(
                nc.sbuf_tensor(f"wu{i}_{b}", [128, NCH, 128], F16))
                for i in range(3)]
            wd_sb = [pc.enter_context(
                nc.sbuf_tensor(f"wd{i}_{b}", [128, NHC, 128], F16))
                for i in range(3)]
            hf = pc.enter_context(nc.sbuf_tensor(f"hf_{b}", [128, TGT], F32))
            sg_sb = pc.enter_context(nc.sbuf_tensor(f"sg_sb_{b}", [128, TGT], F32))
            hh = pc.enter_context(nc.sbuf_tensor(f"hh_{b}", [128, NHC, TGT], F16))
            hl = pc.enter_context(nc.sbuf_tensor(f"hl_{b}", [128, NHC, TGT], F16))
            g_sb = pc.enter_context(nc.sbuf_tensor(f"g_sb_{b}", [128, NCH, TGT],
                                                   F32))
            bufx = pc.enter_context(nc.sbuf_tensor(f"bufx_{b}", [128, NSB, D],
                                                   F32))
            s_sb = pc.enter_context(nc.sbuf_tensor(f"s_sb_{b}", [128, NCH, TGT],
                                                   F32))
            if phases[2]:
                phase_b2(b, xnT2d, xh, xl, wg_sb, wu_sb, wd_sb, hf, sg_sb, hh,
                         hl, g_sb, bufx, s_sb, hf, out_sb)
        em.barrier()

    em.finalize(nc, stack)
    return nc


_PROG_CACHE = {}


def _get_program(S, T, HID):
    key = (S, T, HID)
    if key not in _PROG_CACHE:
        _PROG_CACHE[key] = build_program(S, T, HID)
    return _PROG_CACHE[key]


def kernel(L0_attn_x, L0_attn_w_q, L0_attn_w_k, L0_attn_w_v, L0_attn_g_q,
           L0_attn_g_k, L0_attn_w_o, L0_attn_b_o, L0_attn_gamma, L0_attn_beta,
           L0_ffn_w_gate, L0_ffn_w_up, L0_ffn_w_down):
    x = np.ascontiguousarray(np.asarray(L0_attn_x, dtype=np.float32))
    B, T, D_ = x.shape
    assert D_ == D
    n_cores = 8
    S = B // n_cores
    HID = L0_ffn_w_gate.shape[0]
    cn = _mk_consts(np.asarray(L0_attn_b_o, np.float32),
                    np.asarray(L0_attn_gamma, np.float32),
                    np.asarray(L0_attn_beta, np.float32),
                    np.asarray(L0_attn_g_q, np.float32),
                    np.asarray(L0_attn_g_k, np.float32))
    wgT = np.ascontiguousarray(np.asarray(L0_ffn_w_gate, np.int32).T)
    wuT = np.ascontiguousarray(np.asarray(L0_ffn_w_up, np.int32).T)
    wdT = np.ascontiguousarray(np.asarray(L0_ffn_w_down, np.int32).T)
    base = {
        "wq": np.ascontiguousarray(np.asarray(L0_attn_w_q, np.float32)),
        "wk": np.ascontiguousarray(np.asarray(L0_attn_w_k, np.float32)),
        "wv": np.ascontiguousarray(np.asarray(L0_attn_w_v, np.float32)),
        "wo": np.ascontiguousarray(np.asarray(L0_attn_w_o, np.float32)),
        "wgT": wgT, "wuT": wuT, "wdT": wdT, "cn": cn,
    }
    in_maps = [dict(base, x=np.ascontiguousarray(x[S * i:S * (i + 1)]))
               for i in range(n_cores)]
    nc = _get_program(S, T, HID)
    res = run_bass_kernel_spmd(nc, in_maps, core_ids=list(range(n_cores)))
    return np.concatenate([r["out"] for r in res.results], axis=0)
